# version 08
# baseline (speedup 1.0000x reference)
import numpy as np
from contextlib import ExitStack

import concourse.bass as bass
import concourse.mybir as mybir
from concourse.bass_utils import run_bass_kernel_spmd

F32 = mybir.dt.float32
BF16 = mybir.dt.bfloat16
I32 = mybir.dt.int32
AF = mybir.ActivationFunctionType
ALU = mybir.AluOpType
AX = mybir.AxisListType.X

B, T, D = 4, 4096, 1024
DI, NST, KCONV = 2048, 16, 4
DFF, NE, NH, HD = 4096, 4, 16, 64
EXT, PRE, OWN, NKV = 4096, 2048, 2048, 4096
EPS = 1e-5
ALL = None

DEBUG = False
STAGE = 5


class Buf:
    __slots__ = ("name", "h", "space", "sem", "sem_cnt", "writers", "readers")

    def __init__(self, name, h, space):
        self.name, self.h, self.space = name, h, space
        self.sem = None
        self.sem_cnt = 0
        self.writers = []
        self.readers = []


def _conf(k1, k2):
    return k1 is ALL or k2 is ALL or k1 == k2


class Op:
    __slots__ = ("idx", "eng", "fn", "reads", "writes", "is_dma", "sem",
                 "sem_target", "inc", "inc_seq", "waits")

    def __init__(self, idx, eng, fn, reads, writes, is_dma):
        self.idx, self.eng, self.fn = idx, eng, fn
        self.reads, self.writes, self.is_dma = list(reads), list(writes), is_dma
        self.sem = None
        self.sem_target = 0
        self.inc = False
        self.inc_seq = 0
        self.waits = []


class Bld:

    ENGS = ("pe", "dve", "act", "pool", "sp")

    def __init__(self, nc, stack):
        self.nc = nc
        self.stack = stack
        self.phase = ExitStack()
        self.ops = []
        self.e = {"pe": nc.tensor, "dve": nc.vector, "act": nc.scalar,
                  "pool": nc.gpsimd, "sp": nc.sync}
        self.all_sems = []
        self.engsem = {g: self._sem(f"es_{g}") for g in self.ENGS}

    def _sem(self, name):
        s = self.stack.enter_context(self.nc.semaphore(name))
        self.all_sems.append(s)
        return s

    def sb(self, name, shape, dtype, pers=False):
        st = self.stack if pers else self.phase
        h = st.enter_context(self.nc.sbuf_tensor(name, shape, dtype))
        return Buf(name, h, "sb")

    def ps(self, name, shape, dtype=F32):
        h = self.phase.enter_context(self.nc.psum_tensor(name, shape, dtype))
        return Buf(name, h, "ps")

    def dr(self, name, shape, dtype, kind="Internal"):
        if DEBUG and kind == "Internal":
            kind = "ExternalOutput"
        return Buf(name, self.nc.dram_tensor(name, shape, dtype, kind=kind),
                   "dr")

    def op(self, eng, fn, r=(), w=()):
        o = Op(len(self.ops), eng, fn, r, w, False)
        self.ops.append(o)
        return o

    def wave(self, lists):
        n = max(len(li) for li in lists) if lists else 0
        for step in range(n):
            for li in lists:
                if step < len(li):
                    kind, eng, fn, r, w = li[step]
                    if kind == "dma":
                        self.dma(eng, fn, r=r, w=w)
                    else:
                        self.op(eng, fn, r=r, w=w)

    def dma(self, q, fn, r=(), w=()):
        w = [(bf, ALL) for (bf, _k) in w]
        o = Op(len(self.ops), q, fn, r, w, True)
        tgt = o.writes[0][0]
        if tgt.sem is None:
            tgt.sem = self._sem(f"dm{len(self.all_sems)}")
        tgt.sem_cnt += 1
        o.sem = tgt.sem
        o.sem_target = 16 * tgt.sem_cnt
        self.ops.append(o)
        return o

    def barrier(self):
        self.ops.append("BARRIER")
        self.phase.close()
        self.phase = ExitStack()
        self._bc_ps = None
        self._bc_row = None

    def finalize(self):
        pairs = []
        allbufs = {}
        bar_dma = []
        dma_issued = {}

        def note(bf):
            if id(bf) not in allbufs:
                allbufs[id(bf)] = bf
                bf.writers, bf.readers = [], []

        for item in self.ops:
            if item == "BARRIER":
                bar_dma.append(list(dma_issued.values()))
                for bf in allbufs.values():
                    bf.writers, bf.readers = [], []
                continue
            o = item
            for (bf, k) in o.reads:
                note(bf)
                for (pi, pk) in bf.writers:
                    if _conf(k, pk):
                        pairs.append((pi, o.idx))
                bf.readers.append((o.idx, k))
            for (bf, k) in o.writes:
                note(bf)
                for (pi, pk) in bf.writers:
                    if _conf(k, pk):
                        pairs.append((pi, o.idx))
                for (pi, pk) in bf.readers:
                    if _conf(k, pk) and pi != o.idx:
                        pairs.append((pi, o.idx))
                bf.writers = [(x, kk) for (x, kk) in bf.writers
                              if not _conf(k, kk)]
                bf.readers = [(x, kk) for (x, kk) in bf.readers
                              if not _conf(k, kk)]
                bf.writers.append((o.idx, k))
            if o.is_dma:
                dma_issued[o.sem.name] = (o.sem, o.sem_target)

        idx2op = {o.idx: o for o in self.ops if o != "BARRIER"}
        for (pi, ci) in pairs:
            p = idx2op[pi]
            if not p.is_dma and not (p.eng == "pe" and idx2op[ci].eng == "pe"):
                p.inc = True
        seg_last = {}
        for o in self.ops:
            if o == "BARRIER":
                for g, lo in seg_last.items():
                    lo.inc = True
                seg_last = {}
                continue
            if not o.is_dma:
                seg_last[o.eng] = o
        cnt = {g: 0 for g in self.ENGS}
        bar_snap = []
        for o in self.ops:
            if o == "BARRIER":
                bar_snap.append(dict(cnt))
                continue
            if o.is_dma:
                continue
            if o.inc:
                cnt[o.eng] += 1
                o.inc_seq = cnt[o.eng]
        for (pi, ci) in pairs:
            p, c = idx2op[pi], idx2op[ci]
            if p.is_dma:
                c.waits.append((p.sem, p.sem_target))
            elif not (p.eng == "pe" and c.eng == "pe"):
                c.waits.append((self.engsem[p.eng], p.inc_seq))

        floors = {g: {} for g in self.ENGS}
        bar_i = 0
        for item in self.ops:
            if item == "BARRIER":
                snap = bar_snap[bar_i]
                dsems = bar_dma[bar_i]
                bar_i += 1
                for g in self.ENGS:
                    fl = floors[g]
                    for g2 in self.ENGS:
                        nm = self.engsem[g2].name
                        if g2 != g and snap[g2] > 0 and \
                                fl.get(nm, -1) < snap[g2]:
                            self.e[g].wait_ge(self.engsem[g2], snap[g2])
                            fl[nm] = snap[g2]
                    for (sem, v) in dsems:
                        if fl.get(sem.name, -1) < v:
                            self.e[g].wait_ge(sem, v)
                            fl[sem.name] = v
                continue
            o = item
            fl = floors[o.eng]
            best = {}
            for (sem, val) in o.waits:
                if val > best.get(sem.name, (None, -1))[1]:
                    best[sem.name] = (sem, val)
            for kname, (sem, val) in best.items():
                if fl.get(kname, -1) < val:
                    self.e[o.eng].wait_ge(sem, val)
                    fl[kname] = val
            inst = o.fn(self.e[o.eng])
            if o.is_dma:
                inst.then_inc(o.sem, 16)
            elif o.inc:
                inst.then_inc(self.engsem[o.eng], 1)



def build_nc(stage=STAGE):
    nc = bass.Bass("TRN2", target_bir_lowering=False)
    stack = ExitStack()
    with stack:
        _body(nc, stack, stage)
    return nc


def _body(nc, stack, stage):
    b = Bld(nc, stack)

    def din(name, shape, dtype=F32):
        return Buf(name, nc.dram_tensor(name, shape, dtype,
                                        kind="ExternalInput"), "dr")

    xs_ext = din("xs_ext", [EXT, D])
    xkv = din("xkv", [OWN, D])
    mpref16 = din("mpref16", [16, 1])
    mhalo = din("mhalo", [128, 1])
    identf = din("identf", [128, 128])
    identb = din("identb", [128, 128], BF16)
    w_lng = din("w_lng", [D]);        w_lnb = din("w_lnb", [D])
    w_in = din("w_in", [D, 2 * DI])
    w_convw = din("w_convw", [KCONV, DI])
    w_convb = din("w_convb", [DI])
    w_alog = din("w_alog", [1, NST])
    w_bmat = din("w_bmat", [NST, DI])
    w_cmat = din("w_cmat", [DI, NST])
    w_dvec = din("w_dvec", [DI])
    w_wd = din("w_wd", [DI, DI]);     w_bd = din("w_bd", [DI])
    w_wout = din("w_wout", [DI, D])
    w_wq = din("w_wq", [D, D]);       w_bq = din("w_bq", [D])
    w_wk = din("w_wk", [D, D]);       w_bk = din("w_bk", [D])
    w_wv = din("w_wv", [D, D]);       w_bv = din("w_bv", [D])
    w_wo = din("w_wo", [D, D]);       w_bo = din("w_bo", [1, D])
    w_rw = din("w_rw", [D]);          w_rb = din("w_rb", [1, 1])
    w_w1 = din("w_w1", [NE, D, DFF]); w_b1 = din("w_b1", [NE, DFF])
    w_w2 = din("w_w2", [NE, DFF, D]); w_b2 = din("w_b2", [NE, D])
    w_ln1g = din("w_ln1g", [D]);      w_ln1b = din("w_ln1b", [D])
    w_ln2g = din("w_ln2g", [D]);      w_ln2b = din("w_ln2b", [D])
    out = Buf("out", nc.dram_tensor("out", [OWN, D], F32,
                                    kind="ExternalOutput"), "dr")

    d_xnbf = b.dr("d_xnbf", [NKV, D], BF16)
    d_xn32 = b.dr("d_xn32", [OWN, D], F32)
    d_xhbf = b.dr("d_xhbf", [EXT, D], BF16)
    d_KT = b.dr("d_KT", [D, NKV], BF16)
    d_VT = b.dr("d_VT", [D, NKV], BF16)
    d_QT = b.dr("d_QT", [D, OWN], BF16)
    d_xscbf = b.dr("d_xscbf", [DI, EXT], BF16)
    d_xsc32 = b.dr("d_xsc32", [DI, OWN], F32)
    d_gs32 = b.dr("d_gs32", [DI, OWN], F32)
    d_x1bf = b.dr("d_x1bf", [OWN, D], BF16)
    d_x132 = b.dr("d_x132", [OWN, D], F32)
    d_g32 = b.dr("d_g32", [OWN, D], F32)

    P = dict(pers=True)
    ident_f = b.sb("ident_f", [128, 128], F32, **P)
    ident_b = b.sb("ident_b", [128, 128], BF16, **P)
    c_eps = b.sb("c_eps", [128, 1], F32, **P)
    c_nhalf = b.sb("c_nhalf", [128, 1], F32, **P)
    c_one = b.sb("c_one", [128, 1], F32, **P)
    c_oner = b.sb("c_oner", [1, 128], F32, **P)
    c_eighth = b.sb("c_eighth", [128, 1], F32, **P)
    c_i2048 = b.sb("c_i2048", [128, 1], F32, **P)
    hTb = b.sb("hTb", [NST, OWN], BF16, **P)
    idx_cols = b.sb("idx_cols", [128, 16], F32, **P)
    s_mpref = b.sb("s_mpref", [16, 1], F32, **P)
    s_mhalo = b.sb("s_mhalo", [128, 1], F32, **P)
    a_col = b.sb("a_col", [16, 1], F32, **P)
    NLN = 8
    ln_sets = []
    for si in range(NLN):
        ln_sets.append({n: b.sb(f"ln_{n}{si}", [128, 1], F32, **P)
                        for n in ("sum", "ssq", "mu", "e2", "ms", "var", "rs")})
    bq_c = b.sb("bq_c", [128, 8], F32, **P)
    bk_c = b.sb("bk_c", [128, 8], F32, **P)
    bv_c = b.sb("bv_c", [128, 8], F32, **P)
    bd_c = b.sb("bd_c", [128, 16], F32, **P)
    cb_c = b.sb("cb_c", [128, 16], F32, **P)
    dv_c = b.sb("dv_c", [128, 16], F32, **P)
    cw_c = [b.sb(f"cw{k}_c", [128, 16], F32, **P) for k in range(KCONV)]
    rb_c = b.sb("rb_c", [128, 1], F32, **P)
    xn_r_r = [b.sb(f"xn_rr{i}", [128, D], F32, **P) for i in range(2)]

    _ln_i = [0]

    def ln_tile_ops(x_t, out_t, width=D):
        s = ln_sets[_ln_i[0] % NLN]
        _ln_i[0] += 1
        L = {n: (bb, ALL) for n, bb in s.items()}
        o = []
        o.append(("op", "dve", lambda e, s=s, x_t=x_t: e.reduce_sum(
            out=s["sum"].h[:], in_=x_t.h[:], axis=AX),
            [(x_t, ALL)], [L["sum"]]))
        o.append(("op", "act", lambda e, s=s, x_t=x_t, out_t=out_t,
                  width=width: e.activation(
                      out=out_t.h[:, 0:width], in_=x_t.h[:], func=AF.Square,
                      accum_out=s["ssq"].h[:]),
                  [(x_t, ALL)], [(out_t, ALL), L["ssq"]]))
        o.append(("op", "dve", lambda e, s=s, width=width: e.tensor_scalar(
            out=s["mu"].h[:], in0=s["sum"].h[:], scalar1=1.0 / width,
            scalar2=None, op0=ALU.mult), [L["sum"]], [L["mu"]]))
        o.append(("op", "dve", lambda e, s=s, width=width: e.tensor_scalar(
            out=s["e2"].h[:], in0=s["ssq"].h[:], scalar1=1.0 / width,
            scalar2=None, op0=ALU.mult), [L["ssq"]], [L["e2"]]))
        o.append(("op", "dve", lambda e, s=s: e.tensor_tensor(
            out=s["ms"].h[:], in0=s["mu"].h[:], in1=s["mu"].h[:],
            op=ALU.mult), [L["mu"]], [L["ms"]]))
        o.append(("op", "dve", lambda e, s=s: e.tensor_sub(
            out=s["var"].h[:], in0=s["e2"].h[:], in1=s["ms"].h[:]),
            [L["e2"], L["ms"]], [L["var"]]))
        o.append(("op", "act", lambda e, s=s: e.activation(
            out=s["rs"].h[:], in_=s["var"].h[:], func=AF.Ln,
            bias=c_eps.h[:]), [L["var"], (c_eps, ALL)], [L["rs"]]))
        o.append(("op", "act", lambda e, s=s: e.activation(
            out=s["rs"].h[:], in_=s["rs"].h[:], func=AF.Exp,
            scale=c_nhalf.h[:]), [L["rs"], (c_nhalf, ALL)], [L["rs"]]))
        o.append(("op", "dve", lambda e, s=s, x_t=x_t, out_t=out_t:
                  e.tensor_scalar(
                      out=out_t.h[:], in0=x_t.h[:], scalar1=s["mu"].h[:],
                      scalar2=s["rs"].h[:], op0=ALU.subtract, op1=ALU.mult),
                  [(x_t, ALL), L["mu"], L["rs"]], [(out_t, ALL)]))
        return o

    def ln_tile(x_t, out_t, width=D):
        for (kind, eng, fn, r, w) in ln_tile_ops(x_t, out_t, width):
            b.op(eng, fn, r=r, w=w)
        return

    def _ln_tile_body(x_t, out_t, width=D):
        s = ln_sets[_ln_i[0] % NLN]
        _ln_i[0] += 1
        L = {n: (bb, ALL) for n, bb in s.items()}
        b.op("dve", lambda e: e.reduce_sum(out=s["sum"].h[:], in_=x_t.h[:],
                                           axis=AX),
             r=[(x_t, ALL)], w=[L["sum"]])
        b.op("act", lambda e: e.activation(out=out_t.h[:, 0:width],
                                           in_=x_t.h[:], func=AF.Square,
                                           accum_out=s["ssq"].h[:]),
             r=[(x_t, ALL)], w=[(out_t, ALL), L["ssq"]])
        b.op("dve", lambda e: e.tensor_scalar(
            out=s["mu"].h[:], in0=s["sum"].h[:], scalar1=1.0 / width,
            scalar2=None, op0=ALU.mult),
            r=[L["sum"]], w=[L["mu"]])
        b.op("dve", lambda e: e.tensor_scalar(
            out=s["e2"].h[:], in0=s["ssq"].h[:], scalar1=1.0 / width,
            scalar2=None, op0=ALU.mult),
            r=[L["ssq"]], w=[L["e2"]])
        b.op("dve", lambda e: e.tensor_tensor(
            out=s["ms"].h[:], in0=s["mu"].h[:], in1=s["mu"].h[:],
            op=ALU.mult),
            r=[L["mu"]], w=[L["ms"]])
        b.op("dve", lambda e: e.tensor_sub(out=s["var"].h[:],
                                           in0=s["e2"].h[:],
                                           in1=s["ms"].h[:]),
             r=[L["e2"], L["ms"]], w=[L["var"]])
        b.op("act", lambda e: e.activation(out=s["rs"].h[:], in_=s["var"].h[:],
                                           func=AF.Ln, bias=c_eps.h[:]),
             r=[L["var"], (c_eps, ALL)], w=[L["rs"]])
        b.op("act", lambda e: e.activation(out=s["rs"].h[:], in_=s["rs"].h[:],
                                           func=AF.Exp, scale=c_nhalf.h[:]),
             r=[L["rs"], (c_nhalf, ALL)], w=[L["rs"]])
        b.op("dve", lambda e: e.tensor_scalar(
            out=out_t.h[:], in0=x_t.h[:], scalar1=s["mu"].h[:],
            scalar2=s["rs"].h[:], op0=ALU.subtract, op1=ALU.mult),
            r=[(x_t, ALL), L["mu"], L["rs"]], w=[(out_t, ALL)])

    def bcast_row(src, dst, cols=D):
        if getattr(b, "_bc_row", None) is None:
            b._bc_row = b.sb(f"bcr_{dst.name}", [1, D], F32)
        row = b._bc_row
        if getattr(b, "_bc_ps", None) is None:
            b._bc_ps = b.ps(f"bcp_{dst.name}", [128, 512], F32)
        psb = b._bc_ps
        b.dma("sp", lambda e: e.dma_start(
            out=row.h[:], in_=src.h.ap().rearrange("(a d) -> a d", a=1)),
            r=[], w=[(row, ALL)])
        for j0 in range(0, cols, 512):
            n = min(512, cols - j0)
            b.op("pe", lambda e, j0=j0, n=n: e.matmul(
                psb.h[:, 0:n], c_oner.h[0:1, :], row.h[0:1, j0:j0 + n],
                start=True, stop=True),
                r=[(c_oner, ALL), (row, ALL)], w=[(psb, ALL)])
            b.op("dve", lambda e, j0=j0, n=n: e.tensor_copy(
                out=dst.h[:, j0:j0 + n], in_=psb.h[:, 0:n]),
                r=[(psb, ALL)], w=[(dst, j0)])

    ps_set = b.ps("ps_set", [128, 512], F32)

    def load_cols(dst, ap2d, nch):
        row = b.sb(f"lc_{dst.name}", [nch, 128], F32)
        b.dma("sp", lambda e: e.dma_start(out=row.h[:], in_=ap2d),
              r=[], w=[(row, ALL)])
        b.op("pe", lambda e: e.transpose(ps_set.h[:, 0:nch], row.h[:],
                                         ident_f.h[0:nch, 0:nch]),
             r=[(row, ALL), (ident_f, ALL)], w=[(ps_set, ALL)])
        b.op("dve", lambda e: e.tensor_copy(out=dst.h[:], in_=ps_set.h[:, 0:nch]),
             r=[(ps_set, ALL)], w=[(dst, ALL)])

    b.dma("sp", lambda e: e.dma_start(out=ident_f.h[:], in_=identf.h.ap()),
          w=[(ident_f, ALL)])
    b.dma("sp", lambda e: e.dma_start(out=ident_b.h[:], in_=identb.h.ap()),
          w=[(ident_b, ALL)])
    b.dma("sp", lambda e: e.dma_start(out=s_mpref.h[:], in_=mpref16.h.ap()),
          w=[(s_mpref, ALL)])
    b.dma("sp", lambda e: e.dma_start(out=s_mhalo.h[:], in_=mhalo.h.ap()),
          w=[(s_mhalo, ALL)])
    for buf, val in ((c_eps, EPS), (c_nhalf, -0.5), (c_one, 1.0),
                     (c_eighth, 0.125), (c_i2048, 1.0 / 2048)):
        b.op("dve", lambda e, buf=buf, val=val: e.memset(buf.h[:], val),
             w=[(buf, ALL)])
    b.op("dve", lambda e: e.memset(c_oner.h[:], 1.0), w=[(c_oner, ALL)])

    load_cols(bq_c, w_bq.h.ap().rearrange("(c p) -> c p", p=128), 8)
    load_cols(bk_c, w_bk.h.ap().rearrange("(c p) -> c p", p=128), 8)
    load_cols(bv_c, w_bv.h.ap().rearrange("(c p) -> c p", p=128), 8)
    load_cols(bd_c, w_bd.h.ap().rearrange("(c p) -> c p", p=128), 16)
    load_cols(cb_c, w_convb.h.ap().rearrange("(c p) -> c p", p=128), 16)
    load_cols(dv_c, w_dvec.h.ap().rearrange("(c p) -> c p", p=128), 16)
    for k in range(KCONV):
        load_cols(cw_c[k],
                  w_convw.h[k:k + 1, :].rearrange("a (c p) -> (a c) p", p=128),
                  16)
    rb_row = b.sb("rb_row", [1, 1], F32)
    b.dma("sp", lambda e: e.dma_start(out=rb_row.h[:], in_=w_rb.h.ap()),
          w=[(rb_row, ALL)])
    b.op("pe", lambda e: e.matmul(ps_set.h[:, 0:1], c_oner.h[0:1, :],
                                  rb_row.h[:], start=True, stop=True),
         r=[(c_oner, ALL), (rb_row, ALL)], w=[(ps_set, ALL)])
    b.op("dve", lambda e: e.tensor_copy(out=rb_c.h[:], in_=ps_set.h[:, 0:1]),
         r=[(ps_set, ALL)], w=[(rb_c, ALL)])
    alog_r = b.sb("alog_r", [1, NST], F32)
    b.dma("sp", lambda e: e.dma_start(out=alog_r.h[:], in_=w_alog.h.ap()),
          w=[(alog_r, ALL)])
    b.op("pe", lambda e: e.transpose(ps_set.h[0:NST, 0:1], alog_r.h[:],
                                     ident_f.h[0:1, 0:1]),
         r=[(alog_r, ALL), (ident_f, ALL)], w=[(ps_set, ALL)])
    b.op("act", lambda e: e.activation(out=a_col.h[:], in_=ps_set.h[0:NST, 0:1],
                                       func=AF.Exp),
         r=[(ps_set, ALL)], w=[(a_col, ALL)])
    b.op("dve", lambda e: e.tensor_scalar(
        out=a_col.h[:], in0=a_col.h[:], scalar1=-1.0 / 2048, scalar2=None,
        op0=ALU.mult), r=[(a_col, ALL)], w=[(a_col, ALL)])

    wq_sb = b.sb("wq_sb", [128, 8 * D], BF16)
    wk_sb = b.sb("wk_sb", [128, 8 * D], BF16)
    wv_sb = b.sb("wv_sb", [128, 8 * D], BF16)
    for (wsb, wdr) in ((wq_sb, w_wq), (wk_sb, w_wk), (wv_sb, w_wv)):
        for dc in range(8):
            b.dma("pool", lambda e, wsb=wsb, wdr=wdr, dc=dc: e.dma_start(
                out=wsb.h[:, dc * D:(dc + 1) * D],
                in_=wdr.h[dc * 128:(dc + 1) * 128, :]), w=[(wsb, dc)])
    p1_x = [b.sb(f"p1_x{i}", [128, D], F32) for i in range(4)]
    p1_xn = [b.sb(f"p1_xn{i}", [128, D], F32) for i in range(4)]
    xnfm = [b.sb(f"xnfm{i}", [128, 8 * 512], BF16) for i in range(2)]
    kvq_ps = b.ps("kvq_ps", [128, 512], F32)
    kvq_ps2 = b.ps("kvq_ps2", [128, 512], F32)
    kvq_ev = [b.sb(f"kvq_ev{i}", [128, 512], BF16) for i in range(2)]

    def p1_ln_ops(i):
        own = i < 16
        src = xs_ext.h if own else xkv.h
        r0 = (PRE + i * 128) if own else (i - 16) * 128
        xt, xnt = p1_x[i % 4], p1_xn[i % 4]
        o = [("dma", "sp", lambda e, src=src, r0=r0, xt=xt: e.dma_start(
            out=xt.h[:], in_=src[r0:r0 + 128, :]), [], [(xt, ALL)])]
        o += ln_tile_ops(xt, xnt)
        if own:
            o.append(("dma", "pool", lambda e, i=i, xnt=xnt: e.dma_start(
                out=d_xn32.h[i * 128:(i + 1) * 128, :], in_=xnt.h[:]),
                [(xnt, ALL)], [(d_xn32, i)]))
        o.append(("dma", "pool", lambda e, i=i, xnt=xnt: e.dma_start(
            out=d_xnbf.h[i * 128:(i + 1) * 128, :], in_=xnt.h[:]),
            [(xnt, ALL)], [(d_xnbf, i)]))
        return o

    b.wave([p1_ln_ops(i) for i in range(4)])
    for tt in range(8):
        xf = xnfm[tt % 2]
        for dc in range(8):
            b.dma("act", lambda e, tt=tt, dc=dc, xf=xf: e.dma_start(
                out=xf.h[:, dc * 512:(dc + 1) * 512],
                in_=d_xnbf.h[tt * 512:(tt + 1) * 512,
                             dc * 128:(dc + 1) * 128], transpose=True),
                r=[(d_xnbf, ALL)], w=[(xf, dc)])
        grp = [p1_ln_ops(i) for i in range(4 * tt + 4, min(4 * tt + 8, 32))]
        if grp:
            b.wave(grp)
        plan = [(wk_sb, bk_c, d_KT), (wv_sb, bv_c, d_VT)]
        if tt < 4:
            plan.append((wq_sb, bq_c, d_QT))
        for (wsb, bcols, ddst) in plan:
            for oc in range(8):
                ps = kvq_ps if oc % 2 == 0 else kvq_ps2
                ev = kvq_ev[oc % 2]
                for dc in range(8):
                    b.op("pe", lambda e, ps=ps, wsb=wsb, oc=oc, dc=dc, xf=xf:
                         e.matmul(
                             ps.h[:],
                             wsb.h[:, dc * D + oc * 128:dc * D + (oc + 1) * 128],
                             xf.h[:, dc * 512:(dc + 1) * 512],
                             start=(dc == 0), stop=(dc == 7)),
                         r=[(wsb, dc), (xf, dc)], w=[(ps, ALL)])
                b.op("act", lambda e, ps=ps, bcols=bcols, oc=oc, ev=ev:
                     e.activation(
                         out=ev.h[:], in_=ps.h[:], func=AF.Identity,
                         bias=bcols.h[:, oc:oc + 1]),
                     r=[(ps, ALL), (bcols, ALL)], w=[(ev, ALL)])
                b.dma("sp", lambda e, ddst=ddst, oc=oc, tt=tt, ev=ev:
                      e.dma_start(
                          out=ddst.h[oc * 128:(oc + 1) * 128,
                                     tt * 512:(tt + 1) * 512], in_=ev.h[:]),
                      r=[(ev, ALL)], w=[(ddst, (oc, tt))])

    if stage < 2:
        _tail(b, nc, out, [d_xnbf, d_xn32, d_KT, d_VT, d_QT])
        return

    b.barrier()
    gb_g = b.sb("gb_g", [128, D], F32)
    gb_b = b.sb("gb_b", [128, D], F32)
    bcast_row(w_lng, gb_g)
    bcast_row(w_lnb, gb_b)
    wxs_sb = b.sb("wxs_sb", [128, 8 * DI], BF16)
    wg_sb = b.sb("wg_sb", [128, 8 * DI], BF16)
    for dc in range(8):
        b.dma("pool", lambda e, dc=dc: e.dma_start(
            out=wxs_sb.h[:, dc * DI:(dc + 1) * DI],
            in_=w_in.h[dc * 128:(dc + 1) * 128, 0:DI]), w=[(wxs_sb, dc)])
        b.dma("pool", lambda e, dc=dc: e.dma_start(
            out=wg_sb.h[:, dc * DI:(dc + 1) * DI],
            in_=w_in.h[dc * 128:(dc + 1) * 128, DI:2 * DI]), w=[(wg_sb, dc)])
    p2_x = [b.sb(f"p2_x{i}", [128, D], F32) for i in range(4)]
    p2_xn = [b.sb(f"p2_xn{i}", [128, D], F32) for i in range(4)]
    p2_xh = [b.sb(f"p2_xh{i}", [128, D], F32) for i in range(4)]

    def p2_ln_ops(i):
        xt, xnt, xht = p2_x[i % 4], p2_xn[i % 4], p2_xh[i % 4]
        o = [("dma", "sp", lambda e, i=i, xt=xt: e.dma_start(
            out=xt.h[:], in_=xs_ext.h[i * 128:(i + 1) * 128, :]),
            [], [(xt, ALL)])]
        o += ln_tile_ops(xt, xnt)
        o += ln_tile_ops(xnt, xht)
        o.append(("op", "dve", lambda e, xht=xht: e.tensor_tensor(
            out=xht.h[:], in0=xht.h[:], in1=gb_g.h[:], op=ALU.mult),
            [(xht, ALL), (gb_g, ALL)], [(xht, ALL)]))
        o.append(("op", "pool", lambda e, xht=xht: e.tensor_tensor(
            out=xht.h[:], in0=xht.h[:], in1=gb_b.h[:], op=ALU.add),
            [(xht, ALL), (gb_b, ALL)], [(xht, ALL)]))
        o.append(("dma", "pool", lambda e, i=i, xht=xht: e.dma_start(
            out=d_xhbf.h[i * 128:(i + 1) * 128, :], in_=xht.h[:]),
            [(xht, ALL)], [(d_xhbf, i)]))
        return o

    b.wave([p2_ln_ops(i) for i in range(4)])
    xhfm = b.sb("xhfm", [128, 8 * 512], BF16)
    xsp_r = [b.sb(f"xsp{i}", [128, 515], F32) for i in range(2)]
    xsp_tail = b.sb("xsp_tail", [128, 16 * 3], F32)
    win_ps = b.ps("win_ps", [128, 512], F32)
    win_ps2 = b.ps("win_ps2", [128, 512], F32)
    conv_ps = b.ps("conv_ps", [128, 512], F32)
    conv_ps2 = b.ps("conv_ps2", [128, 512], F32)
    xsc_t_r = [b.sb(f"xsc_t{i}", [128, 512], F32) for i in range(2)]
    gt_t_r = [b.sb(f"gt_t{i}", [128, 512], F32) for i in range(2)]
    cwd = [b.sb(f"cwd{k}", [128, 16 * 128], F32) for k in range(KCONV)]
    for k in range(KCONV):
        for oc in range(16):
            b.op("dve", lambda e, k=k, oc=oc: e.tensor_scalar(
                out=cwd[k].h[:, oc * 128:(oc + 1) * 128], in0=ident_f.h[:],
                scalar1=cw_c[k].h[:, oc:oc + 1], scalar2=None, op0=ALU.mult),
                r=[(ident_f, ALL), (cw_c[k], ALL)], w=[(cwd[k], oc)])
    b.op("dve", lambda e: e.memset(xsp_tail.h[:], 0.0), w=[(xsp_tail, ALL)])
    for tt in range(8):
        for dc in range(8):
            b.dma("act", lambda e, tt=tt, dc=dc: e.dma_start(
                out=xhfm.h[:, dc * 512:(dc + 1) * 512],
                in_=d_xhbf.h[tt * 512:(tt + 1) * 512,
                             dc * 128:(dc + 1) * 128], transpose=True),
                r=[(d_xhbf, ALL)], w=[(xhfm, dc)])
        grp = [p2_ln_ops(i) for i in range(4 * tt + 4, min(4 * tt + 8, 32))]
        if grp:
            b.wave(grp)
        def p2a_A(oc):
            ps = win_ps if oc % 2 == 0 else win_ps2
            for dc in range(8):
                b.op("pe", lambda e, ps=ps, oc=oc, dc=dc: e.matmul(
                    ps.h[:],
                    wxs_sb.h[:, dc * DI + oc * 128:dc * DI + (oc + 1) * 128],
                    xhfm.h[:, dc * 512:(dc + 1) * 512],
                    start=(dc == 0), stop=(dc == 7)),
                    r=[(wxs_sb, dc), (xhfm, dc)], w=[(ps, ALL)])

        def p2a_B(oc, tt):
            ps = win_ps if oc % 2 == 0 else win_ps2
            cps = conv_ps if oc % 2 == 0 else conv_ps2
            xsp = xsp_r[oc % 2]
            b.op("act", lambda e, ps=ps, xsp=xsp: e.activation(
                out=xsp.h[:, 3:515], in_=ps.h[:], func=AF.Identity),
                r=[(ps, ALL)], w=[(xsp, "b")])
            if tt == 3:
                b.op("dve", lambda e, xsp=xsp: e.tensor_scalar(
                    out=xsp.h[:, 512:515], in0=xsp.h[:, 512:515],
                    scalar1=s_mhalo.h[:], scalar2=None, op0=ALU.mult),
                    r=[(xsp, "b"), (s_mhalo, ALL)], w=[(xsp, "b")])
            b.op("dve", lambda e, oc=oc, xsp=xsp: e.tensor_copy(
                out=xsp.h[:, 0:3], in_=xsp_tail.h[:, oc * 3:(oc + 1) * 3]),
                r=[(xsp_tail, oc)], w=[(xsp, "h")])
            b.op("dve", lambda e, oc=oc, xsp=xsp: e.tensor_copy(
                out=xsp_tail.h[:, oc * 3:(oc + 1) * 3],
                in_=xsp.h[:, 512:515]),
                r=[(xsp, "b")], w=[(xsp_tail, oc)])
            xsc_t = xsc_t_r[oc % 2]
            for k in range(KCONV):
                b.op("pe", lambda e, oc=oc, k=k, cps=cps, xsp=xsp: e.matmul(
                    cps.h[:], cwd[k].h[:, oc * 128:(oc + 1) * 128],
                    xsp.h[:, k:k + 512],
                    start=(k == 0), stop=(k == KCONV - 1)),
                    r=[(cwd[k], oc), (xsp, "b"), (xsp, "h")],
                    w=[(cps, ALL)])
            b.op("act", lambda e, oc=oc, cps=cps, xsc_t=xsc_t: e.activation(
                out=xsc_t.h[:], in_=cps.h[:], func=AF.Silu,
                bias=cb_c.h[:, oc:oc + 1]),
                r=[(cps, ALL), (cb_c, ALL)], w=[(xsc_t, ALL)])
            b.dma("pool", lambda e, tt=tt, oc=oc, xsc_t=xsc_t: e.dma_start(
                out=d_xscbf.h[oc * 128:(oc + 1) * 128,
                              tt * 512:(tt + 1) * 512], in_=xsc_t.h[:]),
                r=[(xsc_t, ALL)], w=[(d_xscbf, (oc, tt))])
            if tt >= 4:
                b.dma("sp", lambda e, tt=tt, oc=oc, xsc_t=xsc_t: e.dma_start(
                    out=d_xsc32.h[oc * 128:(oc + 1) * 128,
                                  (tt - 4) * 512:(tt - 3) * 512],
                    in_=xsc_t.h[:]),
                    r=[(xsc_t, ALL)], w=[(d_xsc32, (oc, tt))])

        p2a_A(0)
        for oc in range(16):
            if oc < 15:
                p2a_A(oc + 1)
            p2a_B(oc, tt)
        if tt >= 4:
            for oc in range(16):
                ps = win_ps if oc % 2 == 0 else win_ps2
                for dc in range(8):
                    b.op("pe", lambda e, ps=ps, oc=oc, dc=dc: e.matmul(
                        ps.h[:],
                        wg_sb.h[:, dc * DI + oc * 128:dc * DI + (oc + 1) * 128],
                        xhfm.h[:, dc * 512:(dc + 1) * 512],
                        start=(dc == 0), stop=(dc == 7)),
                        r=[(wg_sb, dc), (xhfm, dc)], w=[(ps, ALL)])
                gt_t = gt_t_r[oc % 2]
                b.op("act", lambda e, ps=ps, gt_t=gt_t: e.activation(
                    out=gt_t.h[:], in_=ps.h[:], func=AF.Silu),
                    r=[(ps, ALL)], w=[(gt_t, ALL)])
                b.dma("sp", lambda e, tt=tt, oc=oc, gt_t=gt_t: e.dma_start(
                    out=d_gs32.h[oc * 128:(oc + 1) * 128,
                                 (tt - 4) * 512:(tt - 3) * 512],
                    in_=gt_t.h[:]),
                    r=[(gt_t, ALL)], w=[(d_gs32, (oc, tt))])

    b.barrier()
    wd_sb = b.sb("wd_sb", [128, 16 * DI], BF16)
    for dc in range(16):
        b.dma("pool", lambda e, dc=dc: e.dma_start(
            out=wd_sb.h[:, dc * DI:(dc + 1) * DI],
            in_=w_wd.h[dc * 128:(dc + 1) * 128, :]), w=[(wd_sb, dc)])
    bmat_sb = b.sb("bmat_sb", [NST, DI], F32)
    bt_sb = b.sb("bt_sb", [128, 16 * NST], BF16)
    ps_bt = b.ps("ps_bt", [128, 512], F32)
    b.dma("sp", lambda e: e.dma_start(out=bmat_sb.h[:], in_=w_bmat.h.ap()),
          w=[(bmat_sb, ALL)])
    for dc in range(16):
        b.op("pe", lambda e, dc=dc: e.transpose(
            ps_bt.h[:, 0:NST], bmat_sb.h[:, dc * 128:(dc + 1) * 128],
            ident_f.h[0:NST, 0:NST]),
            r=[(bmat_sb, ALL), (ident_f, ALL)], w=[(ps_bt, ALL)])
        b.op("act", lambda e, dc=dc: e.activation(
            out=bt_sb.h[:, dc * NST:(dc + 1) * NST], in_=ps_bt.h[:, 0:NST],
            func=AF.Identity, scale=c_i2048.h[:]),
            r=[(ps_bt, ALL), (c_i2048, ALL)], w=[(bt_sb, dc)])

    xscb_r = [b.sb(f"xscb{i}", [128, 16 * 512], BF16) for i in range(2)]
    z_ps = b.ps("z_ps", [128, 512], F32)
    z_ps2 = b.ps("z_ps2", [128, 512], F32)
    ez_t_r = [b.sb(f"ez_t{i}", [128, 512], F32) for i in range(2)]
    spz_t_r = [b.sb(f"spz_t{i}", [128, 512], F32) for i in range(2)]
    del_ps = b.ps("del_ps", [1, 512], F32)
    bx_ps = b.ps("bx_ps", [NST, 512], F32)
    db_ps = b.ps("db_ps", [NST, 512], F32)
    del_row_r = [b.sb(f"del_row{i}", [1, 512], F32) for i in range(2)]
    db_sb_r = [b.sb(f"db_sb{i}", [NST, 512], F32) for i in range(2)]
    dA = b.sb("dA", [NST, EXT], F32)
    BxT = b.sb("BxT", [NST, EXT], F32)
    hT = b.sb("hT", [NST, EXT], F32)

    for tt in range(8):
        xscb = xscb_r[tt % 2]
        for dc in range(16):
            b.dma("sp", lambda e, tt=tt, dc=dc, xscb=xscb: e.dma_start(
                out=xscb.h[:, dc * 512:(dc + 1) * 512],
                in_=d_xscbf.h[dc * 128:(dc + 1) * 128,
                              tt * 512:(tt + 1) * 512]),
                r=[(d_xscbf, (dc, tt))], w=[(xscb, dc)])
        del_row = del_row_r[tt % 2]
        db_sb = db_sb_r[tt % 2]

        def p2b_A(zc):
            ps = z_ps if zc % 2 == 0 else z_ps2
            for dc in range(16):
                b.op("pe", lambda e, ps=ps, zc=zc, dc=dc, xscb=xscb: e.matmul(
                    ps.h[:],
                    wd_sb.h[:, dc * DI + zc * 128:dc * DI + (zc + 1) * 128],
                    xscb.h[:, dc * 512:(dc + 1) * 512],
                    start=(dc == 0), stop=(dc == 15)),
                    r=[(wd_sb, dc), (xscb, dc)], w=[(ps, ALL)])

        def p2b_B(zc):
            ps = z_ps if zc % 2 == 0 else z_ps2
            ez_t, spz_t = ez_t_r[zc % 2], spz_t_r[zc % 2]
            b.op("act", lambda e, ps=ps, zc=zc, ez_t=ez_t: e.activation(
                out=ez_t.h[:], in_=ps.h[:], func=AF.Exp,
                bias=bd_c.h[:, zc:zc + 1]),
                r=[(ps, ALL), (bd_c, ALL)], w=[(ez_t, ALL)])
            b.op("act", lambda e, ez_t=ez_t, spz_t=spz_t: e.activation(
                out=spz_t.h[:], in_=ez_t.h[:], func=AF.Ln, bias=c_one.h[:]),
                r=[(ez_t, ALL), (c_one, ALL)], w=[(spz_t, ALL)])
            b.op("pe", lambda e, zc=zc, spz_t=spz_t: e.matmul(
                del_ps.h[:], c_one.h[:, 0:1], spz_t.h[:],
                start=(zc == 0), stop=(zc == 15)),
                r=[(c_one, ALL), (spz_t, ALL)], w=[(del_ps, ALL)])

        p2b_A(0)
        for zc in range(16):
            if zc < 15:
                p2b_A(zc + 1)
            p2b_B(zc)
        b.op("dve", lambda e, del_row=del_row: e.tensor_copy(
            out=del_row.h[:], in_=del_ps.h[:]),
            r=[(del_ps, ALL)], w=[(del_row, ALL)])
        for dc in range(16):
            b.op("pe", lambda e, dc=dc, xscb=xscb: e.matmul(
                bx_ps.h[:], bt_sb.h[:, dc * NST:(dc + 1) * NST],
                xscb.h[:, dc * 512:(dc + 1) * 512],
                start=(dc == 0), stop=(dc == 15)),
                r=[(bt_sb, dc), (xscb, dc)], w=[(bx_ps, ALL)])
        b.op("pe", lambda e, del_row=del_row: e.matmul(
            db_ps.h[:], c_oner.h[0:1, 0:NST], del_row.h[:],
            start=True, stop=True),
            r=[(c_oner, ALL), (del_row, ALL)], w=[(db_ps, ALL)])
        b.op("dve", lambda e, db_sb=db_sb: e.tensor_copy(
            out=db_sb.h[:], in_=db_ps.h[:]),
            r=[(db_ps, ALL)], w=[(db_sb, ALL)])
        b.op("act", lambda e, tt=tt, db_sb=db_sb: e.activation(
            out=dA.h[:, tt * 512:(tt + 1) * 512], in_=db_sb.h[:],
            func=AF.Exp, scale=a_col.h[:]),
            r=[(db_sb, ALL), (a_col, ALL)], w=[(dA, tt)])
        b.op("dve", lambda e, tt=tt, db_sb=db_sb: e.tensor_tensor(
            out=BxT.h[:, tt * 512:(tt + 1) * 512], in0=bx_ps.h[:],
            in1=db_sb.h[:], op=ALU.mult),
            r=[(bx_ps, ALL), (db_sb, ALL)], w=[(BxT, tt)])
        if tt < 4:
            b.op("dve", lambda e, tt=tt: e.tensor_scalar(
                out=BxT.h[:, tt * 512:(tt + 1) * 512],
                in0=BxT.h[:, tt * 512:(tt + 1) * 512],
                scalar1=s_mpref.h[:], scalar2=None, op0=ALU.mult),
                r=[(BxT, tt), (s_mpref, ALL)], w=[(BxT, tt)])

    b.op("dve", lambda e: e.tensor_tensor_scan(
        out=hT.h[:], data0=dA.h[:], data1=BxT.h[:], initial=0.0,
        op0=ALU.mult, op1=ALU.add),
        r=[(dA, ALL)] + [(dA, t) for t in range(8)] +
          [(BxT, ALL)] + [(BxT, t) for t in range(8)],
        w=[(hT, ALL)])
    b.op("dve", lambda e: e.tensor_copy(out=hTb.h[:], in_=hT.h[:, PRE:]),
         r=[(hT, ALL)], w=[(hTb, ALL)])
    if DEBUG:
        d_hT = b.dr("d_hT", [NST, EXT], F32)
        d_dA = b.dr("d_dA", [NST, EXT], F32)
        d_BxT = b.dr("d_BxT", [NST, EXT], F32)
        b.dma("pool", lambda e: e.dma_start(out=d_hT.h[:], in_=hT.h[:]),
              r=[(hT, ALL)], w=[(d_hT, ALL)])
        b.dma("pool", lambda e: e.dma_start(out=d_dA.h[:], in_=dA.h[:]),
              r=[(dA, ALL)] + [(dA, t) for t in range(8)], w=[(d_dA, ALL)])
        b.dma("pool", lambda e: e.dma_start(out=d_BxT.h[:], in_=BxT.h[:]),
              r=[(BxT, ALL)] + [(BxT, t) for t in range(8)], w=[(d_BxT, ALL)])
        globals()["_dbg_p2"] = [d_hT, d_dA, d_BxT]

    if stage < 3:
        _tail(b, nc, out, [d_xhbf, d_xscbf, d_xsc32, d_gs32]
              + globals().get("_dbg_p2", []))
        return

    b.barrier()
    g1_g = b.sb("g1_g", [128, D], F32)
    g1_b = b.sb("g1_b", [128, D], F32)
    rw_b = b.sb("rw_b", [128, D], F32)
    misc_ps = b.ps("misc_ps", [128, 512], F32)
    b._bc_ps = misc_ps
    bcast_row(w_ln1g, g1_g)
    bcast_row(w_ln1b, g1_b)
    bcast_row(w_rw, rw_b)
    kt_h = b.sb("kt_h", [64, NKV], BF16)
    vt_h = b.sb("vt_h", [64, NKV], BF16)
    qt_h = b.sb("qt_h", [64, OWN], BF16)
    vtok = b.sb("vtok", [128, 32 * 65], BF16)
    pt_r = [b.sb(f"pt_t{i}", [128, 1024], BF16) for i in range(4)]
    qsq = b.sb("qsq", [64, NKV], F32)
    r2row = b.sb("r2row", [1, 8], F32)
    m_sc = b.sb("m_sc", [1, 8], F32)
    negm = b.sb("negm", [128, 1], F32)
    ps_vtr = b.ps("ps_vtr", [128, 512], BF16)
    sT_r = [b.ps(f"sT_ps{i}", [128, 1024], F32) for i in range(2)]
    o_ps = b.ps("o_ps", [65, 1024], F32)
    lrec = b.sb("lrec", [1, 1024], F32)
    lrecb = b.sb("lrecb", [64, 1024], F32)
    attnT = b.sb("attnT", [128, 8 * OWN], BF16)

    for h in range(NH):
        ch, hw = h // 2, (h % 2) * 64
        b.dma("sp", lambda e, h=h: e.dma_start(
            out=kt_h.h[:], in_=d_KT.h[h * 64:(h + 1) * 64, :]),
            r=[(d_KT, ALL)], w=[(kt_h, ALL)])
        b.dma("sp", lambda e, h=h: e.dma_start(
            out=vt_h.h[:], in_=d_VT.h[h * 64:(h + 1) * 64, :]),
            r=[(d_VT, ALL)], w=[(vt_h, ALL)])
        b.dma("sp", lambda e, h=h: e.dma_start(
            out=qt_h.h[:], in_=d_QT.h[h * 64:(h + 1) * 64, :]),
            r=[(d_QT, ALL)], w=[(qt_h, ALL)])
        b.op("act", lambda e: e.activation(out=qsq.h[0:64, 0:OWN],
                                           in_=qt_h.h[:], func=AF.Square),
             r=[(qt_h, ALL)], w=[(qsq, ALL)])
        for j in range(4):
            b.op("pe", lambda e, j=j: e.matmul(
                misc_ps.h[0:1, 0:512], c_one.h[0:64, 0:1],
                qsq.h[0:64, j * 512:(j + 1) * 512],
                start=(j == 0), stop=(j == 3)),
                r=[(c_one, ALL), (qsq, ALL)], w=[(misc_ps, ALL)])
        b.op("dve", lambda e: e.reduce_max(out=r2row.h[0:1, 0:1],
                                           in_=misc_ps.h[0:1, 0:512], axis=AX),
             r=[(misc_ps, ALL)], w=[(r2row, 0)])
        b.op("act", lambda e: e.activation(out=qsq.h[:], in_=kt_h.h[:],
                                           func=AF.Square),
             r=[(kt_h, ALL), (qsq, ALL)], w=[(qsq, ALL)])
        for j in range(8):
            b.op("pe", lambda e, j=j: e.matmul(
                misc_ps.h[0:1, 0:512], c_one.h[0:64, 0:1],
                qsq.h[0:64, j * 512:(j + 1) * 512],
                start=(j == 0), stop=(j == 7)),
                r=[(c_one, ALL), (qsq, ALL)], w=[(misc_ps, ALL)])
        b.op("dve", lambda e: e.reduce_max(out=r2row.h[0:1, 1:2],
                                           in_=misc_ps.h[0:1, 0:512], axis=AX),
             r=[(misc_ps, ALL)], w=[(r2row, 1)])
        b.op("dve", lambda e: e.tensor_tensor(
            out=m_sc.h[0:1, 0:1], in0=r2row.h[0:1, 0:1],
            in1=r2row.h[0:1, 1:2], op=ALU.mult),
            r=[(r2row, 0), (r2row, 1)], w=[(m_sc, ALL)])
        b.op("act", lambda e: e.activation(out=m_sc.h[0:1, 1:2],
                                           in_=m_sc.h[0:1, 0:1],
                                           func=AF.Sqrt),
             r=[(m_sc, ALL)], w=[(m_sc, ALL)])
        b.op("dve", lambda e: e.tensor_scalar(
            out=m_sc.h[0:1, 2:3], in0=m_sc.h[0:1, 1:2], scalar1=-0.125,
            scalar2=None, op0=ALU.mult), r=[(m_sc, ALL)], w=[(m_sc, ALL)])
        b.op("pe", lambda e: e.matmul(
            misc_ps.h[:, 0:1], c_oner.h[0:1, :], m_sc.h[0:1, 2:3],
            start=True, stop=True),
            r=[(c_oner, ALL), (m_sc, ALL)], w=[(misc_ps, ALL)])
        b.op("dve", lambda e: e.tensor_copy(out=negm.h[:],
                                            in_=misc_ps.h[:, 0:1]),
             r=[(misc_ps, ALL)], w=[(negm, ALL)])
        b.op("dve", lambda e: e.memset(vtok.h[:], 1.0), w=[(vtok, ALL)])
        for kt in range(32):
            b.op("pe", lambda e, kt=kt: e.transpose(
                ps_vtr.h[:, 0:64], vt_h.h[:, kt * 128:(kt + 1) * 128],
                ident_b.h[0:64, 0:64]),
                r=[(vt_h, ALL), (ident_b, ALL)], w=[(ps_vtr, ALL)])
            b.op("dve", lambda e, kt=kt: e.tensor_copy(
                out=vtok.h[:, kt * 65:kt * 65 + 64], in_=ps_vtr.h[:, 0:64]),
                r=[(ps_vtr, ALL)], w=[(vtok, kt)])
        for qh in range(2):
            q0 = qh * 1024

            def att_S(kt, q0=q0):
                sps = sT_r[kt % 2]
                for j in range(2):
                    b.op("pe", lambda e, sps=sps, kt=kt, j=j, q0=q0: e.matmul(
                        sps.h[:, j * 512:(j + 1) * 512],
                        kt_h.h[:, kt * 128:(kt + 1) * 128],
                        qt_h.h[:, q0 + j * 512:q0 + (j + 1) * 512],
                        start=True, stop=True),
                        r=[(kt_h, ALL), (qt_h, ALL)], w=[(sps, ALL)])

            att_S(0)
            for kt in range(32):
                sps, ptb = sT_r[kt % 2], pt_r[kt % 4]
                if kt < 31:
                    att_S(kt + 1)
                b.op("act", lambda e, sps=sps, ptb=ptb: e.activation(
                    out=ptb.h[:], in_=sps.h[:], func=AF.Exp,
                    scale=c_eighth.h[:], bias=negm.h[:]),
                    r=[(sps, ALL), (c_eighth, ALL), (negm, ALL)],
                    w=[(ptb, ALL)])
                for j in range(2):
                    b.op("pe", lambda e, ptb=ptb, kt=kt, j=j: e.matmul(
                        o_ps.h[:, j * 512:(j + 1) * 512],
                        vtok.h[:, kt * 65:(kt + 1) * 65],
                        ptb.h[:, j * 512:(j + 1) * 512],
                        start=(kt == 0), stop=(kt == 31)),
                        r=[(vtok, kt), (ptb, ALL)], w=[(o_ps, j)])
            b.op("dve", lambda e: e.reciprocal(out=lrec.h[:],
                                               in_=o_ps.h[64:65, :]),
                 r=[(o_ps, 0), (o_ps, 1)], w=[(lrec, ALL)])
            for j in range(2):
                b.op("pe", lambda e, j=j: e.matmul(
                    misc_ps.h[0:64, 0:512], c_oner.h[0:1, 0:64],
                    lrec.h[0:1, j * 512:(j + 1) * 512],
                    start=True, stop=True),
                    r=[(c_oner, ALL), (lrec, ALL)], w=[(misc_ps, ALL)])
                b.op("dve", lambda e, j=j: e.tensor_copy(
                    out=lrecb.h[:, j * 512:(j + 1) * 512],
                    in_=misc_ps.h[0:64, 0:512]),
                    r=[(misc_ps, ALL)], w=[(lrecb, j)])
            b.op("dve", lambda e, ch=ch, hw=hw, q0=q0: e.tensor_tensor(
                out=attnT.h[hw:hw + 64, ch * OWN + q0:ch * OWN + q0 + 1024],
                in0=o_ps.h[0:64, :], in1=lrecb.h[:], op=ALU.mult),
                r=[(o_ps, 0), (o_ps, 1), (lrecb, 0), (lrecb, 1)],
                w=[(attnT, (ch, hw, q0))])

    wo_sb = b.sb("wo_sb", [128, 8 * D], BF16)
    for dc in range(8):
        b.dma("pool", lambda e, dc=dc: e.dma_start(
            out=wo_sb.h[:, dc * D:(dc + 1) * D],
            in_=w_wo.h[dc * 128:(dc + 1) * 128, :]), w=[(wo_sb, dc)])
    bo_row = b.sb("bo_row", [1, D], F32)
    b.dma("sp", lambda e: e.dma_start(out=bo_row.h[:], in_=w_bo.h.ap()),
          w=[(bo_row, ALL)])
    r1_t_r = [b.sb(f"r1_t{i}", [128, D], F32) for i in range(2)]
    x1_t_r = [b.sb(f"x1_t{i}", [128, D], F32) for i in range(2)]
    keys_t_r = [b.sb(f"keys_t{i}", [128, D], F32) for i in range(2)]
    kcol_r = [b.sb(f"kcol{i}", [128, 1], F32) for i in range(2)]
    kint_r = [b.sb(f"kint{i}", [128, 1], I32) for i in range(2)]
    kback_r = [b.sb(f"kback{i}", [128, 1], F32) for i in range(2)]
    kd_r = [b.sb(f"kd{i}", [128, 1], F32) for i in range(2)]

    def p3_block(tt):
        r1_t, x1_t, keys_t = r1_t_r[tt % 2], x1_t_r[tt % 2], keys_t_r[tt % 2]
        kcol, kint, kback, kd = (kcol_r[tt % 2], kint_r[tt % 2],
                                 kback_r[tt % 2], kd_r[tt % 2])
        xn_r = xn_r_r[tt % 2]
        wps_t = sT_r[tt % 2]
        for j in range(2):
            for dc in range(8):
                b.op("pe", lambda e, tt=tt, j=j, dc=dc, wps=wps_t: e.matmul(
                    wps.h[:, j * 512:(j + 1) * 512],
                    attnT.h[:, dc * OWN + tt * 128:dc * OWN + (tt + 1) * 128],
                    wo_sb.h[:, dc * D + j * 512:dc * D + (j + 1) * 512],
                    start=(dc == 0), stop=False),
                    r=[(attnT, ALL), (wo_sb, dc)], w=[(wps_t, j)])
            b.op("pe", lambda e, j=j, wps=wps_t: e.matmul(
                wps.h[:, j * 512:(j + 1) * 512], c_oner.h[0:1, :],
                bo_row.h[0:1, j * 512:(j + 1) * 512],
                start=False, stop=True),
                r=[(c_oner, ALL), (bo_row, ALL)], w=[(wps_t, j)])
        b.dma("sp", lambda e, tt=tt, xn_r=xn_r: e.dma_start(
            out=xn_r.h[:], in_=d_xn32.h[tt * 128:(tt + 1) * 128, :]),
            r=[(d_xn32, tt)], w=[(xn_r, ALL)])
        b.op("dve", lambda e, wps=wps_t, r1_t=r1_t, xn_r=xn_r:
             e.tensor_tensor(
                 out=r1_t.h[:], in0=wps.h[:], in1=xn_r.h[:], op=ALU.add),
             r=[(wps_t, 0), (wps_t, 1), (xn_r, ALL)], w=[(r1_t, ALL)])
        ln_tile(r1_t, x1_t)
        b.op("dve", lambda e: e.tensor_tensor(
            out=x1_t.h[:], in0=x1_t.h[:], in1=g1_g.h[:], op=ALU.mult),
            r=[(x1_t, ALL), (g1_g, ALL)], w=[(x1_t, ALL)])
        b.op("pool", lambda e: e.tensor_tensor(
            out=x1_t.h[:], in0=x1_t.h[:], in1=g1_b.h[:], op=ALU.add),
            r=[(x1_t, ALL), (g1_b, ALL)], w=[(x1_t, ALL)])
        b.dma("pool", lambda e, tt=tt: e.dma_start(
            out=d_x132.h[tt * 128:(tt + 1) * 128, :], in_=x1_t.h[:]),
            r=[(x1_t, ALL)], w=[(d_x132, tt)])
        b.dma("pool", lambda e, tt=tt: e.dma_start(
            out=d_x1bf.h[tt * 128:(tt + 1) * 128, :], in_=x1_t.h[:]),
            r=[(x1_t, ALL)], w=[(d_x1bf, tt)])
        b.op("dve", lambda e: e.tensor_tensor(
            out=keys_t.h[:], in0=x1_t.h[:], in1=rw_b.h[:], op=ALU.mult),
            r=[(x1_t, ALL), (rw_b, ALL)], w=[(keys_t, ALL)])
        b.op("dve", lambda e: e.reduce_sum(out=kcol.h[:], in_=keys_t.h[:],
                                           axis=AX),
             r=[(keys_t, ALL)], w=[(kcol, ALL)])
        b.op("dve", lambda e: e.tensor_scalar(
            out=kcol.h[:], in0=kcol.h[:], scalar1=rb_c.h[:], scalar2=None,
            op0=ALU.add), r=[(kcol, ALL), (rb_c, ALL)], w=[(kcol, ALL)])
        b.op("dve", lambda e: e.tensor_copy(out=kint.h[:], in_=kcol.h[:]),
             r=[(kcol, ALL)], w=[(kint, ALL)])
        b.op("dve", lambda e: e.tensor_copy(out=kback.h[:], in_=kint.h[:]),
             r=[(kint, ALL)], w=[(kback, ALL)])
        b.op("dve", lambda e: e.tensor_tensor(
            out=kd.h[:], in0=kcol.h[:], in1=kback.h[:], op=ALU.is_lt),
            r=[(kcol, ALL), (kback, ALL)], w=[(kd, ALL)])
        b.op("dve", lambda e: e.tensor_sub(out=kcol.h[:], in0=kback.h[:],
                                           in1=kd.h[:]),
             r=[(kback, ALL), (kd, ALL)], w=[(kcol, ALL)])
        b.op("dve", lambda e: e.tensor_scalar(
            out=kd.h[:], in0=kcol.h[:], scalar1=0.25, scalar2=None,
            op0=ALU.mult), r=[(kcol, ALL)], w=[(kd, ALL)])
        b.op("dve", lambda e: e.tensor_copy(out=kint.h[:], in_=kd.h[:]),
             r=[(kd, ALL)], w=[(kint, ALL)])
        b.op("dve", lambda e: e.tensor_copy(out=kback.h[:], in_=kint.h[:]),
             r=[(kint, ALL)], w=[(kback, ALL)])
        b.op("dve", lambda e: e.tensor_tensor(
            out=kd.h[:], in0=kd.h[:], in1=kback.h[:], op=ALU.is_lt),
            r=[(kd, ALL), (kback, ALL)], w=[(kd, ALL)])
        b.op("dve", lambda e: e.tensor_sub(out=kback.h[:], in0=kback.h[:],
                                           in1=kd.h[:]),
             r=[(kback, ALL), (kd, ALL)], w=[(kback, ALL)])
        b.op("dve", lambda e, tt=tt: e.scalar_tensor_tensor(
            out=idx_cols.h[:, tt:tt + 1], in0=kback.h[:], scalar=-4.0,
            in1=kcol.h[:], op0=ALU.mult, op1=ALU.add),
            r=[(kback, ALL), (kcol, ALL)], w=[(idx_cols, tt)])

    for tt in range(16):
        p3_block(tt)

    if stage < 4:
        _tail(b, nc, out, [d_x1bf, d_x132])
        return

    b.barrier()
    g2_g = b.sb("g2_g", [128, D], F32)
    g2_b = b.sb("g2_b", [128, D], F32)
    bcast_row(w_ln2g, g2_g)
    bcast_row(w_ln2b, g2_b)
    w1_sb = [b.sb(f"w1_sb{i}", [128, 8 * DI], BF16) for i in range(2)]
    w2_sb = [b.sb(f"w2_sb{i}", [128, 8 * D], BF16) for i in range(2)]
    b1cols = b.sb("b1cols", [128, 32], F32)
    b1row = b.sb("b1row", [32, 128], F32)
    b2row = b.sb("b2row", [1, D], F32)
    ps_b1 = b.ps("ps_b1", [128, 512], F32)
    mask_c = b.sb("mask_c", [128, 16], F32)
    x1fm = [b.sb(f"x1fm{i}", [128, 8 * 512], BF16) for i in range(2)]
    h_ps = b.ps("h_ps", [128, 512], F32)
    h_ps2 = b.ps("h_ps2", [128, 512], F32)
    h_sb = b.sb("h_sb", [128, 16 * 512], BF16)
    y_ps = b.ps("y_ps", [128, 1024], F32)
    y_ps2 = b.ps("y_ps2", [128, 1024], F32)
    acc = b.sb("acc", [128, 16 * 1024], BF16)
    g_t = b.sb("g_t", [128, D], F32)
    x2_t = xn_r_r[0]

    def p4_ln2(ti):
        b.op("dve", lambda e, ti=ti: e.tensor_copy(
            out=x2_t.h[:], in_=acc.h[:, ti * 1024:(ti + 1) * 1024]),
            r=[(acc, ti)], w=[(x2_t, ALL)])
        ln_tile(x2_t, g_t)
        b.op("dve", lambda e: e.tensor_tensor(
            out=g_t.h[:], in0=g_t.h[:], in1=g2_g.h[:], op=ALU.mult),
            r=[(g_t, ALL), (g2_g, ALL)], w=[(g_t, ALL)])
        b.op("pool", lambda e: e.tensor_tensor(
            out=g_t.h[:], in0=g_t.h[:], in1=g2_b.h[:], op=ALU.add),
            r=[(g_t, ALL), (g2_b, ALL)], w=[(g_t, ALL)])
        b.op("act", lambda e: e.activation(out=g_t.h[:], in_=g_t.h[:],
                                           func=AF.Silu),
             r=[(g_t, ALL)], w=[(g_t, ALL)])
        b.dma("pool", lambda e, ti=ti: e.dma_start(
            out=d_g32.h[ti * 128:(ti + 1) * 128, :], in_=g_t.h[:]),
            r=[(g_t, ALL)], w=[(d_g32, ti)])

    def p4_evac(e_i, half, yp, ti):
        if e_i == 0 and half == 0:
            xr = xn_r_r[ti % 2]
            b.dma("sp", lambda e, ti=ti, xr=xr: e.dma_start(
                out=xr.h[:], in_=d_x132.h[ti * 128:(ti + 1) * 128, :]),
                r=[(d_x132, ti)], w=[(xr, ALL)])
            b.op("dve", lambda e, yp=yp, ti=ti, xr=xr: e.scalar_tensor_tensor(
                out=acc.h[:, ti * 1024:(ti + 1) * 1024],
                in0=yp.h[:], scalar=mask_c.h[:, ti:ti + 1],
                in1=xr.h[:], op0=ALU.mult, op1=ALU.add),
                r=[(yp, 0), (yp, 1), (mask_c, ALL), (xr, ALL)],
                w=[(acc, ti)])
        else:
            b.op("dve", lambda e, yp=yp, ti=ti: e.scalar_tensor_tensor(
                out=acc.h[:, ti * 1024:(ti + 1) * 1024],
                in0=yp.h[:], scalar=mask_c.h[:, ti:ti + 1],
                in1=acc.h[:, ti * 1024:(ti + 1) * 1024],
                op0=ALU.mult, op1=ALU.add),
                r=[(yp, 0), (yp, 1), (mask_c, ALL), (acc, ti)],
                w=[(acc, ti)])
        if e_i == NE - 1 and half == 1:
            p4_ln2(ti)

    def p4_w1_load(e_i, half):
        wb = w1_sb[(e_i * 2 + half) % 2]
        for dc in range(8):
            b.dma("pool", lambda e, e_i=e_i, half=half, dc=dc, wb=wb:
                  e.dma_start(
                      out=wb.h[:, dc * DI:(dc + 1) * DI],
                      in_=w_w1.h[e_i, dc * 128:(dc + 1) * 128,
                                 half * 2048:(half + 1) * 2048]),
                  w=[(wb, dc)])
        return wb

    def p4_w2_load(e_i, half, quarter):
        wb = w2_sb[quarter % 2]
        for fc8 in range(8):
            fcg = half * 2048 + (quarter * 8 + fc8) * 128
            b.dma("pool", lambda e, e_i=e_i, fcg=fcg, fc8=fc8, wb=wb:
                  e.dma_start(out=wb.h[:, fc8 * D:(fc8 + 1) * D],
                              in_=w_w2.h[e_i, fcg:fcg + 128, :]),
                  w=[(wb, fc8)])
        return wb

    w1_cur = p4_w1_load(0, 0)
    for e_i in range(NE):
        b.dma("sp", lambda e, e_i=e_i: e.dma_start(
            out=b1row.h[:], in_=w_b1.h[e_i:e_i + 1, :].rearrange(
                "a (c p) -> (a c) p", p=128)),
            w=[(b1row, ALL)])
        b.op("pe", lambda e: e.transpose(ps_b1.h[:, 0:32], b1row.h[:],
                                         ident_f.h[0:32, 0:32]),
             r=[(b1row, ALL), (ident_f, ALL)], w=[(ps_b1, ALL)])
        b.op("dve", lambda e: e.tensor_copy(out=b1cols.h[:],
                                            in_=ps_b1.h[:, 0:32]),
             r=[(ps_b1, ALL)], w=[(b1cols, ALL)])
        b.dma("sp", lambda e, e_i=e_i: e.dma_start(
            out=b2row.h[:], in_=w_b2.h[e_i:e_i + 1, :]), w=[(b2row, ALL)])
        b.op("dve", lambda e, e_i=e_i: e.tensor_scalar(
            out=mask_c.h[:], in0=idx_cols.h[:], scalar1=float(e_i),
            scalar2=None, op0=ALU.is_equal),
            r=[(idx_cols, ALL)] + [(idx_cols, t) for t in range(16)],
            w=[(mask_c, ALL)])
        for half in range(2):
            wb1 = w1_cur
            if not (e_i == NE - 1 and half == 1):
                ne_, nh_ = (e_i, 1) if half == 0 else (e_i + 1, 0)
                w1_cur = p4_w1_load(ne_, nh_)
            for tt4 in range(4):
                xf = x1fm[tt4 % 2]
                for dc in range(8):
                    b.dma("act", lambda e, tt4=tt4, dc=dc, xf=xf: e.dma_start(
                        out=xf.h[:, dc * 512:(dc + 1) * 512],
                        in_=d_x1bf.h[tt4 * 512:(tt4 + 1) * 512,
                                     dc * 128:(dc + 1) * 128], transpose=True),
                        r=[(d_x1bf, ALL)], w=[(xf, dc)])
                for fc in range(16):
                    ps = h_ps if fc % 2 == 0 else h_ps2
                    for dc in range(8):
                        b.op("pe", lambda e, ps=ps, fc=fc, dc=dc, xf=xf,
                             wb1=wb1: e.matmul(
                                 ps.h[:],
                                 wb1.h[:, dc * DI + fc * 128:
                                       dc * DI + (fc + 1) * 128],
                                 xf.h[:, dc * 512:(dc + 1) * 512],
                                 start=(dc == 0), stop=(dc == 7)),
                             r=[(wb1, dc), (xf, dc)], w=[(ps, ALL)])
                    b.op("act", lambda e, ps=ps, fc=fc, half=half:
                         e.activation(
                             out=h_sb.h[:, fc * 512:(fc + 1) * 512],
                             in_=ps.h[:], func=AF.Relu,
                             bias=b1cols.h[:, half * 16 + fc:
                                           half * 16 + fc + 1]),
                         r=[(ps, ALL), (b1cols, ALL)], w=[(h_sb, fc)])
                if tt4 == 0:
                    wq = [p4_w2_load(e_i, half, 0), p4_w2_load(e_i, half, 1)]
                else:
                    wq = [w2_sb[0], w2_sb[1]]
                for ts_ in range(4):
                    yp = y_ps if ts_ % 2 == 0 else y_ps2
                    ti = tt4 * 4 + ts_
                    for j in range(2):
                        for fc in range(16):
                            wb2 = wq[fc // 8]
                            fc8 = fc % 8
                            b.op("pe", lambda e, yp=yp, j=j, fc=fc, fc8=fc8,
                                 ts_=ts_, half=half, wb2=wb2: e.matmul(
                                     yp.h[:, j * 512:(j + 1) * 512],
                                     h_sb.h[:, fc * 512 + ts_ * 128:
                                            fc * 512 + (ts_ + 1) * 128],
                                     wb2.h[:, fc8 * D + j * 512:
                                           fc8 * D + (j + 1) * 512],
                                     start=(fc == 0),
                                     stop=(half == 0 and fc == 15)),
                                 r=[(h_sb, fc), (wb2, fc8)],
                                 w=[(yp, j)])
                        if half == 1:
                            b.op("pe", lambda e, yp=yp, j=j: e.matmul(
                                yp.h[:, j * 512:(j + 1) * 512],
                                c_oner.h[0:1, :],
                                b2row.h[0:1, j * 512:(j + 1) * 512],
                                start=False, stop=True),
                                r=[(c_oner, ALL), (b2row, ALL)],
                                w=[(yp, j)])
                    p4_evac(e_i, half, yp, ti)

    if stage < 5:
        _tail(b, nc, out, [d_g32])
        return

    b.barrier()
    cmat_t = b.sb("cmat_t", [128, NST], F32)
    ct_sb = b.sb("ct_sb", [NST, 16 * 128], BF16)
    ps_ct = b.ps("ps_ct", [128, 512], F32)
    for dc in range(16):
        b.dma("sp", lambda e, dc=dc: e.dma_start(
            out=cmat_t.h[:], in_=w_cmat.h[dc * 128:(dc + 1) * 128, :]),
            w=[(cmat_t, ALL)])
        b.op("pe", lambda e, dc=dc: e.transpose(
            ps_ct.h[0:NST, 0:128], cmat_t.h[:], ident_f.h[:]),
            r=[(cmat_t, ALL), (ident_f, ALL)], w=[(ps_ct, ALL)])
        b.op("dve", lambda e, dc=dc: e.tensor_copy(
            out=ct_sb.h[:, dc * 128:(dc + 1) * 128],
            in_=ps_ct.h[0:NST, 0:128]),
            r=[(ps_ct, ALL)], w=[(ct_sb, dc)])
    wout_sb = b.sb("wout_sb", [128, 16 * D], BF16)
    for dc in range(16):
        b.dma("pool", lambda e, dc=dc: e.dma_start(
            out=wout_sb.h[:, dc * D:(dc + 1) * D],
            in_=w_wout.h[dc * 128:(dc + 1) * 128, :]), w=[(wout_sb, dc)])

    y2_ps = b.ps("y2_ps", [128, 512], F32)
    y2_ps2 = b.ps("y2_ps2", [128, 512], F32)
    xsc_r_r = [b.sb(f"xsc_r{i}", [128, 512], F32) for i in range(2)]
    gs_r_r = [b.sb(f"gs_r{i}", [128, 512], F32) for i in range(2)]
    yg_sb = b.sb("yg_sb", [128, 16 * 512], BF16)
    so_ps = b.ps("so_ps", [128, 1024], F32)
    so_ps2 = b.ps("so_ps2", [128, 1024], F32)
    s_t_r = [b.sb(f"s_t{i}", [128, D], F32) for i in range(2)]
    gf_t_r = [b.sb(f"gf_t{i}", [128, D], F32) for i in range(2)]
    xo_t_r = [b.sb(f"xo_t{i}", [128, D], F32) for i in range(2)]
    o_t_r = [b.sb(f"o_t{i}", [128, D], F32) for i in range(2)]

    def p5_y(tt4, dc):
        yps = y2_ps if dc % 2 == 0 else y2_ps2
        xsc_r, gs_r = xsc_r_r[dc % 2], gs_r_r[dc % 2]
        b.op("pe", lambda e, yps=yps, tt4=tt4, dc=dc: e.matmul(
            yps.h[:], ct_sb.h[:, dc * 128:(dc + 1) * 128],
            hTb.h[:, tt4 * 512:(tt4 + 1) * 512], start=True, stop=True),
            r=[(ct_sb, dc), (hTb, ALL)], w=[(yps, ALL)])
        b.dma("sp", lambda e, tt4=tt4, dc=dc, xsc_r=xsc_r: e.dma_start(
            out=xsc_r.h[:], in_=d_xsc32.h[dc * 128:(dc + 1) * 128,
                                          tt4 * 512:(tt4 + 1) * 512]),
            r=[(d_xsc32, (dc, tt4 + 4))], w=[(xsc_r, ALL)])
        b.dma("sp", lambda e, tt4=tt4, dc=dc, gs_r=gs_r: e.dma_start(
            out=gs_r.h[:], in_=d_gs32.h[dc * 128:(dc + 1) * 128,
                                        tt4 * 512:(tt4 + 1) * 512]),
            r=[(d_gs32, (dc, tt4 + 4))], w=[(gs_r, ALL)])
        b.op("dve", lambda e, yps=yps, dc=dc, xsc_r=xsc_r:
             e.scalar_tensor_tensor(
                 out=xsc_r.h[:], in0=xsc_r.h[:], scalar=dv_c.h[:, dc:dc + 1],
                 in1=yps.h[:], op0=ALU.mult, op1=ALU.add),
             r=[(xsc_r, ALL), (dv_c, ALL), (yps, ALL)], w=[(xsc_r, ALL)])
        b.op("dve", lambda e, dc=dc, xsc_r=xsc_r, gs_r=gs_r: e.tensor_tensor(
            out=yg_sb.h[:, dc * 512:(dc + 1) * 512], in0=xsc_r.h[:],
            in1=gs_r.h[:], op=ALU.mult),
            r=[(xsc_r, ALL), (gs_r, ALL)], w=[(yg_sb, dc)])

    def p5_out(tt4, ts_):
        sp_ = so_ps if ts_ % 2 == 0 else so_ps2
        ti = tt4 * 4 + ts_
        s_t, gf_t = s_t_r[ts_ % 2], gf_t_r[ts_ % 2]
        xo_t, o_t = xo_t_r[ts_ % 2], o_t_r[ts_ % 2]
        xr = xn_r_r[ts_ % 2]
        for j in range(2):
            for dc in range(16):
                b.op("pe", lambda e, sp_=sp_, ts_=ts_, j=j, dc=dc:
                     e.matmul(
                         sp_.h[:, j * 512:(j + 1) * 512],
                         yg_sb.h[:, dc * 512 + ts_ * 128:
                                 dc * 512 + (ts_ + 1) * 128],
                         wout_sb.h[:, dc * D + j * 512:
                                   dc * D + (j + 1) * 512],
                         start=(dc == 0), stop=(dc == 15)),
                     r=[(yg_sb, dc), (wout_sb, dc)], w=[(sp_, j)])
        b.dma("sp", lambda e, ti=ti, xr=xr: e.dma_start(
            out=xr.h[:], in_=d_xn32.h[ti * 128:(ti + 1) * 128, :]),
            r=[(d_xn32, ti)], w=[(xr, ALL)])
        b.op("dve", lambda e, sp_=sp_, s_t=s_t, xr=xr: e.tensor_tensor(
            out=s_t.h[:], in0=sp_.h[:], in1=xr.h[:], op=ALU.add),
            r=[(sp_, 0), (sp_, 1), (xr, ALL)], w=[(s_t, ALL)])
        b.dma("sp", lambda e, ti=ti, gf_t=gf_t: e.dma_start(
            out=gf_t.h[:], in_=d_g32.h[ti * 128:(ti + 1) * 128, :]),
            r=[(d_g32, ti)], w=[(gf_t, ALL)])
        b.dma("sp", lambda e, ti=ti, xo_t=xo_t: e.dma_start(
            out=xo_t.h[:],
            in_=xs_ext.h[PRE + ti * 128:PRE + (ti + 1) * 128, :]),
            w=[(xo_t, ALL)])
        b.op("dve", lambda e, s_t=s_t, gf_t=gf_t: e.tensor_tensor(
            out=s_t.h[:], in0=s_t.h[:], in1=gf_t.h[:], op=ALU.mult),
            r=[(s_t, ALL), (gf_t, ALL)], w=[(s_t, ALL)])
        b.op("dve", lambda e, s_t=s_t, xo_t=xo_t, o_t=o_t: e.tensor_tensor(
            out=o_t.h[:], in0=s_t.h[:], in1=xo_t.h[:], op=ALU.add),
            r=[(s_t, ALL), (xo_t, ALL)], w=[(o_t, ALL)])
        b.dma("pool", lambda e, ti=ti, o_t=o_t: e.dma_start(
            out=out.h[ti * 128:(ti + 1) * 128, :], in_=o_t.h[:]),
            r=[(o_t, ALL)], w=[(out, ti)])

    for tt4 in range(4):
        for dc in range(16):
            p5_y(tt4, dc)
        for ts_ in range(4):
            p5_out(tt4, ts_)

    _tail(b, nc, out, [])


def _tail(b, nc, out, extra):
    b.finalize()
    if out.sem is not None:
        nc.sync.wait_ge(out.sem, 16 * out.sem_cnt)
    for buf in extra:
        if buf.sem is not None:
            nc.sync.wait_ge(buf.sem, 16 * buf.sem_cnt)
    nc.all_engine_barrier()
    for s in b.all_sems:
        nc.gpsimd.sem_clear(s)
    nc.all_engine_barrier()
    b.phase.close()



_NC_CACHE = {}


def _get_nc(stage=STAGE):
    if stage not in _NC_CACHE:
        _NC_CACHE[stage] = build_nc(stage)
    return _NC_CACHE[stage]


def make_in_maps(inp):
    import ml_dtypes
    x = np.ascontiguousarray(np.asarray(inp["L0_attn_x"], dtype=np.float32))
    f32 = lambda v: np.ascontiguousarray(np.asarray(v, dtype=np.float32))
    common = dict(
        identf=np.eye(128, dtype=np.float32),
        identb=np.eye(128, dtype=np.float32).astype(ml_dtypes.bfloat16),
        w_lng=f32(inp["L0_attn_ln_gamma"]), w_lnb=f32(inp["L0_attn_ln_beta"]),
        w_in=f32(inp["L0_attn_W_in"]),
        w_convw=f32(np.asarray(inp["L0_attn_conv_w"]).reshape(DI, KCONV).T),
        w_convb=f32(inp["L0_attn_conv_b"]),
        w_alog=f32(np.asarray(inp["L0_attn_A_log"]).reshape(1, NST)),
        w_bmat=f32(inp["L0_attn_B_mat"]), w_cmat=f32(inp["L0_attn_C_mat"]),
        w_dvec=f32(inp["L0_attn_D_vec"]), w_wd=f32(inp["L0_attn_Wd"]),
        w_bd=f32(inp["L0_attn_bd"]), w_wout=f32(inp["L0_attn_W_out"]),
        w_wq=f32(inp["L0_ffn_Wq"]), w_bq=f32(inp["L0_ffn_bq"]),
        w_wk=f32(inp["L0_ffn_Wk"]), w_bk=f32(inp["L0_ffn_bk"]),
        w_wv=f32(inp["L0_ffn_Wv"]), w_bv=f32(inp["L0_ffn_bv"]),
        w_wo=f32(inp["L0_ffn_Wo"]),
        w_bo=f32(np.asarray(inp["L0_ffn_bo"]).reshape(1, D)),
        w_rw=f32(np.asarray(inp["L0_ffn_router_w"]).reshape(D)),
        w_rb=f32(np.asarray(inp["L0_ffn_router_b"]).reshape(1, 1)),
        w_w1=f32(inp["L0_ffn_W1"]), w_b1=f32(inp["L0_ffn_b1"]),
        w_w2=f32(inp["L0_ffn_W2"]), w_b2=f32(inp["L0_ffn_b2"]),
        w_ln1g=f32(inp["L0_ffn_ln1_g"]), w_ln1b=f32(inp["L0_ffn_ln1_b"]),
        w_ln2g=f32(inp["L0_ffn_ln2_g"]), w_ln2b=f32(inp["L0_ffn_ln2_b"]),
    )
    in_maps = []
    for c in range(8):
        bi, p = c // 2, c % 2
        own = x[bi, p * OWN:(p + 1) * OWN]
        oth = x[bi, (1 - p) * OWN:(2 - p) * OWN]
        pre = x[bi, 0:PRE] if p == 1 else np.zeros((PRE, D), np.float32)
        m = np.float32(p)
        in_maps.append(dict(
            common,
            xs_ext=np.ascontiguousarray(np.concatenate([pre, own], 0)),
            xkv=np.ascontiguousarray(oth),
            mpref16=np.full((16, 1), m, np.float32),
            mhalo=np.full((128, 1), m, np.float32)))
    return in_maps


def kernel(**inp):
    nc = _get_nc(STAGE)
    in_maps = make_in_maps(inp)
    res = run_bass_kernel_spmd(nc, in_maps, core_ids=list(range(8)))
    outp = np.empty((B, T, D), np.float32)
    for c in range(8):
        bi, p = c // 2, c % 2
        outp[bi, p * OWN:(p + 1) * OWN] = res.results[c]["out"]
    return outp


# version 09
# speedup vs baseline: 1.0502x; 1.0502x over previous
import numpy as np
from contextlib import ExitStack

import concourse.bass as bass
import concourse.mybir as mybir
from concourse.bass_utils import run_bass_kernel_spmd

F32 = mybir.dt.float32
BF16 = mybir.dt.bfloat16
I32 = mybir.dt.int32
AF = mybir.ActivationFunctionType
ALU = mybir.AluOpType
AX = mybir.AxisListType.X

B, T, D = 4, 4096, 1024
DI, NST, KCONV = 2048, 16, 4
DFF, NE, NH, HD = 4096, 4, 16, 64
EXT, PRE, OWN, NKV = 4096, 2048, 2048, 4096
EPS = 1e-5
ALL = None

DEBUG = False
STAGE = 5


class Buf:
    __slots__ = ("name", "h", "space", "sem", "sem_cnt", "writers", "readers")

    def __init__(self, name, h, space):
        self.name, self.h, self.space = name, h, space
        self.sem = None
        self.sem_cnt = 0
        self.writers = []
        self.readers = []


def _conf(k1, k2):
    return k1 is ALL or k2 is ALL or k1 == k2


class Op:
    __slots__ = ("idx", "eng", "fn", "reads", "writes", "is_dma", "sem",
                 "sem_target", "inc", "inc_seq", "waits")

    def __init__(self, idx, eng, fn, reads, writes, is_dma):
        self.idx, self.eng, self.fn = idx, eng, fn
        self.reads, self.writes, self.is_dma = list(reads), list(writes), is_dma
        self.sem = None
        self.sem_target = 0
        self.inc = False
        self.inc_seq = 0
        self.waits = []


class Bld:

    ENGS = ("pe", "dve", "act", "pool", "sp")

    def __init__(self, nc, stack):
        self.nc = nc
        self.stack = stack
        self.phase = ExitStack()
        self.ops = []
        self.e = {"pe": nc.tensor, "dve": nc.vector, "act": nc.scalar,
                  "pool": nc.gpsimd, "sp": nc.sync}
        self.all_sems = []
        self.engsem = {g: self._sem(f"es_{g}") for g in self.ENGS}

    def _sem(self, name):
        s = self.stack.enter_context(self.nc.semaphore(name))
        self.all_sems.append(s)
        return s

    def sb(self, name, shape, dtype, pers=False):
        st = self.stack if pers else self.phase
        h = st.enter_context(self.nc.sbuf_tensor(name, shape, dtype))
        return Buf(name, h, "sb")

    def ps(self, name, shape, dtype=F32):
        h = self.phase.enter_context(self.nc.psum_tensor(name, shape, dtype))
        return Buf(name, h, "ps")

    def dr(self, name, shape, dtype, kind="Internal"):
        if DEBUG and kind == "Internal":
            kind = "ExternalOutput"
        return Buf(name, self.nc.dram_tensor(name, shape, dtype, kind=kind),
                   "dr")

    def op(self, eng, fn, r=(), w=()):
        o = Op(len(self.ops), eng, fn, r, w, False)
        self.ops.append(o)
        return o

    def wave(self, lists):
        n = max(len(li) for li in lists) if lists else 0
        for step in range(n):
            for li in lists:
                if step < len(li):
                    kind, eng, fn, r, w = li[step]
                    if kind == "dma":
                        self.dma(eng, fn, r=r, w=w)
                    else:
                        self.op(eng, fn, r=r, w=w)

    def dma(self, q, fn, r=(), w=()):
        w = [(bf, ALL) for (bf, _k) in w]
        o = Op(len(self.ops), q, fn, r, w, True)
        tgt = o.writes[0][0]
        if tgt.sem is None:
            tgt.sem = self._sem(f"dm{len(self.all_sems)}")
        tgt.sem_cnt += 1
        o.sem = tgt.sem
        o.sem_target = 16 * tgt.sem_cnt
        self.ops.append(o)
        return o

    def barrier(self):
        self.ops.append("BARRIER")
        self.phase.close()
        self.phase = ExitStack()
        self._bc_ps = None
        self._bc_row = None

    def finalize(self):
        pairs = []
        allbufs = {}
        bar_dma = []
        dma_issued = {}

        def note(bf):
            if id(bf) not in allbufs:
                allbufs[id(bf)] = bf
                bf.writers, bf.readers = [], []

        for item in self.ops:
            if item == "BARRIER":
                bar_dma.append(list(dma_issued.values()))
                for bf in allbufs.values():
                    bf.writers, bf.readers = [], []
                continue
            o = item
            for (bf, k) in o.reads:
                note(bf)
                for (pi, pk) in bf.writers:
                    if _conf(k, pk):
                        pairs.append((pi, o.idx))
                bf.readers.append((o.idx, k))
            for (bf, k) in o.writes:
                note(bf)
                for (pi, pk) in bf.writers:
                    if _conf(k, pk):
                        pairs.append((pi, o.idx))
                for (pi, pk) in bf.readers:
                    if _conf(k, pk) and pi != o.idx:
                        pairs.append((pi, o.idx))
                bf.writers = [(x, kk) for (x, kk) in bf.writers
                              if not _conf(k, kk)]
                bf.readers = [(x, kk) for (x, kk) in bf.readers
                              if not _conf(k, kk)]
                bf.writers.append((o.idx, k))
            if o.is_dma:
                dma_issued[o.sem.name] = (o.sem, o.sem_target)

        idx2op = {o.idx: o for o in self.ops if o != "BARRIER"}
        for (pi, ci) in pairs:
            p = idx2op[pi]
            if not p.is_dma and not (p.eng == "pe" and idx2op[ci].eng == "pe"):
                p.inc = True
        seg_last = {}
        for o in self.ops:
            if o == "BARRIER":
                for g, lo in seg_last.items():
                    lo.inc = True
                seg_last = {}
                continue
            if not o.is_dma:
                seg_last[o.eng] = o
        cnt = {g: 0 for g in self.ENGS}
        bar_snap = []
        for o in self.ops:
            if o == "BARRIER":
                bar_snap.append(dict(cnt))
                continue
            if o.is_dma:
                continue
            if o.inc:
                cnt[o.eng] += 1
                o.inc_seq = cnt[o.eng]
        for (pi, ci) in pairs:
            p, c = idx2op[pi], idx2op[ci]
            if p.is_dma:
                c.waits.append((p.sem, p.sem_target))
            elif not (p.eng == "pe" and c.eng == "pe"):
                c.waits.append((self.engsem[p.eng], p.inc_seq))

        floors = {g: {} for g in self.ENGS}
        bar_i = 0
        for item in self.ops:
            if item == "BARRIER":
                snap = bar_snap[bar_i]
                dsems = bar_dma[bar_i]
                bar_i += 1
                for g in self.ENGS:
                    fl = floors[g]
                    for g2 in self.ENGS:
                        nm = self.engsem[g2].name
                        if g2 != g and snap[g2] > 0 and \
                                fl.get(nm, -1) < snap[g2]:
                            self.e[g].wait_ge(self.engsem[g2], snap[g2])
                            fl[nm] = snap[g2]
                    for (sem, v) in dsems:
                        if fl.get(sem.name, -1) < v:
                            self.e[g].wait_ge(sem, v)
                            fl[sem.name] = v
                continue
            o = item
            fl = floors[o.eng]
            best = {}
            for (sem, val) in o.waits:
                if val > best.get(sem.name, (None, -1))[1]:
                    best[sem.name] = (sem, val)
            for kname, (sem, val) in best.items():
                if fl.get(kname, -1) < val:
                    self.e[o.eng].wait_ge(sem, val)
                    fl[kname] = val
            inst = o.fn(self.e[o.eng])
            if o.is_dma:
                inst.then_inc(o.sem, 16)
            elif o.inc:
                inst.then_inc(self.engsem[o.eng], 1)



def build_nc(stage=STAGE):
    nc = bass.Bass("TRN2", target_bir_lowering=False)
    stack = ExitStack()
    with stack:
        _body(nc, stack, stage)
    return nc


def _body(nc, stack, stage):
    b = Bld(nc, stack)

    def din(name, shape, dtype=F32):
        return Buf(name, nc.dram_tensor(name, shape, dtype,
                                        kind="ExternalInput"), "dr")

    xs_ext = din("xs_ext", [EXT, D])
    xkv = din("xkv", [OWN, D])
    mpref16 = din("mpref16", [16, 1])
    mhalo = din("mhalo", [128, 1])
    identf = din("identf", [128, 128])
    identb = din("identb", [128, 128], BF16)
    w_lng = din("w_lng", [D]);        w_lnb = din("w_lnb", [D])
    w_in = din("w_in", [D, 2 * DI])
    w_convw = din("w_convw", [KCONV, DI])
    w_convb = din("w_convb", [DI])
    w_alog = din("w_alog", [1, NST])
    w_bmat = din("w_bmat", [NST, DI])
    w_cmat = din("w_cmat", [DI, NST])
    w_dvec = din("w_dvec", [DI])
    w_wd = din("w_wd", [DI, DI]);     w_bd = din("w_bd", [DI])
    w_wout = din("w_wout", [DI, D])
    w_wq = din("w_wq", [D, D]);       w_bq = din("w_bq", [D])
    w_wk = din("w_wk", [D, D]);       w_bk = din("w_bk", [D])
    w_wv = din("w_wv", [D, D]);       w_bv = din("w_bv", [D])
    w_wo = din("w_wo", [D, D]);       w_bo = din("w_bo", [1, D])
    w_rw = din("w_rw", [D]);          w_rb = din("w_rb", [1, 1])
    w_w1 = din("w_w1", [NE, D, DFF]); w_b1 = din("w_b1", [NE, DFF])
    w_w2 = din("w_w2", [NE, DFF, D]); w_b2 = din("w_b2", [NE, D])
    w_ln1g = din("w_ln1g", [D]);      w_ln1b = din("w_ln1b", [D])
    w_ln2g = din("w_ln2g", [D]);      w_ln2b = din("w_ln2b", [D])
    out = Buf("out", nc.dram_tensor("out", [OWN, D], F32,
                                    kind="ExternalOutput"), "dr")

    d_xnbf = b.dr("d_xnbf", [NKV, D], BF16)
    d_xn32 = b.dr("d_xn32", [OWN, D], F32)
    d_xhbf = b.dr("d_xhbf", [EXT, D], BF16)
    d_KT = b.dr("d_KT", [D, NKV], BF16)
    d_VT = b.dr("d_VT", [D, NKV], BF16)
    d_QT = b.dr("d_QT", [D, OWN], BF16)
    d_xscbf = b.dr("d_xscbf", [DI, EXT], BF16)
    d_xsc32 = b.dr("d_xsc32", [DI, OWN], F32)
    d_gs32 = b.dr("d_gs32", [DI, OWN], F32)
    d_x1bf = b.dr("d_x1bf", [OWN, D], BF16)
    d_x132 = b.dr("d_x132", [OWN, D], F32)
    d_g32 = b.dr("d_g32", [OWN, D], F32)

    P = dict(pers=True)
    ident_f = b.sb("ident_f", [128, 128], F32, **P)
    ident_b = b.sb("ident_b", [128, 128], BF16, **P)
    c_eps = b.sb("c_eps", [128, 1], F32, **P)
    c_nhalf = b.sb("c_nhalf", [128, 1], F32, **P)
    c_one = b.sb("c_one", [128, 1], F32, **P)
    c_oner = b.sb("c_oner", [1, 128], F32, **P)
    c_oneb = b.sb("c_oneb", [128, 1], BF16, **P)
    c_eighth = b.sb("c_eighth", [128, 1], F32, **P)
    c_i2048 = b.sb("c_i2048", [128, 1], F32, **P)
    hTb = b.sb("hTb", [NST, OWN], BF16, **P)
    idx_cols = b.sb("idx_cols", [128, 16], F32, **P)
    s_mpref = b.sb("s_mpref", [16, 1], F32, **P)
    s_mhalo = b.sb("s_mhalo", [128, 1], F32, **P)
    a_col = b.sb("a_col", [16, 1], F32, **P)
    NLN = 8
    ln_sets = []
    for si in range(NLN):
        ln_sets.append({n: b.sb(f"ln_{n}{si}", [128, 1], F32, **P)
                        for n in ("sum", "ssq", "mu", "e2", "ms", "var", "rs")})
    bq_c = b.sb("bq_c", [128, 8], F32, **P)
    bk_c = b.sb("bk_c", [128, 8], F32, **P)
    bv_c = b.sb("bv_c", [128, 8], F32, **P)
    bd_c = b.sb("bd_c", [128, 16], F32, **P)
    cb_c = b.sb("cb_c", [128, 16], F32, **P)
    dv_c = b.sb("dv_c", [128, 16], F32, **P)
    cw_c = [b.sb(f"cw{k}_c", [128, 16], F32, **P) for k in range(KCONV)]
    rb_c = b.sb("rb_c", [128, 1], F32, **P)
    xn_r_r = [b.sb(f"xn_rr{i}", [128, D], F32, **P) for i in range(2)]

    _ln_i = [0]

    def ln_tile_ops(x_t, out_t, width=D):
        s = ln_sets[_ln_i[0] % NLN]
        _ln_i[0] += 1
        L = {n: (bb, ALL) for n, bb in s.items()}
        o = []
        o.append(("op", "dve", lambda e, s=s, x_t=x_t: e.reduce_sum(
            out=s["sum"].h[:], in_=x_t.h[:], axis=AX),
            [(x_t, ALL)], [L["sum"]]))
        o.append(("op", "act", lambda e, s=s, x_t=x_t, out_t=out_t,
                  width=width: e.activation(
                      out=out_t.h[:, 0:width], in_=x_t.h[:], func=AF.Square,
                      accum_out=s["ssq"].h[:]),
                  [(x_t, ALL)], [(out_t, ALL), L["ssq"]]))
        o.append(("op", "dve", lambda e, s=s, width=width: e.tensor_scalar(
            out=s["mu"].h[:], in0=s["sum"].h[:], scalar1=1.0 / width,
            scalar2=None, op0=ALU.mult), [L["sum"]], [L["mu"]]))
        o.append(("op", "dve", lambda e, s=s, width=width: e.tensor_scalar(
            out=s["e2"].h[:], in0=s["ssq"].h[:], scalar1=1.0 / width,
            scalar2=None, op0=ALU.mult), [L["ssq"]], [L["e2"]]))
        o.append(("op", "dve", lambda e, s=s: e.tensor_tensor(
            out=s["ms"].h[:], in0=s["mu"].h[:], in1=s["mu"].h[:],
            op=ALU.mult), [L["mu"]], [L["ms"]]))
        o.append(("op", "dve", lambda e, s=s: e.tensor_sub(
            out=s["var"].h[:], in0=s["e2"].h[:], in1=s["ms"].h[:]),
            [L["e2"], L["ms"]], [L["var"]]))
        o.append(("op", "act", lambda e, s=s: e.activation(
            out=s["rs"].h[:], in_=s["var"].h[:], func=AF.Ln,
            bias=c_eps.h[:]), [L["var"], (c_eps, ALL)], [L["rs"]]))
        o.append(("op", "act", lambda e, s=s: e.activation(
            out=s["rs"].h[:], in_=s["rs"].h[:], func=AF.Exp,
            scale=c_nhalf.h[:]), [L["rs"], (c_nhalf, ALL)], [L["rs"]]))
        o.append(("op", "dve", lambda e, s=s, x_t=x_t, out_t=out_t:
                  e.tensor_scalar(
                      out=out_t.h[:], in0=x_t.h[:], scalar1=s["mu"].h[:],
                      scalar2=s["rs"].h[:], op0=ALU.subtract, op1=ALU.mult),
                  [(x_t, ALL), L["mu"], L["rs"]], [(out_t, ALL)]))
        return o

    def ln_tile(x_t, out_t, width=D):
        for (kind, eng, fn, r, w) in ln_tile_ops(x_t, out_t, width):
            b.op(eng, fn, r=r, w=w)
        return

    def _ln_tile_body(x_t, out_t, width=D):
        s = ln_sets[_ln_i[0] % NLN]
        _ln_i[0] += 1
        L = {n: (bb, ALL) for n, bb in s.items()}
        b.op("dve", lambda e: e.reduce_sum(out=s["sum"].h[:], in_=x_t.h[:],
                                           axis=AX),
             r=[(x_t, ALL)], w=[L["sum"]])
        b.op("act", lambda e: e.activation(out=out_t.h[:, 0:width],
                                           in_=x_t.h[:], func=AF.Square,
                                           accum_out=s["ssq"].h[:]),
             r=[(x_t, ALL)], w=[(out_t, ALL), L["ssq"]])
        b.op("dve", lambda e: e.tensor_scalar(
            out=s["mu"].h[:], in0=s["sum"].h[:], scalar1=1.0 / width,
            scalar2=None, op0=ALU.mult),
            r=[L["sum"]], w=[L["mu"]])
        b.op("dve", lambda e: e.tensor_scalar(
            out=s["e2"].h[:], in0=s["ssq"].h[:], scalar1=1.0 / width,
            scalar2=None, op0=ALU.mult),
            r=[L["ssq"]], w=[L["e2"]])
        b.op("dve", lambda e: e.tensor_tensor(
            out=s["ms"].h[:], in0=s["mu"].h[:], in1=s["mu"].h[:],
            op=ALU.mult),
            r=[L["mu"]], w=[L["ms"]])
        b.op("dve", lambda e: e.tensor_sub(out=s["var"].h[:],
                                           in0=s["e2"].h[:],
                                           in1=s["ms"].h[:]),
             r=[L["e2"], L["ms"]], w=[L["var"]])
        b.op("act", lambda e: e.activation(out=s["rs"].h[:], in_=s["var"].h[:],
                                           func=AF.Ln, bias=c_eps.h[:]),
             r=[L["var"], (c_eps, ALL)], w=[L["rs"]])
        b.op("act", lambda e: e.activation(out=s["rs"].h[:], in_=s["rs"].h[:],
                                           func=AF.Exp, scale=c_nhalf.h[:]),
             r=[L["rs"], (c_nhalf, ALL)], w=[L["rs"]])
        b.op("dve", lambda e: e.tensor_scalar(
            out=out_t.h[:], in0=x_t.h[:], scalar1=s["mu"].h[:],
            scalar2=s["rs"].h[:], op0=ALU.subtract, op1=ALU.mult),
            r=[(x_t, ALL), L["mu"], L["rs"]], w=[(out_t, ALL)])

    def bcast_row(src, dst, cols=D):
        if getattr(b, "_bc_row", None) is None:
            b._bc_row = b.sb(f"bcr_{dst.name}", [1, D], F32)
        row = b._bc_row
        if getattr(b, "_bc_ps", None) is None:
            b._bc_ps = b.ps(f"bcp_{dst.name}", [128, 512], F32)
        psb = b._bc_ps
        b.dma("sp", lambda e: e.dma_start(
            out=row.h[:], in_=src.h.ap().rearrange("(a d) -> a d", a=1)),
            r=[], w=[(row, ALL)])
        for j0 in range(0, cols, 512):
            n = min(512, cols - j0)
            b.op("pe", lambda e, j0=j0, n=n: e.matmul(
                psb.h[:, 0:n], c_oner.h[0:1, :], row.h[0:1, j0:j0 + n],
                start=True, stop=True),
                r=[(c_oner, ALL), (row, ALL)], w=[(psb, ALL)])
            b.op("dve", lambda e, j0=j0, n=n: e.tensor_copy(
                out=dst.h[:, j0:j0 + n], in_=psb.h[:, 0:n]),
                r=[(psb, ALL)], w=[(dst, j0)])

    ps_set = b.ps("ps_set", [128, 512], F32)

    def load_cols(dst, ap2d, nch):
        row = b.sb(f"lc_{dst.name}", [nch, 128], F32)
        b.dma("sp", lambda e: e.dma_start(out=row.h[:], in_=ap2d),
              r=[], w=[(row, ALL)])
        b.op("pe", lambda e: e.transpose(ps_set.h[:, 0:nch], row.h[:],
                                         ident_f.h[0:nch, 0:nch]),
             r=[(row, ALL), (ident_f, ALL)], w=[(ps_set, ALL)])
        b.op("dve", lambda e: e.tensor_copy(out=dst.h[:], in_=ps_set.h[:, 0:nch]),
             r=[(ps_set, ALL)], w=[(dst, ALL)])

    b.dma("sp", lambda e: e.dma_start(out=ident_f.h[:], in_=identf.h.ap()),
          w=[(ident_f, ALL)])
    b.dma("sp", lambda e: e.dma_start(out=ident_b.h[:], in_=identb.h.ap()),
          w=[(ident_b, ALL)])
    b.dma("sp", lambda e: e.dma_start(out=s_mpref.h[:], in_=mpref16.h.ap()),
          w=[(s_mpref, ALL)])
    b.dma("sp", lambda e: e.dma_start(out=s_mhalo.h[:], in_=mhalo.h.ap()),
          w=[(s_mhalo, ALL)])
    for buf, val in ((c_eps, EPS), (c_nhalf, -0.5), (c_one, 1.0),
                     (c_eighth, 0.125), (c_i2048, 1.0 / 2048)):
        b.op("dve", lambda e, buf=buf, val=val: e.memset(buf.h[:], val),
             w=[(buf, ALL)])
    b.op("dve", lambda e: e.memset(c_oner.h[:], 1.0), w=[(c_oner, ALL)])
    b.op("dve", lambda e: e.memset(c_oneb.h[:], 1.0), w=[(c_oneb, ALL)])

    load_cols(bq_c, w_bq.h.ap().rearrange("(c p) -> c p", p=128), 8)
    load_cols(bk_c, w_bk.h.ap().rearrange("(c p) -> c p", p=128), 8)
    load_cols(bv_c, w_bv.h.ap().rearrange("(c p) -> c p", p=128), 8)
    load_cols(bd_c, w_bd.h.ap().rearrange("(c p) -> c p", p=128), 16)
    load_cols(cb_c, w_convb.h.ap().rearrange("(c p) -> c p", p=128), 16)
    load_cols(dv_c, w_dvec.h.ap().rearrange("(c p) -> c p", p=128), 16)
    for k in range(KCONV):
        load_cols(cw_c[k],
                  w_convw.h[k:k + 1, :].rearrange("a (c p) -> (a c) p", p=128),
                  16)
    rb_row = b.sb("rb_row", [1, 1], F32)
    b.dma("sp", lambda e: e.dma_start(out=rb_row.h[:], in_=w_rb.h.ap()),
          w=[(rb_row, ALL)])
    b.op("pe", lambda e: e.matmul(ps_set.h[:, 0:1], c_oner.h[0:1, :],
                                  rb_row.h[:], start=True, stop=True),
         r=[(c_oner, ALL), (rb_row, ALL)], w=[(ps_set, ALL)])
    b.op("dve", lambda e: e.tensor_copy(out=rb_c.h[:], in_=ps_set.h[:, 0:1]),
         r=[(ps_set, ALL)], w=[(rb_c, ALL)])
    alog_r = b.sb("alog_r", [1, NST], F32)
    b.dma("sp", lambda e: e.dma_start(out=alog_r.h[:], in_=w_alog.h.ap()),
          w=[(alog_r, ALL)])
    b.op("pe", lambda e: e.transpose(ps_set.h[0:NST, 0:1], alog_r.h[:],
                                     ident_f.h[0:1, 0:1]),
         r=[(alog_r, ALL), (ident_f, ALL)], w=[(ps_set, ALL)])
    b.op("act", lambda e: e.activation(out=a_col.h[:], in_=ps_set.h[0:NST, 0:1],
                                       func=AF.Exp),
         r=[(ps_set, ALL)], w=[(a_col, ALL)])
    b.op("dve", lambda e: e.tensor_scalar(
        out=a_col.h[:], in0=a_col.h[:], scalar1=-1.0 / 2048, scalar2=None,
        op0=ALU.mult), r=[(a_col, ALL)], w=[(a_col, ALL)])

    wq_sb = b.sb("wq_sb", [128, 8 * D], BF16)
    wk_sb = b.sb("wk_sb", [128, 8 * D], BF16)
    wv_sb = b.sb("wv_sb", [128, 8 * D], BF16)
    for (wsb, wdr) in ((wq_sb, w_wq), (wk_sb, w_wk), (wv_sb, w_wv)):
        for dc in range(8):
            b.dma("pool", lambda e, wsb=wsb, wdr=wdr, dc=dc: e.dma_start(
                out=wsb.h[:, dc * D:(dc + 1) * D],
                in_=wdr.h[dc * 128:(dc + 1) * 128, :]), w=[(wsb, dc)])
    p1_x = [b.sb(f"p1_x{i}", [128, D], F32) for i in range(4)]
    p1_xn = [b.sb(f"p1_xn{i}", [128, D], F32) for i in range(4)]
    xnfm = [b.sb(f"xnfm{i}", [128, 8 * 512], BF16) for i in range(2)]
    kvq_ps = b.ps("kvq_ps", [128, 512], F32)
    kvq_ps2 = b.ps("kvq_ps2", [128, 512], F32)
    kvq_ev = [b.sb(f"kvq_ev{i}", [128, 512], BF16) for i in range(2)]

    def p1_ln_ops(i):
        own = i < 16
        src = xs_ext.h if own else xkv.h
        r0 = (PRE + i * 128) if own else (i - 16) * 128
        xt, xnt = p1_x[i % 4], p1_xn[i % 4]
        o = [("dma", "sp", lambda e, src=src, r0=r0, xt=xt: e.dma_start(
            out=xt.h[:], in_=src[r0:r0 + 128, :]), [], [(xt, ALL)])]
        o += ln_tile_ops(xt, xnt)
        if own:
            o.append(("dma", "pool", lambda e, i=i, xnt=xnt: e.dma_start(
                out=d_xn32.h[i * 128:(i + 1) * 128, :], in_=xnt.h[:]),
                [(xnt, ALL)], [(d_xn32, i)]))
        o.append(("dma", "pool", lambda e, i=i, xnt=xnt: e.dma_start(
            out=d_xnbf.h[i * 128:(i + 1) * 128, :], in_=xnt.h[:]),
            [(xnt, ALL)], [(d_xnbf, i)]))
        return o

    b.wave([p1_ln_ops(i) for i in range(4)])
    for tt in range(8):
        xf = xnfm[tt % 2]
        for dc in range(8):
            b.dma("act", lambda e, tt=tt, dc=dc, xf=xf: e.dma_start(
                out=xf.h[:, dc * 512:(dc + 1) * 512],
                in_=d_xnbf.h[tt * 512:(tt + 1) * 512,
                             dc * 128:(dc + 1) * 128], transpose=True),
                r=[(d_xnbf, ALL)], w=[(xf, dc)])
        grp = [p1_ln_ops(i) for i in range(4 * tt + 4, min(4 * tt + 8, 32))]
        if grp:
            b.wave(grp)
        plan = [(wk_sb, bk_c, d_KT), (wv_sb, bv_c, d_VT)]
        if tt < 4:
            plan.append((wq_sb, bq_c, d_QT))
        for (wsb, bcols, ddst) in plan:
            for oc in range(8):
                ps = kvq_ps if oc % 2 == 0 else kvq_ps2
                ev = kvq_ev[oc % 2]
                for dc in range(8):
                    b.op("pe", lambda e, ps=ps, wsb=wsb, oc=oc, dc=dc, xf=xf:
                         e.matmul(
                             ps.h[:],
                             wsb.h[:, dc * D + oc * 128:dc * D + (oc + 1) * 128],
                             xf.h[:, dc * 512:(dc + 1) * 512],
                             start=(dc == 0), stop=(dc == 7)),
                         r=[(wsb, dc), (xf, dc)], w=[(ps, ALL)])
                b.op("act", lambda e, ps=ps, bcols=bcols, oc=oc, ev=ev:
                     e.activation(
                         out=ev.h[:], in_=ps.h[:], func=AF.Identity,
                         bias=bcols.h[:, oc:oc + 1]),
                     r=[(ps, ALL), (bcols, ALL)], w=[(ev, ALL)])
                b.dma("sp", lambda e, ddst=ddst, oc=oc, tt=tt, ev=ev:
                      e.dma_start(
                          out=ddst.h[oc * 128:(oc + 1) * 128,
                                     tt * 512:(tt + 1) * 512], in_=ev.h[:]),
                      r=[(ev, ALL)], w=[(ddst, (oc, tt))])

    if stage < 2:
        _tail(b, nc, out, [d_xnbf, d_xn32, d_KT, d_VT, d_QT])
        return

    b.barrier()
    gb_g = b.sb("gb_g", [128, D], F32)
    gb_b = b.sb("gb_b", [128, D], F32)
    bcast_row(w_lng, gb_g)
    bcast_row(w_lnb, gb_b)
    wxs_sb = b.sb("wxs_sb", [128, 8 * DI], BF16)
    wg_sb = b.sb("wg_sb", [128, 8 * DI], BF16)
    for dc in range(8):
        b.dma("pool", lambda e, dc=dc: e.dma_start(
            out=wxs_sb.h[:, dc * DI:(dc + 1) * DI],
            in_=w_in.h[dc * 128:(dc + 1) * 128, 0:DI]), w=[(wxs_sb, dc)])
        b.dma("pool", lambda e, dc=dc: e.dma_start(
            out=wg_sb.h[:, dc * DI:(dc + 1) * DI],
            in_=w_in.h[dc * 128:(dc + 1) * 128, DI:2 * DI]), w=[(wg_sb, dc)])
    p2_x = [b.sb(f"p2_x{i}", [128, D], F32) for i in range(4)]
    p2_xn = [b.sb(f"p2_xn{i}", [128, D], F32) for i in range(4)]
    p2_xh = [b.sb(f"p2_xh{i}", [128, D], F32) for i in range(4)]

    def p2_ln_ops(i):
        xt, xnt, xht = p2_x[i % 4], p2_xn[i % 4], p2_xh[i % 4]
        o = [("dma", "sp", lambda e, i=i, xt=xt: e.dma_start(
            out=xt.h[:], in_=xs_ext.h[i * 128:(i + 1) * 128, :]),
            [], [(xt, ALL)])]
        o += ln_tile_ops(xt, xnt)
        o += ln_tile_ops(xnt, xht)
        o.append(("op", "dve", lambda e, xht=xht: e.tensor_tensor(
            out=xht.h[:], in0=xht.h[:], in1=gb_g.h[:], op=ALU.mult),
            [(xht, ALL), (gb_g, ALL)], [(xht, ALL)]))
        o.append(("op", "pool", lambda e, xht=xht: e.tensor_tensor(
            out=xht.h[:], in0=xht.h[:], in1=gb_b.h[:], op=ALU.add),
            [(xht, ALL), (gb_b, ALL)], [(xht, ALL)]))
        o.append(("dma", "pool", lambda e, i=i, xht=xht: e.dma_start(
            out=d_xhbf.h[i * 128:(i + 1) * 128, :], in_=xht.h[:]),
            [(xht, ALL)], [(d_xhbf, i)]))
        return o

    b.wave([p2_ln_ops(i) for i in range(4)])
    xhfm = b.sb("xhfm", [128, 8 * 512], BF16)
    xsp_r = [b.sb(f"xsp{i}", [128, 515], F32) for i in range(2)]
    xsp_tail = b.sb("xsp_tail", [128, 16 * 3], F32)
    win_ps = b.ps("win_ps", [128, 512], F32)
    win_ps2 = b.ps("win_ps2", [128, 512], F32)
    conv_ps = b.ps("conv_ps", [128, 512], F32)
    conv_ps2 = b.ps("conv_ps2", [128, 512], F32)
    xsc_t_r = [b.sb(f"xsc_t{i}", [128, 512], F32) for i in range(2)]
    gt_t_r = [b.sb(f"gt_t{i}", [128, 512], F32) for i in range(2)]
    cwd = [b.sb(f"cwd{k}", [128, 16 * 128], F32) for k in range(KCONV)]
    for k in range(KCONV):
        for oc in range(16):
            b.op("dve", lambda e, k=k, oc=oc: e.tensor_scalar(
                out=cwd[k].h[:, oc * 128:(oc + 1) * 128], in0=ident_f.h[:],
                scalar1=cw_c[k].h[:, oc:oc + 1], scalar2=None, op0=ALU.mult),
                r=[(ident_f, ALL), (cw_c[k], ALL)], w=[(cwd[k], oc)])
    b.op("dve", lambda e: e.memset(xsp_tail.h[:], 0.0), w=[(xsp_tail, ALL)])
    for tt in range(8):
        for dc in range(8):
            b.dma("act", lambda e, tt=tt, dc=dc: e.dma_start(
                out=xhfm.h[:, dc * 512:(dc + 1) * 512],
                in_=d_xhbf.h[tt * 512:(tt + 1) * 512,
                             dc * 128:(dc + 1) * 128], transpose=True),
                r=[(d_xhbf, ALL)], w=[(xhfm, dc)])
        grp = [p2_ln_ops(i) for i in range(4 * tt + 4, min(4 * tt + 8, 32))]
        if grp:
            b.wave(grp)
        def p2a_A(oc):
            ps = win_ps if oc % 2 == 0 else win_ps2
            for dc in range(8):
                b.op("pe", lambda e, ps=ps, oc=oc, dc=dc: e.matmul(
                    ps.h[:],
                    wxs_sb.h[:, dc * DI + oc * 128:dc * DI + (oc + 1) * 128],
                    xhfm.h[:, dc * 512:(dc + 1) * 512],
                    start=(dc == 0), stop=(dc == 7)),
                    r=[(wxs_sb, dc), (xhfm, dc)], w=[(ps, ALL)])

        def p2a_B(oc, tt):
            ps = win_ps if oc % 2 == 0 else win_ps2
            cps = conv_ps if oc % 2 == 0 else conv_ps2
            xsp = xsp_r[oc % 2]
            b.op("act", lambda e, ps=ps, xsp=xsp: e.activation(
                out=xsp.h[:, 3:515], in_=ps.h[:], func=AF.Identity),
                r=[(ps, ALL)], w=[(xsp, "b")])
            if tt == 3:
                b.op("dve", lambda e, xsp=xsp: e.tensor_scalar(
                    out=xsp.h[:, 512:515], in0=xsp.h[:, 512:515],
                    scalar1=s_mhalo.h[:], scalar2=None, op0=ALU.mult),
                    r=[(xsp, "b"), (s_mhalo, ALL)], w=[(xsp, "b")])
            b.op("dve", lambda e, oc=oc, xsp=xsp: e.tensor_copy(
                out=xsp.h[:, 0:3], in_=xsp_tail.h[:, oc * 3:(oc + 1) * 3]),
                r=[(xsp_tail, oc)], w=[(xsp, "h")])
            b.op("dve", lambda e, oc=oc, xsp=xsp: e.tensor_copy(
                out=xsp_tail.h[:, oc * 3:(oc + 1) * 3],
                in_=xsp.h[:, 512:515]),
                r=[(xsp, "b")], w=[(xsp_tail, oc)])
            xsc_t = xsc_t_r[oc % 2]
            for k in range(KCONV):
                b.op("pe", lambda e, oc=oc, k=k, cps=cps, xsp=xsp: e.matmul(
                    cps.h[:], cwd[k].h[:, oc * 128:(oc + 1) * 128],
                    xsp.h[:, k:k + 512],
                    start=(k == 0), stop=(k == KCONV - 1)),
                    r=[(cwd[k], oc), (xsp, "b"), (xsp, "h")],
                    w=[(cps, ALL)])
            b.op("act", lambda e, oc=oc, cps=cps, xsc_t=xsc_t: e.activation(
                out=xsc_t.h[:], in_=cps.h[:], func=AF.Silu,
                bias=cb_c.h[:, oc:oc + 1]),
                r=[(cps, ALL), (cb_c, ALL)], w=[(xsc_t, ALL)])
            b.dma("pool", lambda e, tt=tt, oc=oc, xsc_t=xsc_t: e.dma_start(
                out=d_xscbf.h[oc * 128:(oc + 1) * 128,
                              tt * 512:(tt + 1) * 512], in_=xsc_t.h[:]),
                r=[(xsc_t, ALL)], w=[(d_xscbf, (oc, tt))])
            if tt >= 4:
                b.dma("sp", lambda e, tt=tt, oc=oc, xsc_t=xsc_t: e.dma_start(
                    out=d_xsc32.h[oc * 128:(oc + 1) * 128,
                                  (tt - 4) * 512:(tt - 3) * 512],
                    in_=xsc_t.h[:]),
                    r=[(xsc_t, ALL)], w=[(d_xsc32, (oc, tt))])

        p2a_A(0)
        for oc in range(16):
            if oc < 15:
                p2a_A(oc + 1)
            p2a_B(oc, tt)
        if tt >= 4:
            for oc in range(16):
                ps = win_ps if oc % 2 == 0 else win_ps2
                for dc in range(8):
                    b.op("pe", lambda e, ps=ps, oc=oc, dc=dc: e.matmul(
                        ps.h[:],
                        wg_sb.h[:, dc * DI + oc * 128:dc * DI + (oc + 1) * 128],
                        xhfm.h[:, dc * 512:(dc + 1) * 512],
                        start=(dc == 0), stop=(dc == 7)),
                        r=[(wg_sb, dc), (xhfm, dc)], w=[(ps, ALL)])
                gt_t = gt_t_r[oc % 2]
                b.op("act", lambda e, ps=ps, gt_t=gt_t: e.activation(
                    out=gt_t.h[:], in_=ps.h[:], func=AF.Silu),
                    r=[(ps, ALL)], w=[(gt_t, ALL)])
                b.dma("sp", lambda e, tt=tt, oc=oc, gt_t=gt_t: e.dma_start(
                    out=d_gs32.h[oc * 128:(oc + 1) * 128,
                                 (tt - 4) * 512:(tt - 3) * 512],
                    in_=gt_t.h[:]),
                    r=[(gt_t, ALL)], w=[(d_gs32, (oc, tt))])

    b.barrier()
    wd_sb = b.sb("wd_sb", [128, 16 * DI], BF16)
    for dc in range(16):
        b.dma("pool", lambda e, dc=dc: e.dma_start(
            out=wd_sb.h[:, dc * DI:(dc + 1) * DI],
            in_=w_wd.h[dc * 128:(dc + 1) * 128, :]), w=[(wd_sb, dc)])
    bmat_sb = b.sb("bmat_sb", [NST, DI], F32)
    bt_sb = b.sb("bt_sb", [128, 16 * NST], BF16)
    ps_bt = b.ps("ps_bt", [128, 512], F32)
    b.dma("sp", lambda e: e.dma_start(out=bmat_sb.h[:], in_=w_bmat.h.ap()),
          w=[(bmat_sb, ALL)])
    for dc in range(16):
        b.op("pe", lambda e, dc=dc: e.transpose(
            ps_bt.h[:, 0:NST], bmat_sb.h[:, dc * 128:(dc + 1) * 128],
            ident_f.h[0:NST, 0:NST]),
            r=[(bmat_sb, ALL), (ident_f, ALL)], w=[(ps_bt, ALL)])
        b.op("act", lambda e, dc=dc: e.activation(
            out=bt_sb.h[:, dc * NST:(dc + 1) * NST], in_=ps_bt.h[:, 0:NST],
            func=AF.Identity, scale=c_i2048.h[:]),
            r=[(ps_bt, ALL), (c_i2048, ALL)], w=[(bt_sb, dc)])

    xscb_r = [b.sb(f"xscb{i}", [128, 16 * 512], BF16) for i in range(2)]
    z_ps = b.ps("z_ps", [128, 512], F32)
    z_ps2 = b.ps("z_ps2", [128, 512], F32)
    ez_t_r = [b.sb(f"ez_t{i}", [128, 512], F32) for i in range(2)]
    spz_t_r = [b.sb(f"spz_t{i}", [128, 512], F32) for i in range(2)]
    del_ps = b.ps("del_ps", [1, 512], F32)
    bx_ps = b.ps("bx_ps", [NST, 512], F32)
    db_ps = b.ps("db_ps", [NST, 512], F32)
    del_row_r = [b.sb(f"del_row{i}", [1, 512], F32) for i in range(2)]
    db_sb_r = [b.sb(f"db_sb{i}", [NST, 512], F32) for i in range(2)]
    dA = b.sb("dA", [NST, EXT], F32)
    BxT = b.sb("BxT", [NST, EXT], F32)
    hT = b.sb("hT", [NST, EXT], F32)

    for tt in range(8):
        xscb = xscb_r[tt % 2]
        for dc in range(16):
            b.dma("sp", lambda e, tt=tt, dc=dc, xscb=xscb: e.dma_start(
                out=xscb.h[:, dc * 512:(dc + 1) * 512],
                in_=d_xscbf.h[dc * 128:(dc + 1) * 128,
                              tt * 512:(tt + 1) * 512]),
                r=[(d_xscbf, (dc, tt))], w=[(xscb, dc)])
        del_row = del_row_r[tt % 2]
        db_sb = db_sb_r[tt % 2]

        def p2b_A(zc):
            ps = z_ps if zc % 2 == 0 else z_ps2
            for dc in range(16):
                b.op("pe", lambda e, ps=ps, zc=zc, dc=dc, xscb=xscb: e.matmul(
                    ps.h[:],
                    wd_sb.h[:, dc * DI + zc * 128:dc * DI + (zc + 1) * 128],
                    xscb.h[:, dc * 512:(dc + 1) * 512],
                    start=(dc == 0), stop=(dc == 15)),
                    r=[(wd_sb, dc), (xscb, dc)], w=[(ps, ALL)])

        def p2b_B(zc):
            ps = z_ps if zc % 2 == 0 else z_ps2
            ez_t, spz_t = ez_t_r[zc % 2], spz_t_r[zc % 2]
            b.op("act", lambda e, ps=ps, zc=zc, ez_t=ez_t: e.activation(
                out=ez_t.h[:], in_=ps.h[:], func=AF.Exp,
                bias=bd_c.h[:, zc:zc + 1]),
                r=[(ps, ALL), (bd_c, ALL)], w=[(ez_t, ALL)])
            b.op("act", lambda e, ez_t=ez_t, spz_t=spz_t: e.activation(
                out=spz_t.h[:], in_=ez_t.h[:], func=AF.Ln, bias=c_one.h[:]),
                r=[(ez_t, ALL), (c_one, ALL)], w=[(spz_t, ALL)])
            b.op("pe", lambda e, zc=zc, spz_t=spz_t: e.matmul(
                del_ps.h[:], c_one.h[:, 0:1], spz_t.h[:],
                start=(zc == 0), stop=(zc == 15)),
                r=[(c_one, ALL), (spz_t, ALL)], w=[(del_ps, ALL)])

        p2b_A(0)
        for zc in range(16):
            if zc < 15:
                p2b_A(zc + 1)
            p2b_B(zc)
        b.op("dve", lambda e, del_row=del_row: e.tensor_copy(
            out=del_row.h[:], in_=del_ps.h[:]),
            r=[(del_ps, ALL)], w=[(del_row, ALL)])
        for dc in range(16):
            b.op("pe", lambda e, dc=dc, xscb=xscb: e.matmul(
                bx_ps.h[:], bt_sb.h[:, dc * NST:(dc + 1) * NST],
                xscb.h[:, dc * 512:(dc + 1) * 512],
                start=(dc == 0), stop=(dc == 15)),
                r=[(bt_sb, dc), (xscb, dc)], w=[(bx_ps, ALL)])
        b.op("pe", lambda e, del_row=del_row: e.matmul(
            db_ps.h[:], c_oner.h[0:1, 0:NST], del_row.h[:],
            start=True, stop=True),
            r=[(c_oner, ALL), (del_row, ALL)], w=[(db_ps, ALL)])
        b.op("dve", lambda e, db_sb=db_sb: e.tensor_copy(
            out=db_sb.h[:], in_=db_ps.h[:]),
            r=[(db_ps, ALL)], w=[(db_sb, ALL)])
        b.op("act", lambda e, tt=tt, db_sb=db_sb: e.activation(
            out=dA.h[:, tt * 512:(tt + 1) * 512], in_=db_sb.h[:],
            func=AF.Exp, scale=a_col.h[:]),
            r=[(db_sb, ALL), (a_col, ALL)], w=[(dA, tt)])
        b.op("dve", lambda e, tt=tt, db_sb=db_sb: e.tensor_tensor(
            out=BxT.h[:, tt * 512:(tt + 1) * 512], in0=bx_ps.h[:],
            in1=db_sb.h[:], op=ALU.mult),
            r=[(bx_ps, ALL), (db_sb, ALL)], w=[(BxT, tt)])
        if tt < 4:
            b.op("dve", lambda e, tt=tt: e.tensor_scalar(
                out=BxT.h[:, tt * 512:(tt + 1) * 512],
                in0=BxT.h[:, tt * 512:(tt + 1) * 512],
                scalar1=s_mpref.h[:], scalar2=None, op0=ALU.mult),
                r=[(BxT, tt), (s_mpref, ALL)], w=[(BxT, tt)])

    b.op("dve", lambda e: e.tensor_tensor_scan(
        out=hT.h[:], data0=dA.h[:], data1=BxT.h[:], initial=0.0,
        op0=ALU.mult, op1=ALU.add),
        r=[(dA, ALL)] + [(dA, t) for t in range(8)] +
          [(BxT, ALL)] + [(BxT, t) for t in range(8)],
        w=[(hT, ALL)])
    b.op("dve", lambda e: e.tensor_copy(out=hTb.h[:], in_=hT.h[:, PRE:]),
         r=[(hT, ALL)], w=[(hTb, ALL)])
    if DEBUG:
        d_hT = b.dr("d_hT", [NST, EXT], F32)
        d_dA = b.dr("d_dA", [NST, EXT], F32)
        d_BxT = b.dr("d_BxT", [NST, EXT], F32)
        b.dma("pool", lambda e: e.dma_start(out=d_hT.h[:], in_=hT.h[:]),
              r=[(hT, ALL)], w=[(d_hT, ALL)])
        b.dma("pool", lambda e: e.dma_start(out=d_dA.h[:], in_=dA.h[:]),
              r=[(dA, ALL)] + [(dA, t) for t in range(8)], w=[(d_dA, ALL)])
        b.dma("pool", lambda e: e.dma_start(out=d_BxT.h[:], in_=BxT.h[:]),
              r=[(BxT, ALL)] + [(BxT, t) for t in range(8)], w=[(d_BxT, ALL)])
        globals()["_dbg_p2"] = [d_hT, d_dA, d_BxT]

    if stage < 3:
        _tail(b, nc, out, [d_xhbf, d_xscbf, d_xsc32, d_gs32]
              + globals().get("_dbg_p2", []))
        return

    b.barrier()
    g1_g = b.sb("g1_g", [128, D], F32)
    g1_b = b.sb("g1_b", [128, D], F32)
    rw_b = b.sb("rw_b", [128, D], F32)
    misc_ps = b.ps("misc_ps", [128, 512], F32)
    b._bc_ps = misc_ps
    bcast_row(w_ln1g, g1_g)
    bcast_row(w_ln1b, g1_b)
    bcast_row(w_rw, rw_b)
    kt_h = [b.sb(f"kt_h{i}", [64, NKV], BF16) for i in range(2)]
    vt_h = [b.sb(f"vt_h{i}", [64, NKV], BF16) for i in range(2)]
    qt_h = [b.sb(f"qt_h{i}", [64, OWN], BF16) for i in range(2)]
    vtok = [b.sb(f"vtok{i}", [128, 32 * 65], BF16) for i in range(2)]
    pt_r = [b.sb(f"pt_t{i}", [128, 1024], BF16) for i in range(4)]
    qsq = [b.sb(f"qsq{i}", [64, NKV], BF16) for i in range(2)]
    r2row = [b.sb(f"r2row{i}", [1, 8], F32) for i in range(2)]
    m_sc = [b.sb(f"m_sc{i}", [1, 8], F32) for i in range(2)]
    negm = [b.sb(f"negm{i}", [128, 1], F32) for i in range(2)]
    ps_vtr = b.ps("ps_vtr", [128, 512], BF16)
    sT_r = [b.ps(f"sT_ps{i}", [128, 1024], F32) for i in range(2)]
    o_ps = b.ps("o_ps", [65, 1024], F32)
    lrec = b.sb("lrec", [1, 1024], F32)
    lrecb = b.sb("lrecb", [64, 1024], F32)
    attnT = b.sb("attnT", [128, 8 * OWN], BF16)

    for i in range(2):
        b.op("dve", lambda e, i=i: e.memset(vtok[i].h[:], 1.0),
             w=[(vtok[i], ALL)])

    def p3_loads(h):
        bi = h % 2
        b.dma("sp", lambda e, h=h, bi=bi: e.dma_start(
            out=kt_h[bi].h[:], in_=d_KT.h[h * 64:(h + 1) * 64, :]),
            r=[(d_KT, ALL)], w=[(kt_h[bi], ALL)])
        b.dma("sp", lambda e, h=h, bi=bi: e.dma_start(
            out=vt_h[bi].h[:], in_=d_VT.h[h * 64:(h + 1) * 64, :]),
            r=[(d_VT, ALL)], w=[(vt_h[bi], ALL)])
        b.dma("sp", lambda e, h=h, bi=bi: e.dma_start(
            out=qt_h[bi].h[:], in_=d_QT.h[h * 64:(h + 1) * 64, :]),
            r=[(d_QT, ALL)], w=[(qt_h[bi], ALL)])

    def p3_setup(h):
        bi = h % 2
        kb, vb, qb = kt_h[bi], vt_h[bi], qt_h[bi]
        qs, r2, ms, nm, vk = qsq[bi], r2row[bi], m_sc[bi], negm[bi], vtok[bi]
        b.op("act", lambda e, qs=qs, qb=qb: e.activation(
            out=qs.h[0:64, 0:OWN], in_=qb.h[:], func=AF.Square),
            r=[(qb, ALL)], w=[(qs, ALL)])
        for j in range(4):
            b.op("pe", lambda e, j=j, qs=qs: e.matmul(
                misc_ps.h[0:1, 0:512], c_oneb.h[0:64, 0:1],
                qs.h[0:64, j * 512:(j + 1) * 512],
                start=(j == 0), stop=(j == 3)),
                r=[(c_oneb, ALL), (qs, ALL)], w=[(misc_ps, ALL)])
        b.op("dve", lambda e, r2=r2: e.reduce_max(
            out=r2.h[0:1, 0:1], in_=misc_ps.h[0:1, 0:512], axis=AX),
            r=[(misc_ps, ALL)], w=[(r2, 0)])
        b.op("act", lambda e, qs=qs, kb=kb: e.activation(
            out=qs.h[:], in_=kb.h[:], func=AF.Square),
            r=[(kb, ALL), (qs, ALL)], w=[(qs, ALL)])
        for j in range(8):
            b.op("pe", lambda e, j=j, qs=qs: e.matmul(
                misc_ps.h[0:1, 0:512], c_oneb.h[0:64, 0:1],
                qs.h[0:64, j * 512:(j + 1) * 512],
                start=(j == 0), stop=(j == 7)),
                r=[(c_oneb, ALL), (qs, ALL)], w=[(misc_ps, ALL)])
        b.op("dve", lambda e, r2=r2: e.reduce_max(
            out=r2.h[0:1, 1:2], in_=misc_ps.h[0:1, 0:512], axis=AX),
            r=[(misc_ps, ALL)], w=[(r2, 1)])
        b.op("dve", lambda e, r2=r2, ms=ms: e.tensor_tensor(
            out=ms.h[0:1, 0:1], in0=r2.h[0:1, 0:1], in1=r2.h[0:1, 1:2],
            op=ALU.mult), r=[(r2, 0), (r2, 1)], w=[(ms, ALL)])
        b.op("act", lambda e, ms=ms: e.activation(
            out=ms.h[0:1, 1:2], in_=ms.h[0:1, 0:1], func=AF.Ln),
            r=[(ms, ALL)], w=[(ms, ALL)])
        b.op("act", lambda e, ms=ms: e.activation(
            out=ms.h[0:1, 2:3], in_=ms.h[0:1, 1:2], func=AF.Exp, scale=0.5),
            r=[(ms, ALL)], w=[(ms, ALL)])
        b.op("dve", lambda e, ms=ms: e.tensor_scalar(
            out=ms.h[0:1, 3:4], in0=ms.h[0:1, 2:3], scalar1=-0.125,
            scalar2=None, op0=ALU.mult), r=[(ms, ALL)], w=[(ms, ALL)])
        b.op("pe", lambda e, ms=ms: e.matmul(
            misc_ps.h[:, 0:1], c_oner.h[0:1, :], ms.h[0:1, 3:4],
            start=True, stop=True),
            r=[(c_oner, ALL), (ms, ALL)], w=[(misc_ps, ALL)])
        b.op("dve", lambda e, nm=nm: e.tensor_copy(out=nm.h[:],
                                                   in_=misc_ps.h[:, 0:1]),
             r=[(misc_ps, ALL)], w=[(nm, ALL)])
        for kt in range(32):
            b.op("pe", lambda e, kt=kt, vb=vb: e.transpose(
                ps_vtr.h[:, 0:64], vb.h[:, kt * 128:(kt + 1) * 128],
                ident_b.h[0:64, 0:64]),
                r=[(vb, ALL), (ident_b, ALL)], w=[(ps_vtr, ALL)])
            b.op("dve", lambda e, kt=kt, vk=vk: e.tensor_copy(
                out=vk.h[:, kt * 65:kt * 65 + 64], in_=ps_vtr.h[:, 0:64]),
                r=[(ps_vtr, ALL)], w=[(vk, kt)])

    def p3_qh(h, qh):
        bi = h % 2
        kb, qb, vk, nm = kt_h[bi], qt_h[bi], vtok[bi], negm[bi]
        ch, hw = h // 2, (h % 2) * 64
        q0 = qh * 1024

        def att_S(kt):
            sps = sT_r[kt % 2]
            for j in range(2):
                b.op("pe", lambda e, sps=sps, kt=kt, j=j: e.matmul(
                    sps.h[:, j * 512:(j + 1) * 512],
                    kb.h[:, kt * 128:(kt + 1) * 128],
                    qb.h[:, q0 + j * 512:q0 + (j + 1) * 512],
                    start=True, stop=True),
                    r=[(kb, ALL), (qb, ALL)], w=[(sps, ALL)])

        att_S(0)
        for kt in range(32):
            sps, ptb = sT_r[kt % 2], pt_r[kt % 4]
            if kt < 31:
                att_S(kt + 1)
            b.op("act", lambda e, sps=sps, ptb=ptb: e.activation(
                out=ptb.h[:], in_=sps.h[:], func=AF.Exp,
                scale=c_eighth.h[:], bias=nm.h[:]),
                r=[(sps, ALL), (c_eighth, ALL), (nm, ALL)],
                w=[(ptb, ALL)])
            for j in range(2):
                b.op("pe", lambda e, ptb=ptb, kt=kt, j=j: e.matmul(
                    o_ps.h[:, j * 512:(j + 1) * 512],
                    vk.h[:, kt * 65:(kt + 1) * 65],
                    ptb.h[:, j * 512:(j + 1) * 512],
                    start=(kt == 0), stop=(kt == 31)),
                    r=[(vk, kt), (ptb, ALL)], w=[(o_ps, j)])
        b.op("dve", lambda e: e.reciprocal(out=lrec.h[:],
                                           in_=o_ps.h[64:65, :]),
             r=[(o_ps, 0), (o_ps, 1)], w=[(lrec, ALL)])
        for j in range(2):
            b.op("pe", lambda e, j=j: e.matmul(
                misc_ps.h[0:64, 0:512], c_oner.h[0:1, 0:64],
                lrec.h[0:1, j * 512:(j + 1) * 512],
                start=True, stop=True),
                r=[(c_oner, ALL), (lrec, ALL)], w=[(misc_ps, ALL)])
            b.op("dve", lambda e, j=j: e.tensor_copy(
                out=lrecb.h[:, j * 512:(j + 1) * 512],
                in_=misc_ps.h[0:64, 0:512]),
                r=[(misc_ps, ALL)], w=[(lrecb, j)])
        b.op("dve", lambda e, ch=ch, hw=hw, q0=q0: e.tensor_tensor(
            out=attnT.h[hw:hw + 64, ch * OWN + q0:ch * OWN + q0 + 1024],
            in0=o_ps.h[0:64, :], in1=lrecb.h[:], op=ALU.mult),
            r=[(o_ps, 0), (o_ps, 1), (lrecb, 0), (lrecb, 1)],
            w=[(attnT, (ch, hw, q0))])

    p3_loads(0)
    p3_setup(0)
    for h in range(NH):
        if h + 1 < NH:
            p3_loads(h + 1)
        p3_qh(h, 0)
        if h + 1 < NH:
            p3_setup(h + 1)
        p3_qh(h, 1)

    wo_sb = b.sb("wo_sb", [128, 8 * D], BF16)
    for dc in range(8):
        b.dma("pool", lambda e, dc=dc: e.dma_start(
            out=wo_sb.h[:, dc * D:(dc + 1) * D],
            in_=w_wo.h[dc * 128:(dc + 1) * 128, :]), w=[(wo_sb, dc)])
    bo_row = b.sb("bo_row", [1, D], F32)
    b.dma("sp", lambda e: e.dma_start(out=bo_row.h[:], in_=w_bo.h.ap()),
          w=[(bo_row, ALL)])
    r1_t_r = [b.sb(f"r1_t{i}", [128, D], F32) for i in range(2)]
    x1_t_r = [b.sb(f"x1_t{i}", [128, D], F32) for i in range(2)]
    keys_t_r = [b.sb(f"keys_t{i}", [128, D], F32) for i in range(2)]
    kcol_r = [b.sb(f"kcol{i}", [128, 1], F32) for i in range(2)]
    kint_r = [b.sb(f"kint{i}", [128, 1], I32) for i in range(2)]
    kback_r = [b.sb(f"kback{i}", [128, 1], F32) for i in range(2)]
    kd_r = [b.sb(f"kd{i}", [128, 1], F32) for i in range(2)]

    def p3_block(tt):
        r1_t, x1_t, keys_t = r1_t_r[tt % 2], x1_t_r[tt % 2], keys_t_r[tt % 2]
        kcol, kint, kback, kd = (kcol_r[tt % 2], kint_r[tt % 2],
                                 kback_r[tt % 2], kd_r[tt % 2])
        xn_r = xn_r_r[tt % 2]
        wps_t = sT_r[tt % 2]
        for j in range(2):
            for dc in range(8):
                b.op("pe", lambda e, tt=tt, j=j, dc=dc, wps=wps_t: e.matmul(
                    wps.h[:, j * 512:(j + 1) * 512],
                    attnT.h[:, dc * OWN + tt * 128:dc * OWN + (tt + 1) * 128],
                    wo_sb.h[:, dc * D + j * 512:dc * D + (j + 1) * 512],
                    start=(dc == 0), stop=False),
                    r=[(attnT, ALL), (wo_sb, dc)], w=[(wps_t, j)])
            b.op("pe", lambda e, j=j, wps=wps_t: e.matmul(
                wps.h[:, j * 512:(j + 1) * 512], c_oner.h[0:1, :],
                bo_row.h[0:1, j * 512:(j + 1) * 512],
                start=False, stop=True),
                r=[(c_oner, ALL), (bo_row, ALL)], w=[(wps_t, j)])
        b.dma("sp", lambda e, tt=tt, xn_r=xn_r: e.dma_start(
            out=xn_r.h[:], in_=d_xn32.h[tt * 128:(tt + 1) * 128, :]),
            r=[(d_xn32, tt)], w=[(xn_r, ALL)])
        b.op("dve", lambda e, wps=wps_t, r1_t=r1_t, xn_r=xn_r:
             e.tensor_tensor(
                 out=r1_t.h[:], in0=wps.h[:], in1=xn_r.h[:], op=ALU.add),
             r=[(wps_t, 0), (wps_t, 1), (xn_r, ALL)], w=[(r1_t, ALL)])
        ln_tile(r1_t, x1_t)
        b.op("dve", lambda e: e.tensor_tensor(
            out=x1_t.h[:], in0=x1_t.h[:], in1=g1_g.h[:], op=ALU.mult),
            r=[(x1_t, ALL), (g1_g, ALL)], w=[(x1_t, ALL)])
        b.op("pool", lambda e: e.tensor_tensor(
            out=x1_t.h[:], in0=x1_t.h[:], in1=g1_b.h[:], op=ALU.add),
            r=[(x1_t, ALL), (g1_b, ALL)], w=[(x1_t, ALL)])
        b.dma("pool", lambda e, tt=tt: e.dma_start(
            out=d_x132.h[tt * 128:(tt + 1) * 128, :], in_=x1_t.h[:]),
            r=[(x1_t, ALL)], w=[(d_x132, tt)])
        b.dma("pool", lambda e, tt=tt: e.dma_start(
            out=d_x1bf.h[tt * 128:(tt + 1) * 128, :], in_=x1_t.h[:]),
            r=[(x1_t, ALL)], w=[(d_x1bf, tt)])
        b.op("dve", lambda e: e.tensor_tensor(
            out=keys_t.h[:], in0=x1_t.h[:], in1=rw_b.h[:], op=ALU.mult),
            r=[(x1_t, ALL), (rw_b, ALL)], w=[(keys_t, ALL)])
        b.op("dve", lambda e: e.reduce_sum(out=kcol.h[:], in_=keys_t.h[:],
                                           axis=AX),
             r=[(keys_t, ALL)], w=[(kcol, ALL)])
        b.op("dve", lambda e: e.tensor_scalar(
            out=kcol.h[:], in0=kcol.h[:], scalar1=rb_c.h[:], scalar2=None,
            op0=ALU.add), r=[(kcol, ALL), (rb_c, ALL)], w=[(kcol, ALL)])
        b.op("dve", lambda e: e.tensor_copy(out=kint.h[:], in_=kcol.h[:]),
             r=[(kcol, ALL)], w=[(kint, ALL)])
        b.op("dve", lambda e: e.tensor_copy(out=kback.h[:], in_=kint.h[:]),
             r=[(kint, ALL)], w=[(kback, ALL)])
        b.op("dve", lambda e: e.tensor_tensor(
            out=kd.h[:], in0=kcol.h[:], in1=kback.h[:], op=ALU.is_lt),
            r=[(kcol, ALL), (kback, ALL)], w=[(kd, ALL)])
        b.op("dve", lambda e: e.tensor_sub(out=kcol.h[:], in0=kback.h[:],
                                           in1=kd.h[:]),
             r=[(kback, ALL), (kd, ALL)], w=[(kcol, ALL)])
        b.op("dve", lambda e: e.tensor_scalar(
            out=kd.h[:], in0=kcol.h[:], scalar1=0.25, scalar2=None,
            op0=ALU.mult), r=[(kcol, ALL)], w=[(kd, ALL)])
        b.op("dve", lambda e: e.tensor_copy(out=kint.h[:], in_=kd.h[:]),
             r=[(kd, ALL)], w=[(kint, ALL)])
        b.op("dve", lambda e: e.tensor_copy(out=kback.h[:], in_=kint.h[:]),
             r=[(kint, ALL)], w=[(kback, ALL)])
        b.op("dve", lambda e: e.tensor_tensor(
            out=kd.h[:], in0=kd.h[:], in1=kback.h[:], op=ALU.is_lt),
            r=[(kd, ALL), (kback, ALL)], w=[(kd, ALL)])
        b.op("dve", lambda e: e.tensor_sub(out=kback.h[:], in0=kback.h[:],
                                           in1=kd.h[:]),
             r=[(kback, ALL), (kd, ALL)], w=[(kback, ALL)])
        b.op("dve", lambda e, tt=tt: e.scalar_tensor_tensor(
            out=idx_cols.h[:, tt:tt + 1], in0=kback.h[:], scalar=-4.0,
            in1=kcol.h[:], op0=ALU.mult, op1=ALU.add),
            r=[(kback, ALL), (kcol, ALL)], w=[(idx_cols, tt)])

    for tt in range(16):
        p3_block(tt)

    if stage < 4:
        _tail(b, nc, out, [d_x1bf, d_x132])
        return

    b.barrier()
    g2_g = b.sb("g2_g", [128, D], F32)
    g2_b = b.sb("g2_b", [128, D], F32)
    bcast_row(w_ln2g, g2_g)
    bcast_row(w_ln2b, g2_b)
    w1_sb = [b.sb(f"w1_sb{i}", [128, 8 * DI], BF16) for i in range(2)]
    w2_sb = [b.sb(f"w2_sb{i}", [128, 8 * D], BF16) for i in range(2)]
    b1cols = b.sb("b1cols", [128, 32], F32)
    b1row = b.sb("b1row", [32, 128], F32)
    b2row = b.sb("b2row", [1, D], F32)
    ps_b1 = b.ps("ps_b1", [128, 512], F32)
    mask_c = b.sb("mask_c", [128, 16], F32)
    x1fm = [b.sb(f"x1fm{i}", [128, 8 * 512], BF16) for i in range(2)]
    h_ps = b.ps("h_ps", [128, 512], F32)
    h_ps2 = b.ps("h_ps2", [128, 512], F32)
    h_sb = b.sb("h_sb", [128, 16 * 512], BF16)
    y_ps = b.ps("y_ps", [128, 1024], F32)
    y_ps2 = b.ps("y_ps2", [128, 1024], F32)
    acc = b.sb("acc", [128, 16 * 1024], BF16)
    g_t = b.sb("g_t", [128, D], F32)
    x2_t = xn_r_r[0]

    def p4_ln2(ti):
        b.op("dve", lambda e, ti=ti: e.tensor_copy(
            out=x2_t.h[:], in_=acc.h[:, ti * 1024:(ti + 1) * 1024]),
            r=[(acc, ti)], w=[(x2_t, ALL)])
        ln_tile(x2_t, g_t)
        b.op("dve", lambda e: e.tensor_tensor(
            out=g_t.h[:], in0=g_t.h[:], in1=g2_g.h[:], op=ALU.mult),
            r=[(g_t, ALL), (g2_g, ALL)], w=[(g_t, ALL)])
        b.op("pool", lambda e: e.tensor_tensor(
            out=g_t.h[:], in0=g_t.h[:], in1=g2_b.h[:], op=ALU.add),
            r=[(g_t, ALL), (g2_b, ALL)], w=[(g_t, ALL)])
        b.op("act", lambda e: e.activation(out=g_t.h[:], in_=g_t.h[:],
                                           func=AF.Silu),
             r=[(g_t, ALL)], w=[(g_t, ALL)])
        b.dma("pool", lambda e, ti=ti: e.dma_start(
            out=d_g32.h[ti * 128:(ti + 1) * 128, :], in_=g_t.h[:]),
            r=[(g_t, ALL)], w=[(d_g32, ti)])

    def p4_evac(e_i, half, yp, ti):
        if e_i == 0 and half == 0:
            xr = xn_r_r[ti % 2]
            b.dma("sp", lambda e, ti=ti, xr=xr: e.dma_start(
                out=xr.h[:], in_=d_x132.h[ti * 128:(ti + 1) * 128, :]),
                r=[(d_x132, ti)], w=[(xr, ALL)])
            b.op("dve", lambda e, yp=yp, ti=ti, xr=xr: e.scalar_tensor_tensor(
                out=acc.h[:, ti * 1024:(ti + 1) * 1024],
                in0=yp.h[:], scalar=mask_c.h[:, ti:ti + 1],
                in1=xr.h[:], op0=ALU.mult, op1=ALU.add),
                r=[(yp, 0), (yp, 1), (mask_c, ALL), (xr, ALL)],
                w=[(acc, ti)])
        else:
            b.op("dve", lambda e, yp=yp, ti=ti: e.scalar_tensor_tensor(
                out=acc.h[:, ti * 1024:(ti + 1) * 1024],
                in0=yp.h[:], scalar=mask_c.h[:, ti:ti + 1],
                in1=acc.h[:, ti * 1024:(ti + 1) * 1024],
                op0=ALU.mult, op1=ALU.add),
                r=[(yp, 0), (yp, 1), (mask_c, ALL), (acc, ti)],
                w=[(acc, ti)])
        if e_i == NE - 1 and half == 1:
            p4_ln2(ti)

    def p4_w1_load(e_i, half):
        wb = w1_sb[(e_i * 2 + half) % 2]
        for dc in range(8):
            b.dma("pool", lambda e, e_i=e_i, half=half, dc=dc, wb=wb:
                  e.dma_start(
                      out=wb.h[:, dc * DI:(dc + 1) * DI],
                      in_=w_w1.h[e_i, dc * 128:(dc + 1) * 128,
                                 half * 2048:(half + 1) * 2048]),
                  w=[(wb, dc)])
        return wb

    def p4_w2_load(e_i, half, quarter):
        wb = w2_sb[quarter % 2]
        for fc8 in range(8):
            fcg = half * 2048 + (quarter * 8 + fc8) * 128
            b.dma("pool", lambda e, e_i=e_i, fcg=fcg, fc8=fc8, wb=wb:
                  e.dma_start(out=wb.h[:, fc8 * D:(fc8 + 1) * D],
                              in_=w_w2.h[e_i, fcg:fcg + 128, :]),
                  w=[(wb, fc8)])
        return wb

    w1_cur = p4_w1_load(0, 0)
    for e_i in range(NE):
        b.dma("sp", lambda e, e_i=e_i: e.dma_start(
            out=b1row.h[:], in_=w_b1.h[e_i:e_i + 1, :].rearrange(
                "a (c p) -> (a c) p", p=128)),
            w=[(b1row, ALL)])
        b.op("pe", lambda e: e.transpose(ps_b1.h[:, 0:32], b1row.h[:],
                                         ident_f.h[0:32, 0:32]),
             r=[(b1row, ALL), (ident_f, ALL)], w=[(ps_b1, ALL)])
        b.op("dve", lambda e: e.tensor_copy(out=b1cols.h[:],
                                            in_=ps_b1.h[:, 0:32]),
             r=[(ps_b1, ALL)], w=[(b1cols, ALL)])
        b.dma("sp", lambda e, e_i=e_i: e.dma_start(
            out=b2row.h[:], in_=w_b2.h[e_i:e_i + 1, :]), w=[(b2row, ALL)])
        b.op("dve", lambda e, e_i=e_i: e.tensor_scalar(
            out=mask_c.h[:], in0=idx_cols.h[:], scalar1=float(e_i),
            scalar2=None, op0=ALU.is_equal),
            r=[(idx_cols, ALL)] + [(idx_cols, t) for t in range(16)],
            w=[(mask_c, ALL)])
        for half in range(2):
            wb1 = w1_cur
            if not (e_i == NE - 1 and half == 1):
                ne_, nh_ = (e_i, 1) if half == 0 else (e_i + 1, 0)
                w1_cur = p4_w1_load(ne_, nh_)
            for tt4 in range(4):
                xf = x1fm[tt4 % 2]
                for dc in range(8):
                    b.dma("act", lambda e, tt4=tt4, dc=dc, xf=xf: e.dma_start(
                        out=xf.h[:, dc * 512:(dc + 1) * 512],
                        in_=d_x1bf.h[tt4 * 512:(tt4 + 1) * 512,
                                     dc * 128:(dc + 1) * 128], transpose=True),
                        r=[(d_x1bf, ALL)], w=[(xf, dc)])
                for fc in range(16):
                    ps = h_ps if fc % 2 == 0 else h_ps2
                    for dc in range(8):
                        b.op("pe", lambda e, ps=ps, fc=fc, dc=dc, xf=xf,
                             wb1=wb1: e.matmul(
                                 ps.h[:],
                                 wb1.h[:, dc * DI + fc * 128:
                                       dc * DI + (fc + 1) * 128],
                                 xf.h[:, dc * 512:(dc + 1) * 512],
                                 start=(dc == 0), stop=(dc == 7)),
                             r=[(wb1, dc), (xf, dc)], w=[(ps, ALL)])
                    b.op("act", lambda e, ps=ps, fc=fc, half=half:
                         e.activation(
                             out=h_sb.h[:, fc * 512:(fc + 1) * 512],
                             in_=ps.h[:], func=AF.Relu,
                             bias=b1cols.h[:, half * 16 + fc:
                                           half * 16 + fc + 1]),
                         r=[(ps, ALL), (b1cols, ALL)], w=[(h_sb, fc)])
                if tt4 == 0:
                    wq = [p4_w2_load(e_i, half, 0), p4_w2_load(e_i, half, 1)]
                else:
                    wq = [w2_sb[0], w2_sb[1]]
                for ts_ in range(4):
                    yp = y_ps if ts_ % 2 == 0 else y_ps2
                    ti = tt4 * 4 + ts_
                    for j in range(2):
                        for fc in range(16):
                            wb2 = wq[fc // 8]
                            fc8 = fc % 8
                            b.op("pe", lambda e, yp=yp, j=j, fc=fc, fc8=fc8,
                                 ts_=ts_, half=half, wb2=wb2: e.matmul(
                                     yp.h[:, j * 512:(j + 1) * 512],
                                     h_sb.h[:, fc * 512 + ts_ * 128:
                                            fc * 512 + (ts_ + 1) * 128],
                                     wb2.h[:, fc8 * D + j * 512:
                                           fc8 * D + (j + 1) * 512],
                                     start=(fc == 0),
                                     stop=(half == 0 and fc == 15)),
                                 r=[(h_sb, fc), (wb2, fc8)],
                                 w=[(yp, j)])
                        if half == 1:
                            b.op("pe", lambda e, yp=yp, j=j: e.matmul(
                                yp.h[:, j * 512:(j + 1) * 512],
                                c_oner.h[0:1, :],
                                b2row.h[0:1, j * 512:(j + 1) * 512],
                                start=False, stop=True),
                                r=[(c_oner, ALL), (b2row, ALL)],
                                w=[(yp, j)])
                    p4_evac(e_i, half, yp, ti)

    if stage < 5:
        _tail(b, nc, out, [d_g32])
        return

    b.barrier()
    cmat_t = b.sb("cmat_t", [128, NST], F32)
    ct_sb = b.sb("ct_sb", [NST, 16 * 128], BF16)
    ps_ct = b.ps("ps_ct", [128, 512], F32)
    for dc in range(16):
        b.dma("sp", lambda e, dc=dc: e.dma_start(
            out=cmat_t.h[:], in_=w_cmat.h[dc * 128:(dc + 1) * 128, :]),
            w=[(cmat_t, ALL)])
        b.op("pe", lambda e, dc=dc: e.transpose(
            ps_ct.h[0:NST, 0:128], cmat_t.h[:], ident_f.h[:]),
            r=[(cmat_t, ALL), (ident_f, ALL)], w=[(ps_ct, ALL)])
        b.op("dve", lambda e, dc=dc: e.tensor_copy(
            out=ct_sb.h[:, dc * 128:(dc + 1) * 128],
            in_=ps_ct.h[0:NST, 0:128]),
            r=[(ps_ct, ALL)], w=[(ct_sb, dc)])
    wout_sb = b.sb("wout_sb", [128, 16 * D], BF16)
    for dc in range(16):
        b.dma("pool", lambda e, dc=dc: e.dma_start(
            out=wout_sb.h[:, dc * D:(dc + 1) * D],
            in_=w_wout.h[dc * 128:(dc + 1) * 128, :]), w=[(wout_sb, dc)])

    y2_ps = b.ps("y2_ps", [128, 512], F32)
    y2_ps2 = b.ps("y2_ps2", [128, 512], F32)
    xsc_r_r = [b.sb(f"xsc_r{i}", [128, 512], F32) for i in range(2)]
    gs_r_r = [b.sb(f"gs_r{i}", [128, 512], F32) for i in range(2)]
    yg_sb = b.sb("yg_sb", [128, 16 * 512], BF16)
    so_ps = b.ps("so_ps", [128, 1024], F32)
    so_ps2 = b.ps("so_ps2", [128, 1024], F32)
    s_t_r = [b.sb(f"s_t{i}", [128, D], F32) for i in range(2)]
    gf_t_r = [b.sb(f"gf_t{i}", [128, D], F32) for i in range(2)]
    xo_t_r = [b.sb(f"xo_t{i}", [128, D], F32) for i in range(2)]
    o_t_r = [b.sb(f"o_t{i}", [128, D], F32) for i in range(2)]

    def p5_y(tt4, dc):
        yps = y2_ps if dc % 2 == 0 else y2_ps2
        xsc_r, gs_r = xsc_r_r[dc % 2], gs_r_r[dc % 2]
        b.op("pe", lambda e, yps=yps, tt4=tt4, dc=dc: e.matmul(
            yps.h[:], ct_sb.h[:, dc * 128:(dc + 1) * 128],
            hTb.h[:, tt4 * 512:(tt4 + 1) * 512], start=True, stop=True),
            r=[(ct_sb, dc), (hTb, ALL)], w=[(yps, ALL)])
        b.dma("sp", lambda e, tt4=tt4, dc=dc, xsc_r=xsc_r: e.dma_start(
            out=xsc_r.h[:], in_=d_xsc32.h[dc * 128:(dc + 1) * 128,
                                          tt4 * 512:(tt4 + 1) * 512]),
            r=[(d_xsc32, (dc, tt4 + 4))], w=[(xsc_r, ALL)])
        b.dma("sp", lambda e, tt4=tt4, dc=dc, gs_r=gs_r: e.dma_start(
            out=gs_r.h[:], in_=d_gs32.h[dc * 128:(dc + 1) * 128,
                                        tt4 * 512:(tt4 + 1) * 512]),
            r=[(d_gs32, (dc, tt4 + 4))], w=[(gs_r, ALL)])
        b.op("dve", lambda e, yps=yps, dc=dc, xsc_r=xsc_r:
             e.scalar_tensor_tensor(
                 out=xsc_r.h[:], in0=xsc_r.h[:], scalar=dv_c.h[:, dc:dc + 1],
                 in1=yps.h[:], op0=ALU.mult, op1=ALU.add),
             r=[(xsc_r, ALL), (dv_c, ALL), (yps, ALL)], w=[(xsc_r, ALL)])
        b.op("dve", lambda e, dc=dc, xsc_r=xsc_r, gs_r=gs_r: e.tensor_tensor(
            out=yg_sb.h[:, dc * 512:(dc + 1) * 512], in0=xsc_r.h[:],
            in1=gs_r.h[:], op=ALU.mult),
            r=[(xsc_r, ALL), (gs_r, ALL)], w=[(yg_sb, dc)])

    def p5_out(tt4, ts_):
        sp_ = so_ps if ts_ % 2 == 0 else so_ps2
        ti = tt4 * 4 + ts_
        s_t, gf_t = s_t_r[ts_ % 2], gf_t_r[ts_ % 2]
        xo_t, o_t = xo_t_r[ts_ % 2], o_t_r[ts_ % 2]
        xr = xn_r_r[ts_ % 2]
        for j in range(2):
            for dc in range(16):
                b.op("pe", lambda e, sp_=sp_, ts_=ts_, j=j, dc=dc:
                     e.matmul(
                         sp_.h[:, j * 512:(j + 1) * 512],
                         yg_sb.h[:, dc * 512 + ts_ * 128:
                                 dc * 512 + (ts_ + 1) * 128],
                         wout_sb.h[:, dc * D + j * 512:
                                   dc * D + (j + 1) * 512],
                         start=(dc == 0), stop=(dc == 15)),
                     r=[(yg_sb, dc), (wout_sb, dc)], w=[(sp_, j)])
        b.dma("sp", lambda e, ti=ti, xr=xr: e.dma_start(
            out=xr.h[:], in_=d_xn32.h[ti * 128:(ti + 1) * 128, :]),
            r=[(d_xn32, ti)], w=[(xr, ALL)])
        b.op("dve", lambda e, sp_=sp_, s_t=s_t, xr=xr: e.tensor_tensor(
            out=s_t.h[:], in0=sp_.h[:], in1=xr.h[:], op=ALU.add),
            r=[(sp_, 0), (sp_, 1), (xr, ALL)], w=[(s_t, ALL)])
        b.dma("sp", lambda e, ti=ti, gf_t=gf_t: e.dma_start(
            out=gf_t.h[:], in_=d_g32.h[ti * 128:(ti + 1) * 128, :]),
            r=[(d_g32, ti)], w=[(gf_t, ALL)])
        b.dma("sp", lambda e, ti=ti, xo_t=xo_t: e.dma_start(
            out=xo_t.h[:],
            in_=xs_ext.h[PRE + ti * 128:PRE + (ti + 1) * 128, :]),
            w=[(xo_t, ALL)])
        b.op("dve", lambda e, s_t=s_t, gf_t=gf_t: e.tensor_tensor(
            out=s_t.h[:], in0=s_t.h[:], in1=gf_t.h[:], op=ALU.mult),
            r=[(s_t, ALL), (gf_t, ALL)], w=[(s_t, ALL)])
        b.op("dve", lambda e, s_t=s_t, xo_t=xo_t, o_t=o_t: e.tensor_tensor(
            out=o_t.h[:], in0=s_t.h[:], in1=xo_t.h[:], op=ALU.add),
            r=[(s_t, ALL), (xo_t, ALL)], w=[(o_t, ALL)])
        b.dma("pool", lambda e, ti=ti, o_t=o_t: e.dma_start(
            out=out.h[ti * 128:(ti + 1) * 128, :], in_=o_t.h[:]),
            r=[(o_t, ALL)], w=[(out, ti)])

    for tt4 in range(4):
        for dc in range(16):
            p5_y(tt4, dc)
        for ts_ in range(4):
            p5_out(tt4, ts_)

    _tail(b, nc, out, [])


def _tail(b, nc, out, extra):
    b.finalize()
    if out.sem is not None:
        nc.sync.wait_ge(out.sem, 16 * out.sem_cnt)
    for buf in extra:
        if buf.sem is not None:
            nc.sync.wait_ge(buf.sem, 16 * buf.sem_cnt)
    nc.all_engine_barrier()
    for s in b.all_sems:
        nc.gpsimd.sem_clear(s)
    nc.all_engine_barrier()
    b.phase.close()



_NC_CACHE = {}


def _get_nc(stage=STAGE):
    if stage not in _NC_CACHE:
        _NC_CACHE[stage] = build_nc(stage)
    return _NC_CACHE[stage]


def make_in_maps(inp):
    import ml_dtypes
    x = np.ascontiguousarray(np.asarray(inp["L0_attn_x"], dtype=np.float32))
    f32 = lambda v: np.ascontiguousarray(np.asarray(v, dtype=np.float32))
    common = dict(
        identf=np.eye(128, dtype=np.float32),
        identb=np.eye(128, dtype=np.float32).astype(ml_dtypes.bfloat16),
        w_lng=f32(inp["L0_attn_ln_gamma"]), w_lnb=f32(inp["L0_attn_ln_beta"]),
        w_in=f32(inp["L0_attn_W_in"]),
        w_convw=f32(np.asarray(inp["L0_attn_conv_w"]).reshape(DI, KCONV).T),
        w_convb=f32(inp["L0_attn_conv_b"]),
        w_alog=f32(np.asarray(inp["L0_attn_A_log"]).reshape(1, NST)),
        w_bmat=f32(inp["L0_attn_B_mat"]), w_cmat=f32(inp["L0_attn_C_mat"]),
        w_dvec=f32(inp["L0_attn_D_vec"]), w_wd=f32(inp["L0_attn_Wd"]),
        w_bd=f32(inp["L0_attn_bd"]), w_wout=f32(inp["L0_attn_W_out"]),
        w_wq=f32(inp["L0_ffn_Wq"]), w_bq=f32(inp["L0_ffn_bq"]),
        w_wk=f32(inp["L0_ffn_Wk"]), w_bk=f32(inp["L0_ffn_bk"]),
        w_wv=f32(inp["L0_ffn_Wv"]), w_bv=f32(inp["L0_ffn_bv"]),
        w_wo=f32(inp["L0_ffn_Wo"]),
        w_bo=f32(np.asarray(inp["L0_ffn_bo"]).reshape(1, D)),
        w_rw=f32(np.asarray(inp["L0_ffn_router_w"]).reshape(D)),
        w_rb=f32(np.asarray(inp["L0_ffn_router_b"]).reshape(1, 1)),
        w_w1=f32(inp["L0_ffn_W1"]), w_b1=f32(inp["L0_ffn_b1"]),
        w_w2=f32(inp["L0_ffn_W2"]), w_b2=f32(inp["L0_ffn_b2"]),
        w_ln1g=f32(inp["L0_ffn_ln1_g"]), w_ln1b=f32(inp["L0_ffn_ln1_b"]),
        w_ln2g=f32(inp["L0_ffn_ln2_g"]), w_ln2b=f32(inp["L0_ffn_ln2_b"]),
    )
    in_maps = []
    for c in range(8):
        bi, p = c // 2, c % 2
        own = x[bi, p * OWN:(p + 1) * OWN]
        oth = x[bi, (1 - p) * OWN:(2 - p) * OWN]
        pre = x[bi, 0:PRE] if p == 1 else np.zeros((PRE, D), np.float32)
        m = np.float32(p)
        in_maps.append(dict(
            common,
            xs_ext=np.ascontiguousarray(np.concatenate([pre, own], 0)),
            xkv=np.ascontiguousarray(oth),
            mpref16=np.full((16, 1), m, np.float32),
            mhalo=np.full((128, 1), m, np.float32)))
    return in_maps


def kernel(**inp):
    nc = _get_nc(STAGE)
    in_maps = make_in_maps(inp)
    res = run_bass_kernel_spmd(nc, in_maps, core_ids=list(range(8)))
    outp = np.empty((B, T, D), np.float32)
    for c in range(8):
        bi, p = c // 2, c % 2
        outp[bi, p * OWN:(p + 1) * OWN] = res.results[c]["out"]
    return outp


# version 10
# speedup vs baseline: 1.0963x; 1.0440x over previous
import numpy as np
from contextlib import ExitStack

import concourse.bass as bass
import concourse.mybir as mybir
from concourse.bass_utils import run_bass_kernel_spmd

F32 = mybir.dt.float32
BF16 = mybir.dt.bfloat16
I32 = mybir.dt.int32
AF = mybir.ActivationFunctionType
ALU = mybir.AluOpType
AX = mybir.AxisListType.X

B, T, D = 4, 4096, 1024
DI, NST, KCONV = 2048, 16, 4
DFF, NE, NH, HD = 4096, 4, 16, 64
EXT, PRE, OWN, NKV = 4096, 2048, 2048, 4096
EPS = 1e-5
ALL = None

DEBUG = False
STAGE = 5


class Buf:
    __slots__ = ("name", "h", "space", "sem", "sem_cnt", "writers", "readers")

    def __init__(self, name, h, space):
        self.name, self.h, self.space = name, h, space
        self.sem = None
        self.sem_cnt = 0
        self.writers = []
        self.readers = []


def _conf(k1, k2):
    return k1 is ALL or k2 is ALL or k1 == k2


class Op:
    __slots__ = ("idx", "eng", "fn", "reads", "writes", "is_dma", "sem",
                 "sem_target", "inc", "inc_seq", "waits")

    def __init__(self, idx, eng, fn, reads, writes, is_dma):
        self.idx, self.eng, self.fn = idx, eng, fn
        self.reads, self.writes, self.is_dma = list(reads), list(writes), is_dma
        self.sem = None
        self.sem_target = 0
        self.inc = False
        self.inc_seq = 0
        self.waits = []


class Bld:

    ENGS = ("pe", "dve", "act", "pool", "sp")

    def __init__(self, nc, stack):
        self.nc = nc
        self.stack = stack
        self.phase = ExitStack()
        self.ops = []
        self.e = {"pe": nc.tensor, "dve": nc.vector, "act": nc.scalar,
                  "pool": nc.gpsimd, "sp": nc.sync}
        self.all_sems = []
        self.engsem = {g: self._sem(f"es_{g}") for g in self.ENGS}

    def _sem(self, name):
        s = self.stack.enter_context(self.nc.semaphore(name))
        self.all_sems.append(s)
        return s

    def sb(self, name, shape, dtype, pers=False):
        st = self.stack if pers else self.phase
        h = st.enter_context(self.nc.sbuf_tensor(name, shape, dtype))
        return Buf(name, h, "sb")

    def ps(self, name, shape, dtype=F32):
        h = self.phase.enter_context(self.nc.psum_tensor(name, shape, dtype))
        return Buf(name, h, "ps")

    def dr(self, name, shape, dtype, kind="Internal"):
        if DEBUG and kind == "Internal":
            kind = "ExternalOutput"
        return Buf(name, self.nc.dram_tensor(name, shape, dtype, kind=kind),
                   "dr")

    def op(self, eng, fn, r=(), w=()):
        o = Op(len(self.ops), eng, fn, r, w, False)
        self.ops.append(o)
        return o

    def wave(self, lists):
        n = max(len(li) for li in lists) if lists else 0
        for step in range(n):
            for li in lists:
                if step < len(li):
                    kind, eng, fn, r, w = li[step]
                    if kind == "dma":
                        self.dma(eng, fn, r=r, w=w)
                    else:
                        self.op(eng, fn, r=r, w=w)

    def dma(self, q, fn, r=(), w=()):
        w = [(bf, ALL) for (bf, _k) in w]
        o = Op(len(self.ops), q, fn, r, w, True)
        tgt = o.writes[0][0]
        if tgt.sem is None:
            tgt.sem = self._sem(f"dm{len(self.all_sems)}")
        tgt.sem_cnt += 1
        o.sem = tgt.sem
        o.sem_target = 16 * tgt.sem_cnt
        self.ops.append(o)
        return o

    def barrier(self):
        self.ops.append("BARRIER")
        self.phase.close()
        self.phase = ExitStack()
        self._bc_ps = None
        self._bc_row = None

    def finalize(self):
        pairs = []
        allbufs = {}
        bar_dma = []
        dma_issued = {}

        def note(bf):
            if id(bf) not in allbufs:
                allbufs[id(bf)] = bf
                bf.writers, bf.readers = [], []

        for item in self.ops:
            if item == "BARRIER":
                bar_dma.append(list(dma_issued.values()))
                for bf in allbufs.values():
                    bf.writers, bf.readers = [], []
                continue
            o = item
            for (bf, k) in o.reads:
                note(bf)
                for (pi, pk) in bf.writers:
                    if _conf(k, pk):
                        pairs.append((pi, o.idx))
                bf.readers.append((o.idx, k))
            for (bf, k) in o.writes:
                note(bf)
                for (pi, pk) in bf.writers:
                    if _conf(k, pk):
                        pairs.append((pi, o.idx))
                for (pi, pk) in bf.readers:
                    if _conf(k, pk) and pi != o.idx:
                        pairs.append((pi, o.idx))
                bf.writers = [(x, kk) for (x, kk) in bf.writers
                              if not _conf(k, kk)]
                bf.readers = [(x, kk) for (x, kk) in bf.readers
                              if not _conf(k, kk)]
                bf.writers.append((o.idx, k))
            if o.is_dma:
                dma_issued[o.sem.name] = (o.sem, o.sem_target)

        idx2op = {o.idx: o for o in self.ops if o != "BARRIER"}
        for (pi, ci) in pairs:
            p = idx2op[pi]
            if not p.is_dma and not (p.eng == "pe" and idx2op[ci].eng == "pe"):
                p.inc = True
        seg_last = {}
        for o in self.ops:
            if o == "BARRIER":
                for g, lo in seg_last.items():
                    lo.inc = True
                seg_last = {}
                continue
            if not o.is_dma:
                seg_last[o.eng] = o
        cnt = {g: 0 for g in self.ENGS}
        bar_snap = []
        for o in self.ops:
            if o == "BARRIER":
                bar_snap.append(dict(cnt))
                continue
            if o.is_dma:
                continue
            if o.inc:
                cnt[o.eng] += 1
                o.inc_seq = cnt[o.eng]
        for (pi, ci) in pairs:
            p, c = idx2op[pi], idx2op[ci]
            if p.is_dma:
                c.waits.append((p.sem, p.sem_target))
            elif not (p.eng == "pe" and c.eng == "pe"):
                c.waits.append((self.engsem[p.eng], p.inc_seq))

        floors = {g: {} for g in self.ENGS}
        bar_i = 0
        for item in self.ops:
            if item == "BARRIER":
                snap = bar_snap[bar_i]
                dsems = bar_dma[bar_i]
                bar_i += 1
                for g in self.ENGS:
                    fl = floors[g]
                    for g2 in self.ENGS:
                        nm = self.engsem[g2].name
                        if g2 != g and snap[g2] > 0 and \
                                fl.get(nm, -1) < snap[g2]:
                            self.e[g].wait_ge(self.engsem[g2], snap[g2])
                            fl[nm] = snap[g2]
                    for (sem, v) in dsems:
                        if fl.get(sem.name, -1) < v:
                            self.e[g].wait_ge(sem, v)
                            fl[sem.name] = v
                continue
            o = item
            fl = floors[o.eng]
            best = {}
            for (sem, val) in o.waits:
                if val > best.get(sem.name, (None, -1))[1]:
                    best[sem.name] = (sem, val)
            for kname, (sem, val) in best.items():
                if fl.get(kname, -1) < val:
                    self.e[o.eng].wait_ge(sem, val)
                    fl[kname] = val
            inst = o.fn(self.e[o.eng])
            if o.is_dma:
                inst.then_inc(o.sem, 16)
            elif o.inc:
                inst.then_inc(self.engsem[o.eng], 1)



def build_nc(stage=STAGE):
    nc = bass.Bass("TRN2", target_bir_lowering=False)
    stack = ExitStack()
    with stack:
        _body(nc, stack, stage)
    return nc


def _body(nc, stack, stage):
    b = Bld(nc, stack)

    def din(name, shape, dtype=F32):
        return Buf(name, nc.dram_tensor(name, shape, dtype,
                                        kind="ExternalInput"), "dr")

    xs_ext = din("xs_ext", [EXT, D])
    xkv = din("xkv", [OWN, D])
    mpref16 = din("mpref16", [16, 1])
    mhalo = din("mhalo", [128, 1])
    identf = din("identf", [128, 128])
    identb = din("identb", [128, 128], BF16)
    w_lng = din("w_lng", [D]);        w_lnb = din("w_lnb", [D])
    w_in = din("w_in", [D, 2 * DI])
    w_convw = din("w_convw", [KCONV, DI])
    w_convb = din("w_convb", [DI])
    w_alog = din("w_alog", [1, NST])
    w_bmat = din("w_bmat", [NST, DI])
    w_cmat = din("w_cmat", [DI, NST])
    w_dvec = din("w_dvec", [DI])
    w_wd = din("w_wd", [DI, DI]);     w_bd = din("w_bd", [DI])
    w_wout = din("w_wout", [DI, D])
    w_wq = din("w_wq", [D, D]);       w_bq = din("w_bq", [D])
    w_wk = din("w_wk", [D, D]);       w_bk = din("w_bk", [D])
    w_wv = din("w_wv", [D, D]);       w_bv = din("w_bv", [D])
    w_wo = din("w_wo", [D, D]);       w_bo = din("w_bo", [1, D])
    w_rw = din("w_rw", [D]);          w_rb = din("w_rb", [1, 1])
    w_w1 = din("w_w1", [NE, D, DFF]); w_b1 = din("w_b1", [NE, DFF])
    w_w2 = din("w_w2", [NE, DFF, D]); w_b2 = din("w_b2", [NE, D])
    w_ln1g = din("w_ln1g", [D]);      w_ln1b = din("w_ln1b", [D])
    w_ln2g = din("w_ln2g", [D]);      w_ln2b = din("w_ln2b", [D])
    out = Buf("out", nc.dram_tensor("out", [OWN, D], F32,
                                    kind="ExternalOutput"), "dr")

    d_xnbf = b.dr("d_xnbf", [NKV, D], BF16)
    d_xn32 = b.dr("d_xn32", [OWN, D], F32)
    d_xhbf = b.dr("d_xhbf", [EXT, D], BF16)
    d_KT = b.dr("d_KT", [D, NKV], BF16)
    d_VT = b.dr("d_VT", [D, NKV], BF16)
    d_QT = b.dr("d_QT", [D, OWN], BF16)
    d_xscbf = b.dr("d_xscbf", [DI, EXT], BF16)
    d_xsc32 = b.dr("d_xsc32", [DI, OWN], F32)
    d_gs32 = b.dr("d_gs32", [DI, OWN], F32)
    d_x1bf = b.dr("d_x1bf", [OWN, D], BF16)
    d_x132 = b.dr("d_x132", [OWN, D], F32)
    d_g32 = b.dr("d_g32", [OWN, D], F32)

    P = dict(pers=True)
    ident_f = b.sb("ident_f", [128, 128], F32, **P)
    ident_b = b.sb("ident_b", [128, 128], BF16, **P)
    c_eps = b.sb("c_eps", [128, 1], F32, **P)
    c_nhalf = b.sb("c_nhalf", [128, 1], F32, **P)
    c_one = b.sb("c_one", [128, 1], F32, **P)
    c_oner = b.sb("c_oner", [1, 128], F32, **P)
    c_oneb = b.sb("c_oneb", [128, 1], BF16, **P)
    c_eighth = b.sb("c_eighth", [128, 1], F32, **P)
    c_i2048 = b.sb("c_i2048", [128, 1], F32, **P)
    hTb = b.sb("hTb", [NST, OWN], BF16, **P)
    idx_cols = b.sb("idx_cols", [128, 16], F32, **P)
    s_mpref = b.sb("s_mpref", [16, 1], F32, **P)
    s_mhalo = b.sb("s_mhalo", [128, 1], F32, **P)
    a_col = b.sb("a_col", [16, 1], F32, **P)
    NLN = 8
    ln_sets = []
    for si in range(NLN):
        ln_sets.append({n: b.sb(f"ln_{n}{si}", [128, 1], F32, **P)
                        for n in ("sum", "ssq", "mu", "e2", "ms", "var", "rs")})
    bq_c = b.sb("bq_c", [128, 8], F32, **P)
    bk_c = b.sb("bk_c", [128, 8], F32, **P)
    bv_c = b.sb("bv_c", [128, 8], F32, **P)
    bd_c = b.sb("bd_c", [128, 16], F32, **P)
    cb_c = b.sb("cb_c", [128, 16], F32, **P)
    dv_c = b.sb("dv_c", [128, 16], F32, **P)
    cw_c = [b.sb(f"cw{k}_c", [128, 16], F32, **P) for k in range(KCONV)]
    rb_c = b.sb("rb_c", [128, 1], F32, **P)
    xn_r_r = [b.sb(f"xn_rr{i}", [128, D], F32, **P) for i in range(2)]

    _ln_i = [0]

    def ln_tile_ops(x_t, out_t, width=D):
        s = ln_sets[_ln_i[0] % NLN]
        _ln_i[0] += 1
        L = {n: (bb, ALL) for n, bb in s.items()}
        o = []
        o.append(("op", "dve", lambda e, s=s, x_t=x_t: e.reduce_sum(
            out=s["sum"].h[:], in_=x_t.h[:], axis=AX),
            [(x_t, ALL)], [L["sum"]]))
        o.append(("op", "act", lambda e, s=s, x_t=x_t, out_t=out_t,
                  width=width: e.activation(
                      out=out_t.h[:, 0:width], in_=x_t.h[:], func=AF.Square,
                      accum_out=s["ssq"].h[:]),
                  [(x_t, ALL)], [(out_t, ALL), L["ssq"]]))
        o.append(("op", "dve", lambda e, s=s, width=width: e.tensor_scalar(
            out=s["mu"].h[:], in0=s["sum"].h[:], scalar1=1.0 / width,
            scalar2=None, op0=ALU.mult), [L["sum"]], [L["mu"]]))
        o.append(("op", "dve", lambda e, s=s, width=width: e.tensor_scalar(
            out=s["e2"].h[:], in0=s["ssq"].h[:], scalar1=1.0 / width,
            scalar2=None, op0=ALU.mult), [L["ssq"]], [L["e2"]]))
        o.append(("op", "dve", lambda e, s=s: e.tensor_tensor(
            out=s["ms"].h[:], in0=s["mu"].h[:], in1=s["mu"].h[:],
            op=ALU.mult), [L["mu"]], [L["ms"]]))
        o.append(("op", "dve", lambda e, s=s: e.tensor_sub(
            out=s["var"].h[:], in0=s["e2"].h[:], in1=s["ms"].h[:]),
            [L["e2"], L["ms"]], [L["var"]]))
        o.append(("op", "act", lambda e, s=s: e.activation(
            out=s["rs"].h[:], in_=s["var"].h[:], func=AF.Ln,
            bias=c_eps.h[:]), [L["var"], (c_eps, ALL)], [L["rs"]]))
        o.append(("op", "act", lambda e, s=s: e.activation(
            out=s["rs"].h[:], in_=s["rs"].h[:], func=AF.Exp,
            scale=c_nhalf.h[:]), [L["rs"], (c_nhalf, ALL)], [L["rs"]]))
        o.append(("op", "dve", lambda e, s=s, x_t=x_t, out_t=out_t:
                  e.tensor_scalar(
                      out=out_t.h[:], in0=x_t.h[:], scalar1=s["mu"].h[:],
                      scalar2=s["rs"].h[:], op0=ALU.subtract, op1=ALU.mult),
                  [(x_t, ALL), L["mu"], L["rs"]], [(out_t, ALL)]))
        return o

    def ln_tile(x_t, out_t, width=D):
        for (kind, eng, fn, r, w) in ln_tile_ops(x_t, out_t, width):
            b.op(eng, fn, r=r, w=w)
        return

    def _ln_tile_body(x_t, out_t, width=D):
        s = ln_sets[_ln_i[0] % NLN]
        _ln_i[0] += 1
        L = {n: (bb, ALL) for n, bb in s.items()}
        b.op("dve", lambda e: e.reduce_sum(out=s["sum"].h[:], in_=x_t.h[:],
                                           axis=AX),
             r=[(x_t, ALL)], w=[L["sum"]])
        b.op("act", lambda e: e.activation(out=out_t.h[:, 0:width],
                                           in_=x_t.h[:], func=AF.Square,
                                           accum_out=s["ssq"].h[:]),
             r=[(x_t, ALL)], w=[(out_t, ALL), L["ssq"]])
        b.op("dve", lambda e: e.tensor_scalar(
            out=s["mu"].h[:], in0=s["sum"].h[:], scalar1=1.0 / width,
            scalar2=None, op0=ALU.mult),
            r=[L["sum"]], w=[L["mu"]])
        b.op("dve", lambda e: e.tensor_scalar(
            out=s["e2"].h[:], in0=s["ssq"].h[:], scalar1=1.0 / width,
            scalar2=None, op0=ALU.mult),
            r=[L["ssq"]], w=[L["e2"]])
        b.op("dve", lambda e: e.tensor_tensor(
            out=s["ms"].h[:], in0=s["mu"].h[:], in1=s["mu"].h[:],
            op=ALU.mult),
            r=[L["mu"]], w=[L["ms"]])
        b.op("dve", lambda e: e.tensor_sub(out=s["var"].h[:],
                                           in0=s["e2"].h[:],
                                           in1=s["ms"].h[:]),
             r=[L["e2"], L["ms"]], w=[L["var"]])
        b.op("act", lambda e: e.activation(out=s["rs"].h[:], in_=s["var"].h[:],
                                           func=AF.Ln, bias=c_eps.h[:]),
             r=[L["var"], (c_eps, ALL)], w=[L["rs"]])
        b.op("act", lambda e: e.activation(out=s["rs"].h[:], in_=s["rs"].h[:],
                                           func=AF.Exp, scale=c_nhalf.h[:]),
             r=[L["rs"], (c_nhalf, ALL)], w=[L["rs"]])
        b.op("dve", lambda e: e.tensor_scalar(
            out=out_t.h[:], in0=x_t.h[:], scalar1=s["mu"].h[:],
            scalar2=s["rs"].h[:], op0=ALU.subtract, op1=ALU.mult),
            r=[(x_t, ALL), L["mu"], L["rs"]], w=[(out_t, ALL)])

    def bcast_row(src, dst, cols=D):
        if getattr(b, "_bc_row", None) is None:
            b._bc_row = b.sb(f"bcr_{dst.name}", [1, D], F32)
        row = b._bc_row
        if getattr(b, "_bc_ps", None) is None:
            b._bc_ps = b.ps(f"bcp_{dst.name}", [128, 512], F32)
        psb = b._bc_ps
        b.dma("sp", lambda e: e.dma_start(
            out=row.h[:], in_=src.h.ap().rearrange("(a d) -> a d", a=1)),
            r=[], w=[(row, ALL)])
        for j0 in range(0, cols, 512):
            n = min(512, cols - j0)
            b.op("pe", lambda e, j0=j0, n=n: e.matmul(
                psb.h[:, 0:n], c_oner.h[0:1, :], row.h[0:1, j0:j0 + n],
                start=True, stop=True),
                r=[(c_oner, ALL), (row, ALL)], w=[(psb, ALL)])
            b.op("dve", lambda e, j0=j0, n=n: e.tensor_copy(
                out=dst.h[:, j0:j0 + n], in_=psb.h[:, 0:n]),
                r=[(psb, ALL)], w=[(dst, j0)])

    ps_set = b.ps("ps_set", [128, 512], F32)

    def load_cols(dst, ap2d, nch):
        row = b.sb(f"lc_{dst.name}", [nch, 128], F32)
        b.dma("sp", lambda e: e.dma_start(out=row.h[:], in_=ap2d),
              r=[], w=[(row, ALL)])
        b.op("pe", lambda e: e.transpose(ps_set.h[:, 0:nch], row.h[:],
                                         ident_f.h[0:nch, 0:nch]),
             r=[(row, ALL), (ident_f, ALL)], w=[(ps_set, ALL)])
        b.op("dve", lambda e: e.tensor_copy(out=dst.h[:], in_=ps_set.h[:, 0:nch]),
             r=[(ps_set, ALL)], w=[(dst, ALL)])

    b.dma("sp", lambda e: e.dma_start(out=ident_f.h[:], in_=identf.h.ap()),
          w=[(ident_f, ALL)])
    b.dma("sp", lambda e: e.dma_start(out=ident_b.h[:], in_=identb.h.ap()),
          w=[(ident_b, ALL)])
    b.dma("sp", lambda e: e.dma_start(out=s_mpref.h[:], in_=mpref16.h.ap()),
          w=[(s_mpref, ALL)])
    b.dma("sp", lambda e: e.dma_start(out=s_mhalo.h[:], in_=mhalo.h.ap()),
          w=[(s_mhalo, ALL)])
    for buf, val in ((c_eps, EPS), (c_nhalf, -0.5), (c_one, 1.0),
                     (c_eighth, 0.125), (c_i2048, 1.0 / 2048)):
        b.op("dve", lambda e, buf=buf, val=val: e.memset(buf.h[:], val),
             w=[(buf, ALL)])
    b.op("dve", lambda e: e.memset(c_oner.h[:], 1.0), w=[(c_oner, ALL)])
    b.op("dve", lambda e: e.memset(c_oneb.h[:], 1.0), w=[(c_oneb, ALL)])

    load_cols(bq_c, w_bq.h.ap().rearrange("(c p) -> c p", p=128), 8)
    load_cols(bk_c, w_bk.h.ap().rearrange("(c p) -> c p", p=128), 8)
    load_cols(bv_c, w_bv.h.ap().rearrange("(c p) -> c p", p=128), 8)
    load_cols(bd_c, w_bd.h.ap().rearrange("(c p) -> c p", p=128), 16)
    load_cols(cb_c, w_convb.h.ap().rearrange("(c p) -> c p", p=128), 16)
    load_cols(dv_c, w_dvec.h.ap().rearrange("(c p) -> c p", p=128), 16)
    for k in range(KCONV):
        load_cols(cw_c[k],
                  w_convw.h[k:k + 1, :].rearrange("a (c p) -> (a c) p", p=128),
                  16)
    rb_row = b.sb("rb_row", [1, 1], F32)
    b.dma("sp", lambda e: e.dma_start(out=rb_row.h[:], in_=w_rb.h.ap()),
          w=[(rb_row, ALL)])
    b.op("pe", lambda e: e.matmul(ps_set.h[:, 0:1], c_oner.h[0:1, :],
                                  rb_row.h[:], start=True, stop=True),
         r=[(c_oner, ALL), (rb_row, ALL)], w=[(ps_set, ALL)])
    b.op("dve", lambda e: e.tensor_copy(out=rb_c.h[:], in_=ps_set.h[:, 0:1]),
         r=[(ps_set, ALL)], w=[(rb_c, ALL)])
    alog_r = b.sb("alog_r", [1, NST], F32)
    b.dma("sp", lambda e: e.dma_start(out=alog_r.h[:], in_=w_alog.h.ap()),
          w=[(alog_r, ALL)])
    b.op("pe", lambda e: e.transpose(ps_set.h[0:NST, 0:1], alog_r.h[:],
                                     ident_f.h[0:1, 0:1]),
         r=[(alog_r, ALL), (ident_f, ALL)], w=[(ps_set, ALL)])
    b.op("act", lambda e: e.activation(out=a_col.h[:], in_=ps_set.h[0:NST, 0:1],
                                       func=AF.Exp),
         r=[(ps_set, ALL)], w=[(a_col, ALL)])
    b.op("dve", lambda e: e.tensor_scalar(
        out=a_col.h[:], in0=a_col.h[:], scalar1=-1.0 / 2048, scalar2=None,
        op0=ALU.mult), r=[(a_col, ALL)], w=[(a_col, ALL)])

    wq_sb = b.sb("wq_sb", [128, 8 * D], BF16)
    wk_sb = b.sb("wk_sb", [128, 8 * D], BF16)
    wv_sb = b.sb("wv_sb", [128, 8 * D], BF16)
    for (wsb, wdr) in ((wq_sb, w_wq), (wk_sb, w_wk), (wv_sb, w_wv)):
        for dc in range(8):
            b.dma("pool", lambda e, wsb=wsb, wdr=wdr, dc=dc: e.dma_start(
                out=wsb.h[:, dc * D:(dc + 1) * D],
                in_=wdr.h[dc * 128:(dc + 1) * 128, :]), w=[(wsb, dc)])
    p1_x = [b.sb(f"p1_x{i}", [128, D], F32) for i in range(4)]
    p1_xn = [b.sb(f"p1_xn{i}", [128, D], F32) for i in range(4)]
    xnfm = [b.sb(f"xnfm{i}", [128, 8 * 512], BF16) for i in range(2)]
    kvq_ps = b.ps("kvq_ps", [128, 512], F32)
    kvq_ps2 = b.ps("kvq_ps2", [128, 512], F32)
    kvq_ev = [b.sb(f"kvq_ev{i}", [128, 512], BF16) for i in range(2)]

    def p1_ln_ops(i):
        own = i < 16
        src = xs_ext.h if own else xkv.h
        r0 = (PRE + i * 128) if own else (i - 16) * 128
        xt, xnt = p1_x[i % 4], p1_xn[i % 4]
        o = [("dma", "sp", lambda e, src=src, r0=r0, xt=xt: e.dma_start(
            out=xt.h[:], in_=src[r0:r0 + 128, :]), [], [(xt, ALL)])]
        o += ln_tile_ops(xt, xnt)
        if own:
            o.append(("dma", "pool", lambda e, i=i, xnt=xnt: e.dma_start(
                out=d_xn32.h[i * 128:(i + 1) * 128, :], in_=xnt.h[:]),
                [(xnt, ALL)], [(d_xn32, i)]))
        o.append(("dma", "pool", lambda e, i=i, xnt=xnt: e.dma_start(
            out=d_xnbf.h[i * 128:(i + 1) * 128, :], in_=xnt.h[:]),
            [(xnt, ALL)], [(d_xnbf, i)]))
        return o

    b.wave([p1_ln_ops(i) for i in range(4)])
    for tt in range(8):
        xf = xnfm[tt % 2]
        for dc in range(8):
            b.dma("act", lambda e, tt=tt, dc=dc, xf=xf: e.dma_start(
                out=xf.h[:, dc * 512:(dc + 1) * 512],
                in_=d_xnbf.h[tt * 512:(tt + 1) * 512,
                             dc * 128:(dc + 1) * 128], transpose=True),
                r=[(d_xnbf, ALL)], w=[(xf, dc)])
        grp = [p1_ln_ops(i) for i in range(4 * tt + 4, min(4 * tt + 8, 32))]
        if grp:
            b.wave(grp)
        plan = [(wk_sb, bk_c, d_KT), (wv_sb, bv_c, d_VT)]
        if tt < 4:
            plan.append((wq_sb, bq_c, d_QT))
        for (wsb, bcols, ddst) in plan:
            for oc in range(8):
                ps = kvq_ps if oc % 2 == 0 else kvq_ps2
                ev = kvq_ev[oc % 2]
                for dc in range(8):
                    b.op("pe", lambda e, ps=ps, wsb=wsb, oc=oc, dc=dc, xf=xf:
                         e.matmul(
                             ps.h[:],
                             wsb.h[:, dc * D + oc * 128:dc * D + (oc + 1) * 128],
                             xf.h[:, dc * 512:(dc + 1) * 512],
                             start=(dc == 0), stop=(dc == 7)),
                         r=[(wsb, dc), (xf, dc)], w=[(ps, ALL)])
                b.op("act", lambda e, ps=ps, bcols=bcols, oc=oc, ev=ev:
                     e.activation(
                         out=ev.h[:], in_=ps.h[:], func=AF.Identity,
                         bias=bcols.h[:, oc:oc + 1]),
                     r=[(ps, ALL), (bcols, ALL)], w=[(ev, ALL)])
                b.dma("sp", lambda e, ddst=ddst, oc=oc, tt=tt, ev=ev:
                      e.dma_start(
                          out=ddst.h[oc * 128:(oc + 1) * 128,
                                     tt * 512:(tt + 1) * 512], in_=ev.h[:]),
                      r=[(ev, ALL)], w=[(ddst, (oc, tt))])

    if stage < 2:
        _tail(b, nc, out, [d_xnbf, d_xn32, d_KT, d_VT, d_QT])
        return

    b.barrier()
    gb_g = b.sb("gb_g", [128, D], F32)
    gb_b = b.sb("gb_b", [128, D], F32)
    bcast_row(w_lng, gb_g)
    bcast_row(w_lnb, gb_b)
    wxs_sb = b.sb("wxs_sb", [128, 8 * DI], BF16)
    wg_sb = b.sb("wg_sb", [128, 8 * DI], BF16)
    for dc in range(8):
        b.dma("pool", lambda e, dc=dc: e.dma_start(
            out=wxs_sb.h[:, dc * DI:(dc + 1) * DI],
            in_=w_in.h[dc * 128:(dc + 1) * 128, 0:DI]), w=[(wxs_sb, dc)])
        b.dma("pool", lambda e, dc=dc: e.dma_start(
            out=wg_sb.h[:, dc * DI:(dc + 1) * DI],
            in_=w_in.h[dc * 128:(dc + 1) * 128, DI:2 * DI]), w=[(wg_sb, dc)])
    p2_x = [b.sb(f"p2_x{i}", [128, D], F32) for i in range(4)]
    p2_xn = [b.sb(f"p2_xn{i}", [128, D], F32) for i in range(4)]
    p2_xh = [b.sb(f"p2_xh{i}", [128, D], F32) for i in range(4)]

    def p2_ln_ops(i):
        xt, xnt, xht = p2_x[i % 4], p2_xn[i % 4], p2_xh[i % 4]
        o = [("dma", "sp", lambda e, i=i, xt=xt: e.dma_start(
            out=xt.h[:], in_=xs_ext.h[i * 128:(i + 1) * 128, :]),
            [], [(xt, ALL)])]
        o += ln_tile_ops(xt, xnt)
        o += ln_tile_ops(xnt, xht)
        o.append(("op", "dve", lambda e, xht=xht: e.tensor_tensor(
            out=xht.h[:], in0=xht.h[:], in1=gb_g.h[:], op=ALU.mult),
            [(xht, ALL), (gb_g, ALL)], [(xht, ALL)]))
        o.append(("op", "pool", lambda e, xht=xht: e.tensor_tensor(
            out=xht.h[:], in0=xht.h[:], in1=gb_b.h[:], op=ALU.add),
            [(xht, ALL), (gb_b, ALL)], [(xht, ALL)]))
        o.append(("dma", "pool", lambda e, i=i, xht=xht: e.dma_start(
            out=d_xhbf.h[i * 128:(i + 1) * 128, :], in_=xht.h[:]),
            [(xht, ALL)], [(d_xhbf, i)]))
        return o

    b.wave([p2_ln_ops(i) for i in range(4)])
    xhfm = b.sb("xhfm", [128, 8 * 512], BF16)
    xsp_r = [b.sb(f"xsp{i}", [128, 515], F32) for i in range(2)]
    xsp_tail = b.sb("xsp_tail", [128, 16 * 3], F32)
    win_ps = b.ps("win_ps", [128, 512], F32)
    win_ps2 = b.ps("win_ps2", [128, 512], F32)
    conv_ps = b.ps("conv_ps", [128, 512], F32)
    conv_ps2 = b.ps("conv_ps2", [128, 512], F32)
    xsc_t_r = [b.sb(f"xsc_t{i}", [128, 512], F32) for i in range(2)]
    gt_t_r = [b.sb(f"gt_t{i}", [128, 512], F32) for i in range(2)]
    cwd = [b.sb(f"cwd{k}", [128, 16 * 128], F32) for k in range(KCONV)]
    for k in range(KCONV):
        for oc in range(16):
            b.op("dve", lambda e, k=k, oc=oc: e.tensor_scalar(
                out=cwd[k].h[:, oc * 128:(oc + 1) * 128], in0=ident_f.h[:],
                scalar1=cw_c[k].h[:, oc:oc + 1], scalar2=None, op0=ALU.mult),
                r=[(ident_f, ALL), (cw_c[k], ALL)], w=[(cwd[k], oc)])
    b.op("dve", lambda e: e.memset(xsp_tail.h[:], 0.0), w=[(xsp_tail, ALL)])
    for tt in range(8):
        for dc in range(8):
            b.dma("act", lambda e, tt=tt, dc=dc: e.dma_start(
                out=xhfm.h[:, dc * 512:(dc + 1) * 512],
                in_=d_xhbf.h[tt * 512:(tt + 1) * 512,
                             dc * 128:(dc + 1) * 128], transpose=True),
                r=[(d_xhbf, ALL)], w=[(xhfm, dc)])
        grp = [p2_ln_ops(i) for i in range(4 * tt + 4, min(4 * tt + 8, 32))]
        if grp:
            b.wave(grp)
        def p2a_A(oc):
            ps = win_ps if oc % 2 == 0 else win_ps2
            for dc in range(8):
                b.op("pe", lambda e, ps=ps, oc=oc, dc=dc: e.matmul(
                    ps.h[:],
                    wxs_sb.h[:, dc * DI + oc * 128:dc * DI + (oc + 1) * 128],
                    xhfm.h[:, dc * 512:(dc + 1) * 512],
                    start=(dc == 0), stop=(dc == 7)),
                    r=[(wxs_sb, dc), (xhfm, dc)], w=[(ps, ALL)])

        def p2a_B(oc, tt):
            ps = win_ps if oc % 2 == 0 else win_ps2
            cps = conv_ps if oc % 2 == 0 else conv_ps2
            xsp = xsp_r[oc % 2]
            b.op("act", lambda e, ps=ps, xsp=xsp: e.activation(
                out=xsp.h[:, 3:515], in_=ps.h[:], func=AF.Identity),
                r=[(ps, ALL)], w=[(xsp, "b")])
            if tt == 3:
                b.op("dve", lambda e, xsp=xsp: e.tensor_scalar(
                    out=xsp.h[:, 512:515], in0=xsp.h[:, 512:515],
                    scalar1=s_mhalo.h[:], scalar2=None, op0=ALU.mult),
                    r=[(xsp, "b"), (s_mhalo, ALL)], w=[(xsp, "b")])
            b.op("dve", lambda e, oc=oc, xsp=xsp: e.tensor_copy(
                out=xsp.h[:, 0:3], in_=xsp_tail.h[:, oc * 3:(oc + 1) * 3]),
                r=[(xsp_tail, oc)], w=[(xsp, "h")])
            b.op("dve", lambda e, oc=oc, xsp=xsp: e.tensor_copy(
                out=xsp_tail.h[:, oc * 3:(oc + 1) * 3],
                in_=xsp.h[:, 512:515]),
                r=[(xsp, "b")], w=[(xsp_tail, oc)])
            xsc_t = xsc_t_r[oc % 2]
            for k in range(KCONV):
                b.op("pe", lambda e, oc=oc, k=k, cps=cps, xsp=xsp: e.matmul(
                    cps.h[:], cwd[k].h[:, oc * 128:(oc + 1) * 128],
                    xsp.h[:, k:k + 512],
                    start=(k == 0), stop=(k == KCONV - 1)),
                    r=[(cwd[k], oc), (xsp, "b"), (xsp, "h")],
                    w=[(cps, ALL)])
            b.op("act", lambda e, oc=oc, cps=cps, xsc_t=xsc_t: e.activation(
                out=xsc_t.h[:], in_=cps.h[:], func=AF.Silu,
                bias=cb_c.h[:, oc:oc + 1]),
                r=[(cps, ALL), (cb_c, ALL)], w=[(xsc_t, ALL)])
            b.dma("pool", lambda e, tt=tt, oc=oc, xsc_t=xsc_t: e.dma_start(
                out=d_xscbf.h[oc * 128:(oc + 1) * 128,
                              tt * 512:(tt + 1) * 512], in_=xsc_t.h[:]),
                r=[(xsc_t, ALL)], w=[(d_xscbf, (oc, tt))])
            if tt >= 4:
                b.dma("sp", lambda e, tt=tt, oc=oc, xsc_t=xsc_t: e.dma_start(
                    out=d_xsc32.h[oc * 128:(oc + 1) * 128,
                                  (tt - 4) * 512:(tt - 3) * 512],
                    in_=xsc_t.h[:]),
                    r=[(xsc_t, ALL)], w=[(d_xsc32, (oc, tt))])

        p2a_A(0)
        for oc in range(16):
            if oc < 15:
                p2a_A(oc + 1)
            p2a_B(oc, tt)
        if tt >= 4:
            for oc in range(16):
                ps = win_ps if oc % 2 == 0 else win_ps2
                for dc in range(8):
                    b.op("pe", lambda e, ps=ps, oc=oc, dc=dc: e.matmul(
                        ps.h[:],
                        wg_sb.h[:, dc * DI + oc * 128:dc * DI + (oc + 1) * 128],
                        xhfm.h[:, dc * 512:(dc + 1) * 512],
                        start=(dc == 0), stop=(dc == 7)),
                        r=[(wg_sb, dc), (xhfm, dc)], w=[(ps, ALL)])
                gt_t = gt_t_r[oc % 2]
                b.op("act", lambda e, ps=ps, gt_t=gt_t: e.activation(
                    out=gt_t.h[:], in_=ps.h[:], func=AF.Silu),
                    r=[(ps, ALL)], w=[(gt_t, ALL)])
                b.dma("sp", lambda e, tt=tt, oc=oc, gt_t=gt_t: e.dma_start(
                    out=d_gs32.h[oc * 128:(oc + 1) * 128,
                                 (tt - 4) * 512:(tt - 3) * 512],
                    in_=gt_t.h[:]),
                    r=[(gt_t, ALL)], w=[(d_gs32, (oc, tt))])

    b.barrier()
    wd_sb = b.sb("wd_sb", [128, 16 * DI], BF16)
    for dc in range(16):
        b.dma("pool", lambda e, dc=dc: e.dma_start(
            out=wd_sb.h[:, dc * DI:(dc + 1) * DI],
            in_=w_wd.h[dc * 128:(dc + 1) * 128, :]), w=[(wd_sb, dc)])
    bmat_sb = b.sb("bmat_sb", [NST, DI], F32)
    bt_sb = b.sb("bt_sb", [128, 16 * NST], BF16)
    ps_bt = b.ps("ps_bt", [128, 512], F32)
    b.dma("sp", lambda e: e.dma_start(out=bmat_sb.h[:], in_=w_bmat.h.ap()),
          w=[(bmat_sb, ALL)])
    for dc in range(16):
        b.op("pe", lambda e, dc=dc: e.transpose(
            ps_bt.h[:, 0:NST], bmat_sb.h[:, dc * 128:(dc + 1) * 128],
            ident_f.h[0:NST, 0:NST]),
            r=[(bmat_sb, ALL), (ident_f, ALL)], w=[(ps_bt, ALL)])
        b.op("act", lambda e, dc=dc: e.activation(
            out=bt_sb.h[:, dc * NST:(dc + 1) * NST], in_=ps_bt.h[:, 0:NST],
            func=AF.Identity, scale=c_i2048.h[:]),
            r=[(ps_bt, ALL), (c_i2048, ALL)], w=[(bt_sb, dc)])

    xscb_r = [b.sb(f"xscb{i}", [128, 16 * 512], BF16) for i in range(2)]
    z_ps = b.ps("z_ps", [128, 512], F32)
    z_ps2 = b.ps("z_ps2", [128, 512], F32)
    ez_t_r = [b.sb(f"ez_t{i}", [128, 512], F32) for i in range(2)]
    spz_t_r = [b.sb(f"spz_t{i}", [128, 512], F32) for i in range(2)]
    del_ps = b.ps("del_ps", [1, 512], F32)
    bx_ps = b.ps("bx_ps", [NST, 512], F32)
    db_ps = b.ps("db_ps", [NST, 512], F32)
    del_row_r = [b.sb(f"del_row{i}", [1, 512], F32) for i in range(2)]
    db_sb_r = [b.sb(f"db_sb{i}", [NST, 512], F32) for i in range(2)]
    dA = b.sb("dA", [NST, EXT], F32)
    BxT = b.sb("BxT", [NST, EXT], F32)
    hT = b.sb("hT", [NST, EXT], F32)

    for tt in range(8):
        xscb = xscb_r[tt % 2]
        for dc in range(16):
            b.dma("sp", lambda e, tt=tt, dc=dc, xscb=xscb: e.dma_start(
                out=xscb.h[:, dc * 512:(dc + 1) * 512],
                in_=d_xscbf.h[dc * 128:(dc + 1) * 128,
                              tt * 512:(tt + 1) * 512]),
                r=[(d_xscbf, (dc, tt))], w=[(xscb, dc)])
        del_row = del_row_r[tt % 2]
        db_sb = db_sb_r[tt % 2]

        def p2b_A(zc):
            ps = z_ps if zc % 2 == 0 else z_ps2
            for dc in range(16):
                b.op("pe", lambda e, ps=ps, zc=zc, dc=dc, xscb=xscb: e.matmul(
                    ps.h[:],
                    wd_sb.h[:, dc * DI + zc * 128:dc * DI + (zc + 1) * 128],
                    xscb.h[:, dc * 512:(dc + 1) * 512],
                    start=(dc == 0), stop=(dc == 15)),
                    r=[(wd_sb, dc), (xscb, dc)], w=[(ps, ALL)])

        def p2b_B(zc):
            ps = z_ps if zc % 2 == 0 else z_ps2
            ez_t, spz_t = ez_t_r[zc % 2], spz_t_r[zc % 2]
            b.op("act", lambda e, ps=ps, zc=zc, ez_t=ez_t: e.activation(
                out=ez_t.h[:], in_=ps.h[:], func=AF.Exp,
                bias=bd_c.h[:, zc:zc + 1]),
                r=[(ps, ALL), (bd_c, ALL)], w=[(ez_t, ALL)])
            b.op("act", lambda e, ez_t=ez_t, spz_t=spz_t: e.activation(
                out=spz_t.h[:], in_=ez_t.h[:], func=AF.Ln, bias=c_one.h[:]),
                r=[(ez_t, ALL), (c_one, ALL)], w=[(spz_t, ALL)])
            b.op("pe", lambda e, zc=zc, spz_t=spz_t: e.matmul(
                del_ps.h[:], c_one.h[:, 0:1], spz_t.h[:],
                start=(zc == 0), stop=(zc == 15)),
                r=[(c_one, ALL), (spz_t, ALL)], w=[(del_ps, ALL)])

        p2b_A(0)
        for zc in range(16):
            if zc < 15:
                p2b_A(zc + 1)
            p2b_B(zc)
        b.op("dve", lambda e, del_row=del_row: e.tensor_copy(
            out=del_row.h[:], in_=del_ps.h[:]),
            r=[(del_ps, ALL)], w=[(del_row, ALL)])
        for dc in range(16):
            b.op("pe", lambda e, dc=dc, xscb=xscb: e.matmul(
                bx_ps.h[:], bt_sb.h[:, dc * NST:(dc + 1) * NST],
                xscb.h[:, dc * 512:(dc + 1) * 512],
                start=(dc == 0), stop=(dc == 15)),
                r=[(bt_sb, dc), (xscb, dc)], w=[(bx_ps, ALL)])
        b.op("pe", lambda e, del_row=del_row: e.matmul(
            db_ps.h[:], c_oner.h[0:1, 0:NST], del_row.h[:],
            start=True, stop=True),
            r=[(c_oner, ALL), (del_row, ALL)], w=[(db_ps, ALL)])
        b.op("dve", lambda e, db_sb=db_sb: e.tensor_copy(
            out=db_sb.h[:], in_=db_ps.h[:]),
            r=[(db_ps, ALL)], w=[(db_sb, ALL)])
        b.op("act", lambda e, tt=tt, db_sb=db_sb: e.activation(
            out=dA.h[:, tt * 512:(tt + 1) * 512], in_=db_sb.h[:],
            func=AF.Exp, scale=a_col.h[:]),
            r=[(db_sb, ALL), (a_col, ALL)], w=[(dA, tt)])
        b.op("dve", lambda e, tt=tt, db_sb=db_sb: e.tensor_tensor(
            out=BxT.h[:, tt * 512:(tt + 1) * 512], in0=bx_ps.h[:],
            in1=db_sb.h[:], op=ALU.mult),
            r=[(bx_ps, ALL), (db_sb, ALL)], w=[(BxT, tt)])
        if tt < 4:
            b.op("dve", lambda e, tt=tt: e.tensor_scalar(
                out=BxT.h[:, tt * 512:(tt + 1) * 512],
                in0=BxT.h[:, tt * 512:(tt + 1) * 512],
                scalar1=s_mpref.h[:], scalar2=None, op0=ALU.mult),
                r=[(BxT, tt), (s_mpref, ALL)], w=[(BxT, tt)])

    b.op("dve", lambda e: e.tensor_tensor_scan(
        out=hT.h[:], data0=dA.h[:], data1=BxT.h[:], initial=0.0,
        op0=ALU.mult, op1=ALU.add),
        r=[(dA, ALL)] + [(dA, t) for t in range(8)] +
          [(BxT, ALL)] + [(BxT, t) for t in range(8)],
        w=[(hT, ALL)])
    b.op("dve", lambda e: e.tensor_copy(out=hTb.h[:], in_=hT.h[:, PRE:]),
         r=[(hT, ALL)], w=[(hTb, ALL)])
    if DEBUG:
        d_hT = b.dr("d_hT", [NST, EXT], F32)
        d_dA = b.dr("d_dA", [NST, EXT], F32)
        d_BxT = b.dr("d_BxT", [NST, EXT], F32)
        b.dma("pool", lambda e: e.dma_start(out=d_hT.h[:], in_=hT.h[:]),
              r=[(hT, ALL)], w=[(d_hT, ALL)])
        b.dma("pool", lambda e: e.dma_start(out=d_dA.h[:], in_=dA.h[:]),
              r=[(dA, ALL)] + [(dA, t) for t in range(8)], w=[(d_dA, ALL)])
        b.dma("pool", lambda e: e.dma_start(out=d_BxT.h[:], in_=BxT.h[:]),
              r=[(BxT, ALL)] + [(BxT, t) for t in range(8)], w=[(d_BxT, ALL)])
        globals()["_dbg_p2"] = [d_hT, d_dA, d_BxT]

    if stage < 3:
        _tail(b, nc, out, [d_xhbf, d_xscbf, d_xsc32, d_gs32]
              + globals().get("_dbg_p2", []))
        return

    b.barrier()
    g1_g = b.sb("g1_g", [128, D], F32)
    g1_b = b.sb("g1_b", [128, D], F32)
    rw_b = b.sb("rw_b", [128, D], F32)
    psA = b.ps("psA", [128, 1024], F32)
    psB = b.ps("psB", [128, 1024], F32)
    b._bc_ps = Buf("psA_h", psA.h, "ps")
    b._bc_ps = psA
    bcast_row(w_ln1g, g1_g)
    bcast_row(w_ln1b, g1_b)
    bcast_row(w_rw, rw_b)
    ktp = [b.sb(f"ktp{i}", [128, NKV], BF16) for i in range(2)]
    vtp = [b.sb(f"vtp{i}", [128, NKV], F32) for i in range(2)]
    qtp = [b.sb(f"qtp{i}", [128, OWN], BF16) for i in range(2)]
    vtok = [b.sb(f"vtok{i}", [128, 32 * 65], BF16) for i in range(4)]
    ptA = [b.sb(f"ptA{i}", [128, 1024], BF16) for i in range(2)]
    ptB = [b.sb(f"ptB{i}", [128, 1024], BF16) for i in range(2)]
    qsq_s = b.sb("qsq_s", [128, NKV], BF16)
    qsq = [qsq_s, qsq_s]
    r2row = [b.sb(f"r2row{i}", [1, 8], F32) for i in range(2)]
    m_sc = [b.sb(f"m_sc{i}", [1, 8], F32) for i in range(2)]
    negm = [b.sb(f"negm{i}", [128, 2], F32) for i in range(2)]
    oA = b.ps("oA", [65, 1024], F32)
    oB = b.ps("oB", [65, 1024], F32)
    lrec = b.sb("lrec", [1, 1024], F32)
    lrecb = b.sb("lrecb", [64, 1024], F32)
    attnT = b.sb("attnT", [128, 8 * OWN], BF16)

    for i in range(4):
        b.op("dve", lambda e, i=i: e.memset(vtok[i].h[:], 1.0),
             w=[(vtok[i], ALL)])

    def p3_loads(hp):
        bi = hp % 2
        b.dma("sp", lambda e, hp=hp, bi=bi: e.dma_start(
            out=ktp[bi].h[:], in_=d_KT.h[hp * 128:(hp + 1) * 128, :]),
            r=[(d_KT, ALL)], w=[(ktp[bi], ALL)])
        b.dma("pool", lambda e, hp=hp, bi=bi: e.dma_start(
            out=vtp[bi].h[:], in_=d_VT.h[hp * 128:(hp + 1) * 128, :]),
            r=[(d_VT, ALL)], w=[(vtp[bi], ALL)])
        b.dma("sp", lambda e, hp=hp, bi=bi: e.dma_start(
            out=qtp[bi].h[:], in_=d_QT.h[hp * 128:(hp + 1) * 128, :]),
            r=[(d_QT, ALL)], w=[(qtp[bi], ALL)])

    def p3_setup(hp):
        bi = hp % 2
        kb, vb, qb = ktp[bi], vtp[bi], qtp[bi]
        qs, r2, ms, nm = qsq[bi], r2row[bi], m_sc[bi], negm[bi]
        b.op("act", lambda e, qs=qs, qb=qb: e.activation(
            out=qs.h[:, 0:OWN], in_=qb.h[:], func=AF.Square),
            r=[(qb, ALL)], w=[(qs, ALL)])
        for hi in range(2):
            p0 = hi * 64
            for j in range(4):
                b.op("pe", lambda e, j=j, qs=qs, p0=p0, hi=hi: e.matmul(
                    psA.h[0:1, hi * 512:hi * 512 + 512],
                    c_oneb.h[p0:p0 + 64, 0:1],
                    qs.h[p0:p0 + 64, j * 512:(j + 1) * 512],
                    start=(j == 0), stop=(j == 3)),
                    r=[(c_oneb, ALL), (qs, ALL)], w=[(psA, ALL)])
        for hi in range(2):
            b.op("dve", lambda e, r2=r2, hi=hi: e.reduce_max(
                out=r2.h[0:1, hi:hi + 1],
                in_=psA.h[0:1, hi * 512:hi * 512 + 512], axis=AX),
                r=[(psA, ALL)], w=[(r2, ("q", hi))])
        b.op("act", lambda e, qs=qs, kb=kb: e.activation(
            out=qs.h[:], in_=kb.h[:], func=AF.Square),
            r=[(kb, ALL), (qs, ALL)], w=[(qs, ALL)])
        for hi in range(2):
            p0 = hi * 64
            for j in range(8):
                b.op("pe", lambda e, j=j, qs=qs, p0=p0, hi=hi: e.matmul(
                    psA.h[0:1, hi * 512:hi * 512 + 512],
                    c_oneb.h[p0:p0 + 64, 0:1],
                    qs.h[p0:p0 + 64, j * 512:(j + 1) * 512],
                    start=(j == 0), stop=(j == 7)),
                    r=[(c_oneb, ALL), (qs, ALL)], w=[(psA, ALL)])
        for hi in range(2):
            b.op("dve", lambda e, r2=r2, hi=hi: e.reduce_max(
                out=r2.h[0:1, 2 + hi:3 + hi],
                in_=psA.h[0:1, hi * 512:hi * 512 + 512], axis=AX),
                r=[(psA, ALL)], w=[(r2, ("k", hi))])
        for hi in range(2):
            b.op("dve", lambda e, r2=r2, ms=ms, hi=hi: e.tensor_tensor(
                out=ms.h[0:1, hi:hi + 1], in0=r2.h[0:1, hi:hi + 1],
                in1=r2.h[0:1, 2 + hi:3 + hi], op=ALU.mult),
                r=[(r2, ("q", hi)), (r2, ("k", hi))], w=[(ms, ("p", hi))])
            b.op("act", lambda e, ms=ms, hi=hi: e.activation(
                out=ms.h[0:1, 2 + hi:3 + hi], in_=ms.h[0:1, hi:hi + 1],
                func=AF.Ln), r=[(ms, ("p", hi))], w=[(ms, ("l", hi))])
            b.op("act", lambda e, ms=ms, hi=hi: e.activation(
                out=ms.h[0:1, 4 + hi:5 + hi], in_=ms.h[0:1, 2 + hi:3 + hi],
                func=AF.Exp, scale=0.5),
                r=[(ms, ("l", hi))], w=[(ms, ("e", hi))])
            b.op("dve", lambda e, ms=ms, hi=hi: e.tensor_scalar(
                out=ms.h[0:1, 6 + hi:7 + hi], in0=ms.h[0:1, 4 + hi:5 + hi],
                scalar1=-0.125, scalar2=None, op0=ALU.mult),
                r=[(ms, ("e", hi))], w=[(ms, ("m", hi))])
            b.op("pe", lambda e, ms=ms, hi=hi: e.matmul(
                psA.h[:, hi:hi + 1], c_oner.h[0:1, :],
                ms.h[0:1, 6 + hi:7 + hi], start=True, stop=True),
                r=[(c_oner, ALL), (ms, ("m", hi))], w=[(psA, ALL)])
            b.op("dve", lambda e, nm=nm, hi=hi: e.tensor_copy(
                out=nm.h[:, hi:hi + 1], in_=psA.h[:, hi:hi + 1]),
                r=[(psA, ALL)], w=[(nm, hi)])
        for hi in range(2):
            vk = vtok[hp % 2 * 2 + hi]
            p0 = hi * 64
            for kt in range(32):
                b.op("pe", lambda e, kt=kt, vb=vb, p0=p0: e.transpose(
                    psA.h[:, 0:64], vb.h[p0:p0 + 64, kt * 128:(kt + 1) * 128],
                    ident_f.h[p0:p0 + 64, p0:p0 + 64]),
                    r=[(vb, ALL), (ident_f, ALL)], w=[(psA, ALL)])
                b.op("dve", lambda e, kt=kt, vk=vk: e.tensor_copy(
                    out=vk.h[:, kt * 65:kt * 65 + 64], in_=psA.h[:, 0:64]),
                    r=[(psA, ALL)], w=[(vk, kt)])

    def p3_qh(hp, qh):
        bi = hp % 2
        kb, qb, nm = ktp[bi], qtp[bi], negm[bi]
        vkA, vkB = vtok[bi * 2], vtok[bi * 2 + 1]
        ch, hwA, hwB = hp // 1, 0, 64
        chA, rwA = (2 * hp) // 2, ((2 * hp) % 2) * 64
        chB, rwB = (2 * hp + 1) // 2, ((2 * hp + 1) % 2) * 64
        q0 = qh * 1024

        def att_S(kt):
            for j in range(2):
                b.op("pe", lambda e, kt=kt, j=j, kb=kb, qb=qb: e.matmul(
                    psA.h[:, j * 512:(j + 1) * 512],
                    kb.h[0:64, kt * 128:(kt + 1) * 128],
                    qb.h[0:64, q0 + j * 512:q0 + (j + 1) * 512],
                    start=True, stop=True, tile_position=(0, 0)),
                    r=[(kb, ALL), (qb, ALL)], w=[(psA, ALL)])
                b.op("pe", lambda e, kt=kt, j=j, kb=kb, qb=qb: e.matmul(
                    psB.h[:, j * 512:(j + 1) * 512],
                    kb.h[64:128, kt * 128:(kt + 1) * 128],
                    qb.h[64:128, q0 + j * 512:q0 + (j + 1) * 512],
                    start=True, stop=True, tile_position=(64, 0)),
                    r=[(kb, ALL), (qb, ALL)], w=[(psB, ALL)])

        att_S(0)
        for kt in range(32):
            pA, pB = ptA[kt % 2], ptB[kt % 2]
            b.op("act", lambda e, pA=pA, nm=nm: e.activation(
                out=pA.h[:], in_=psA.h[:], func=AF.Exp,
                scale=c_eighth.h[:], bias=nm.h[:, 0:1]),
                r=[(psA, ALL), (c_eighth, ALL), (nm, 0)], w=[(pA, ALL)])
            b.op("act", lambda e, pB=pB, nm=nm: e.activation(
                out=pB.h[:], in_=psB.h[:], func=AF.Exp,
                scale=c_eighth.h[:], bias=nm.h[:, 1:2]),
                r=[(psB, ALL), (c_eighth, ALL), (nm, 1)], w=[(pB, ALL)])
            if kt < 31:
                att_S(kt + 1)
            for j in range(2):
                b.op("pe", lambda e, pA=pA, kt=kt, j=j, vkA=vkA: e.matmul(
                    oA.h[:, j * 512:(j + 1) * 512],
                    vkA.h[:, kt * 65:(kt + 1) * 65],
                    pA.h[:, j * 512:(j + 1) * 512],
                    start=(kt == 0), stop=(kt == 31)),
                    r=[(vkA, kt), (pA, ALL)], w=[(oA, j)])
                b.op("pe", lambda e, pB=pB, kt=kt, j=j, vkB=vkB: e.matmul(
                    oB.h[:, j * 512:(j + 1) * 512],
                    vkB.h[:, kt * 65:(kt + 1) * 65],
                    pB.h[:, j * 512:(j + 1) * 512],
                    start=(kt == 0), stop=(kt == 31)),
                    r=[(vkB, kt), (pB, ALL)], w=[(oB, j)])
        for (ob, chx, rwx) in ((oA, chA, rwA), (oB, chB, rwB)):
            b.op("dve", lambda e, ob=ob: e.reciprocal(
                out=lrec.h[:], in_=ob.h[64:65, :]),
                r=[(ob, 0), (ob, 1)], w=[(lrec, ALL)])
            for j in range(2):
                b.op("pe", lambda e, j=j: e.matmul(
                    psA.h[0:64, 0:512], c_oner.h[0:1, 0:64],
                    lrec.h[0:1, j * 512:(j + 1) * 512],
                    start=True, stop=True),
                    r=[(c_oner, ALL), (lrec, ALL)], w=[(psA, ALL)])
                b.op("dve", lambda e, j=j: e.tensor_copy(
                    out=lrecb.h[:, j * 512:(j + 1) * 512],
                    in_=psA.h[0:64, 0:512]),
                    r=[(psA, ALL)], w=[(lrecb, j)])
            b.op("dve", lambda e, ob=ob, chx=chx, rwx=rwx, q0=q0:
                 e.tensor_tensor(
                     out=attnT.h[rwx:rwx + 64,
                                 chx * OWN + q0:chx * OWN + q0 + 1024],
                     in0=ob.h[0:64, :], in1=lrecb.h[:], op=ALU.mult),
                 r=[(ob, 0), (ob, 1), (lrecb, 0), (lrecb, 1)],
                 w=[(attnT, (chx, rwx, q0))])

    p3_loads(0)
    p3_setup(0)
    for hp in range(NH // 2):
        if hp + 1 < NH // 2:
            p3_loads(hp + 1)
        p3_qh(hp, 0)
        if hp + 1 < NH // 2:
            p3_setup(hp + 1)
        p3_qh(hp, 1)

    wo_sb = b.sb("wo_sb", [128, 8 * D], BF16)
    for dc in range(8):
        b.dma("pool", lambda e, dc=dc: e.dma_start(
            out=wo_sb.h[:, dc * D:(dc + 1) * D],
            in_=w_wo.h[dc * 128:(dc + 1) * 128, :]), w=[(wo_sb, dc)])
    bo_row = b.sb("bo_row", [1, D], F32)
    b.dma("sp", lambda e: e.dma_start(out=bo_row.h[:], in_=w_bo.h.ap()),
          w=[(bo_row, ALL)])
    r1_t_r = [b.sb(f"r1_t{i}", [128, D], F32) for i in range(2)]
    x1_t_r = [b.sb(f"x1_t{i}", [128, D], F32) for i in range(2)]
    keys_t_r = [b.sb(f"keys_t{i}", [128, D], F32) for i in range(2)]
    kcol_r = [b.sb(f"kcol{i}", [128, 1], F32) for i in range(2)]
    kint_r = [b.sb(f"kint{i}", [128, 1], I32) for i in range(2)]
    kback_r = [b.sb(f"kback{i}", [128, 1], F32) for i in range(2)]
    kd_r = [b.sb(f"kd{i}", [128, 1], F32) for i in range(2)]

    def p3_block(tt):
        r1_t, x1_t, keys_t = r1_t_r[tt % 2], x1_t_r[tt % 2], keys_t_r[tt % 2]
        kcol, kint, kback, kd = (kcol_r[tt % 2], kint_r[tt % 2],
                                 kback_r[tt % 2], kd_r[tt % 2])
        xn_r = xn_r_r[tt % 2]
        wps_t = psA if tt % 2 == 0 else psB
        for j in range(2):
            for dc in range(8):
                b.op("pe", lambda e, tt=tt, j=j, dc=dc, wps=wps_t: e.matmul(
                    wps.h[:, j * 512:(j + 1) * 512],
                    attnT.h[:, dc * OWN + tt * 128:dc * OWN + (tt + 1) * 128],
                    wo_sb.h[:, dc * D + j * 512:dc * D + (j + 1) * 512],
                    start=(dc == 0), stop=False),
                    r=[(attnT, ALL), (wo_sb, dc)], w=[(wps_t, j)])
            b.op("pe", lambda e, j=j, wps=wps_t: e.matmul(
                wps.h[:, j * 512:(j + 1) * 512], c_oner.h[0:1, :],
                bo_row.h[0:1, j * 512:(j + 1) * 512],
                start=False, stop=True),
                r=[(c_oner, ALL), (bo_row, ALL)], w=[(wps_t, j)])
        b.dma("sp", lambda e, tt=tt, xn_r=xn_r: e.dma_start(
            out=xn_r.h[:], in_=d_xn32.h[tt * 128:(tt + 1) * 128, :]),
            r=[(d_xn32, tt)], w=[(xn_r, ALL)])
        b.op("dve", lambda e, wps=wps_t, r1_t=r1_t, xn_r=xn_r:
             e.tensor_tensor(
                 out=r1_t.h[:], in0=wps.h[:], in1=xn_r.h[:], op=ALU.add),
             r=[(wps_t, 0), (wps_t, 1), (xn_r, ALL)], w=[(r1_t, ALL)])
        ln_tile(r1_t, x1_t)
        b.op("dve", lambda e: e.tensor_tensor(
            out=x1_t.h[:], in0=x1_t.h[:], in1=g1_g.h[:], op=ALU.mult),
            r=[(x1_t, ALL), (g1_g, ALL)], w=[(x1_t, ALL)])
        b.op("pool", lambda e: e.tensor_tensor(
            out=x1_t.h[:], in0=x1_t.h[:], in1=g1_b.h[:], op=ALU.add),
            r=[(x1_t, ALL), (g1_b, ALL)], w=[(x1_t, ALL)])
        b.dma("pool", lambda e, tt=tt: e.dma_start(
            out=d_x132.h[tt * 128:(tt + 1) * 128, :], in_=x1_t.h[:]),
            r=[(x1_t, ALL)], w=[(d_x132, tt)])
        b.dma("pool", lambda e, tt=tt: e.dma_start(
            out=d_x1bf.h[tt * 128:(tt + 1) * 128, :], in_=x1_t.h[:]),
            r=[(x1_t, ALL)], w=[(d_x1bf, tt)])
        b.op("dve", lambda e: e.tensor_tensor(
            out=keys_t.h[:], in0=x1_t.h[:], in1=rw_b.h[:], op=ALU.mult),
            r=[(x1_t, ALL), (rw_b, ALL)], w=[(keys_t, ALL)])
        b.op("dve", lambda e: e.reduce_sum(out=kcol.h[:], in_=keys_t.h[:],
                                           axis=AX),
             r=[(keys_t, ALL)], w=[(kcol, ALL)])
        b.op("dve", lambda e: e.tensor_scalar(
            out=kcol.h[:], in0=kcol.h[:], scalar1=rb_c.h[:], scalar2=None,
            op0=ALU.add), r=[(kcol, ALL), (rb_c, ALL)], w=[(kcol, ALL)])
        b.op("dve", lambda e: e.tensor_copy(out=kint.h[:], in_=kcol.h[:]),
             r=[(kcol, ALL)], w=[(kint, ALL)])
        b.op("dve", lambda e: e.tensor_copy(out=kback.h[:], in_=kint.h[:]),
             r=[(kint, ALL)], w=[(kback, ALL)])
        b.op("dve", lambda e: e.tensor_tensor(
            out=kd.h[:], in0=kcol.h[:], in1=kback.h[:], op=ALU.is_lt),
            r=[(kcol, ALL), (kback, ALL)], w=[(kd, ALL)])
        b.op("dve", lambda e: e.tensor_sub(out=kcol.h[:], in0=kback.h[:],
                                           in1=kd.h[:]),
             r=[(kback, ALL), (kd, ALL)], w=[(kcol, ALL)])
        b.op("dve", lambda e: e.tensor_scalar(
            out=kd.h[:], in0=kcol.h[:], scalar1=0.25, scalar2=None,
            op0=ALU.mult), r=[(kcol, ALL)], w=[(kd, ALL)])
        b.op("dve", lambda e: e.tensor_copy(out=kint.h[:], in_=kd.h[:]),
             r=[(kd, ALL)], w=[(kint, ALL)])
        b.op("dve", lambda e: e.tensor_copy(out=kback.h[:], in_=kint.h[:]),
             r=[(kint, ALL)], w=[(kback, ALL)])
        b.op("dve", lambda e: e.tensor_tensor(
            out=kd.h[:], in0=kd.h[:], in1=kback.h[:], op=ALU.is_lt),
            r=[(kd, ALL), (kback, ALL)], w=[(kd, ALL)])
        b.op("dve", lambda e: e.tensor_sub(out=kback.h[:], in0=kback.h[:],
                                           in1=kd.h[:]),
             r=[(kback, ALL), (kd, ALL)], w=[(kback, ALL)])
        b.op("dve", lambda e, tt=tt: e.scalar_tensor_tensor(
            out=idx_cols.h[:, tt:tt + 1], in0=kback.h[:], scalar=-4.0,
            in1=kcol.h[:], op0=ALU.mult, op1=ALU.add),
            r=[(kback, ALL), (kcol, ALL)], w=[(idx_cols, tt)])

    for tt in range(16):
        p3_block(tt)

    if stage < 4:
        _tail(b, nc, out, [d_x1bf, d_x132])
        return

    b.barrier()
    g2_g = b.sb("g2_g", [128, D], F32)
    g2_b = b.sb("g2_b", [128, D], F32)
    bcast_row(w_ln2g, g2_g)
    bcast_row(w_ln2b, g2_b)
    w1_sb = [b.sb(f"w1_sb{i}", [128, 8 * DI], BF16) for i in range(2)]
    w2_sb = [b.sb(f"w2_sb{i}", [128, 8 * D], BF16) for i in range(2)]
    b1cols = b.sb("b1cols", [128, 32], F32)
    b1row = b.sb("b1row", [32, 128], F32)
    b2row = b.sb("b2row", [1, D], F32)
    ps_b1 = b.ps("ps_b1", [128, 512], F32)
    mask_c = b.sb("mask_c", [128, 16], F32)
    x1fm = [b.sb(f"x1fm{i}", [128, 8 * 512], BF16) for i in range(2)]
    h_ps = b.ps("h_ps", [128, 512], F32)
    h_ps2 = b.ps("h_ps2", [128, 512], F32)
    h_sb = b.sb("h_sb", [128, 16 * 512], BF16)
    y_ps = b.ps("y_ps", [128, 1024], F32)
    y_ps2 = b.ps("y_ps2", [128, 1024], F32)
    acc = b.sb("acc", [128, 16 * 1024], BF16)
    g_t = b.sb("g_t", [128, D], F32)
    x2_t = xn_r_r[0]

    def p4_ln2(ti):
        b.op("dve", lambda e, ti=ti: e.tensor_copy(
            out=x2_t.h[:], in_=acc.h[:, ti * 1024:(ti + 1) * 1024]),
            r=[(acc, ti)], w=[(x2_t, ALL)])
        ln_tile(x2_t, g_t)
        b.op("dve", lambda e: e.tensor_tensor(
            out=g_t.h[:], in0=g_t.h[:], in1=g2_g.h[:], op=ALU.mult),
            r=[(g_t, ALL), (g2_g, ALL)], w=[(g_t, ALL)])
        b.op("pool", lambda e: e.tensor_tensor(
            out=g_t.h[:], in0=g_t.h[:], in1=g2_b.h[:], op=ALU.add),
            r=[(g_t, ALL), (g2_b, ALL)], w=[(g_t, ALL)])
        b.op("act", lambda e: e.activation(out=g_t.h[:], in_=g_t.h[:],
                                           func=AF.Silu),
             r=[(g_t, ALL)], w=[(g_t, ALL)])
        b.dma("pool", lambda e, ti=ti: e.dma_start(
            out=d_g32.h[ti * 128:(ti + 1) * 128, :], in_=g_t.h[:]),
            r=[(g_t, ALL)], w=[(d_g32, ti)])

    def p4_evac(e_i, half, yp, ti):
        if e_i == 0 and half == 0:
            xr = xn_r_r[ti % 2]
            b.dma("sp", lambda e, ti=ti, xr=xr: e.dma_start(
                out=xr.h[:], in_=d_x132.h[ti * 128:(ti + 1) * 128, :]),
                r=[(d_x132, ti)], w=[(xr, ALL)])
            b.op("dve", lambda e, yp=yp, ti=ti, xr=xr: e.scalar_tensor_tensor(
                out=acc.h[:, ti * 1024:(ti + 1) * 1024],
                in0=yp.h[:], scalar=mask_c.h[:, ti:ti + 1],
                in1=xr.h[:], op0=ALU.mult, op1=ALU.add),
                r=[(yp, 0), (yp, 1), (mask_c, ALL), (xr, ALL)],
                w=[(acc, ti)])
        else:
            b.op("dve", lambda e, yp=yp, ti=ti: e.scalar_tensor_tensor(
                out=acc.h[:, ti * 1024:(ti + 1) * 1024],
                in0=yp.h[:], scalar=mask_c.h[:, ti:ti + 1],
                in1=acc.h[:, ti * 1024:(ti + 1) * 1024],
                op0=ALU.mult, op1=ALU.add),
                r=[(yp, 0), (yp, 1), (mask_c, ALL), (acc, ti)],
                w=[(acc, ti)])
        if e_i == NE - 1 and half == 1:
            p4_ln2(ti)

    def p4_w1_load(e_i, half):
        wb = w1_sb[(e_i * 2 + half) % 2]
        for dc in range(8):
            b.dma("pool", lambda e, e_i=e_i, half=half, dc=dc, wb=wb:
                  e.dma_start(
                      out=wb.h[:, dc * DI:(dc + 1) * DI],
                      in_=w_w1.h[e_i, dc * 128:(dc + 1) * 128,
                                 half * 2048:(half + 1) * 2048]),
                  w=[(wb, dc)])
        return wb

    def p4_w2_load(e_i, half, quarter):
        wb = w2_sb[quarter % 2]
        for fc8 in range(8):
            fcg = half * 2048 + (quarter * 8 + fc8) * 128
            b.dma("pool", lambda e, e_i=e_i, fcg=fcg, fc8=fc8, wb=wb:
                  e.dma_start(out=wb.h[:, fc8 * D:(fc8 + 1) * D],
                              in_=w_w2.h[e_i, fcg:fcg + 128, :]),
                  w=[(wb, fc8)])
        return wb

    w1_cur = p4_w1_load(0, 0)
    for e_i in range(NE):
        b.dma("sp", lambda e, e_i=e_i: e.dma_start(
            out=b1row.h[:], in_=w_b1.h[e_i:e_i + 1, :].rearrange(
                "a (c p) -> (a c) p", p=128)),
            w=[(b1row, ALL)])
        b.op("pe", lambda e: e.transpose(ps_b1.h[:, 0:32], b1row.h[:],
                                         ident_f.h[0:32, 0:32]),
             r=[(b1row, ALL), (ident_f, ALL)], w=[(ps_b1, ALL)])
        b.op("dve", lambda e: e.tensor_copy(out=b1cols.h[:],
                                            in_=ps_b1.h[:, 0:32]),
             r=[(ps_b1, ALL)], w=[(b1cols, ALL)])
        b.dma("sp", lambda e, e_i=e_i: e.dma_start(
            out=b2row.h[:], in_=w_b2.h[e_i:e_i + 1, :]), w=[(b2row, ALL)])
        b.op("dve", lambda e, e_i=e_i: e.tensor_scalar(
            out=mask_c.h[:], in0=idx_cols.h[:], scalar1=float(e_i),
            scalar2=None, op0=ALU.is_equal),
            r=[(idx_cols, ALL)] + [(idx_cols, t) for t in range(16)],
            w=[(mask_c, ALL)])
        for half in range(2):
            wb1 = w1_cur
            if not (e_i == NE - 1 and half == 1):
                ne_, nh_ = (e_i, 1) if half == 0 else (e_i + 1, 0)
                w1_cur = p4_w1_load(ne_, nh_)
            for tt4 in range(4):
                xf = x1fm[tt4 % 2]
                for dc in range(8):
                    b.dma("act", lambda e, tt4=tt4, dc=dc, xf=xf: e.dma_start(
                        out=xf.h[:, dc * 512:(dc + 1) * 512],
                        in_=d_x1bf.h[tt4 * 512:(tt4 + 1) * 512,
                                     dc * 128:(dc + 1) * 128], transpose=True),
                        r=[(d_x1bf, ALL)], w=[(xf, dc)])
                for fc in range(16):
                    ps = h_ps if fc % 2 == 0 else h_ps2
                    for dc in range(8):
                        b.op("pe", lambda e, ps=ps, fc=fc, dc=dc, xf=xf,
                             wb1=wb1: e.matmul(
                                 ps.h[:],
                                 wb1.h[:, dc * DI + fc * 128:
                                       dc * DI + (fc + 1) * 128],
                                 xf.h[:, dc * 512:(dc + 1) * 512],
                                 start=(dc == 0), stop=(dc == 7)),
                             r=[(wb1, dc), (xf, dc)], w=[(ps, ALL)])
                    b.op("act", lambda e, ps=ps, fc=fc, half=half:
                         e.activation(
                             out=h_sb.h[:, fc * 512:(fc + 1) * 512],
                             in_=ps.h[:], func=AF.Relu,
                             bias=b1cols.h[:, half * 16 + fc:
                                           half * 16 + fc + 1]),
                         r=[(ps, ALL), (b1cols, ALL)], w=[(h_sb, fc)])
                if tt4 == 0:
                    wq = [p4_w2_load(e_i, half, 0), p4_w2_load(e_i, half, 1)]
                else:
                    wq = [w2_sb[0], w2_sb[1]]
                for ts_ in range(4):
                    yp = y_ps if ts_ % 2 == 0 else y_ps2
                    ti = tt4 * 4 + ts_
                    for j in range(2):
                        for fc in range(16):
                            wb2 = wq[fc // 8]
                            fc8 = fc % 8
                            b.op("pe", lambda e, yp=yp, j=j, fc=fc, fc8=fc8,
                                 ts_=ts_, half=half, wb2=wb2: e.matmul(
                                     yp.h[:, j * 512:(j + 1) * 512],
                                     h_sb.h[:, fc * 512 + ts_ * 128:
                                            fc * 512 + (ts_ + 1) * 128],
                                     wb2.h[:, fc8 * D + j * 512:
                                           fc8 * D + (j + 1) * 512],
                                     start=(fc == 0),
                                     stop=(half == 0 and fc == 15)),
                                 r=[(h_sb, fc), (wb2, fc8)],
                                 w=[(yp, j)])
                        if half == 1:
                            b.op("pe", lambda e, yp=yp, j=j: e.matmul(
                                yp.h[:, j * 512:(j + 1) * 512],
                                c_oner.h[0:1, :],
                                b2row.h[0:1, j * 512:(j + 1) * 512],
                                start=False, stop=True),
                                r=[(c_oner, ALL), (b2row, ALL)],
                                w=[(yp, j)])
                    p4_evac(e_i, half, yp, ti)

    if stage < 5:
        _tail(b, nc, out, [d_g32])
        return

    b.barrier()
    cmat_t = b.sb("cmat_t", [128, NST], F32)
    ct_sb = b.sb("ct_sb", [NST, 16 * 128], BF16)
    ps_ct = b.ps("ps_ct", [128, 512], F32)
    for dc in range(16):
        b.dma("sp", lambda e, dc=dc: e.dma_start(
            out=cmat_t.h[:], in_=w_cmat.h[dc * 128:(dc + 1) * 128, :]),
            w=[(cmat_t, ALL)])
        b.op("pe", lambda e, dc=dc: e.transpose(
            ps_ct.h[0:NST, 0:128], cmat_t.h[:], ident_f.h[:]),
            r=[(cmat_t, ALL), (ident_f, ALL)], w=[(ps_ct, ALL)])
        b.op("dve", lambda e, dc=dc: e.tensor_copy(
            out=ct_sb.h[:, dc * 128:(dc + 1) * 128],
            in_=ps_ct.h[0:NST, 0:128]),
            r=[(ps_ct, ALL)], w=[(ct_sb, dc)])
    wout_sb = b.sb("wout_sb", [128, 16 * D], BF16)
    for dc in range(16):
        b.dma("pool", lambda e, dc=dc: e.dma_start(
            out=wout_sb.h[:, dc * D:(dc + 1) * D],
            in_=w_wout.h[dc * 128:(dc + 1) * 128, :]), w=[(wout_sb, dc)])

    y2_ps = b.ps("y2_ps", [128, 512], F32)
    y2_ps2 = b.ps("y2_ps2", [128, 512], F32)
    xsc_r_r = [b.sb(f"xsc_r{i}", [128, 512], F32) for i in range(2)]
    gs_r_r = [b.sb(f"gs_r{i}", [128, 512], F32) for i in range(2)]
    yg_sb = b.sb("yg_sb", [128, 16 * 512], BF16)
    so_ps = b.ps("so_ps", [128, 1024], F32)
    so_ps2 = b.ps("so_ps2", [128, 1024], F32)
    s_t_r = [b.sb(f"s_t{i}", [128, D], F32) for i in range(2)]
    gf_t_r = [b.sb(f"gf_t{i}", [128, D], F32) for i in range(2)]
    xo_t_r = [b.sb(f"xo_t{i}", [128, D], F32) for i in range(2)]
    o_t_r = [b.sb(f"o_t{i}", [128, D], F32) for i in range(2)]

    def p5_y(tt4, dc):
        yps = y2_ps if dc % 2 == 0 else y2_ps2
        xsc_r, gs_r = xsc_r_r[dc % 2], gs_r_r[dc % 2]
        b.op("pe", lambda e, yps=yps, tt4=tt4, dc=dc: e.matmul(
            yps.h[:], ct_sb.h[:, dc * 128:(dc + 1) * 128],
            hTb.h[:, tt4 * 512:(tt4 + 1) * 512], start=True, stop=True),
            r=[(ct_sb, dc), (hTb, ALL)], w=[(yps, ALL)])
        b.dma("sp", lambda e, tt4=tt4, dc=dc, xsc_r=xsc_r: e.dma_start(
            out=xsc_r.h[:], in_=d_xsc32.h[dc * 128:(dc + 1) * 128,
                                          tt4 * 512:(tt4 + 1) * 512]),
            r=[(d_xsc32, (dc, tt4 + 4))], w=[(xsc_r, ALL)])
        b.dma("sp", lambda e, tt4=tt4, dc=dc, gs_r=gs_r: e.dma_start(
            out=gs_r.h[:], in_=d_gs32.h[dc * 128:(dc + 1) * 128,
                                        tt4 * 512:(tt4 + 1) * 512]),
            r=[(d_gs32, (dc, tt4 + 4))], w=[(gs_r, ALL)])
        b.op("dve", lambda e, yps=yps, dc=dc, xsc_r=xsc_r:
             e.scalar_tensor_tensor(
                 out=xsc_r.h[:], in0=xsc_r.h[:], scalar=dv_c.h[:, dc:dc + 1],
                 in1=yps.h[:], op0=ALU.mult, op1=ALU.add),
             r=[(xsc_r, ALL), (dv_c, ALL), (yps, ALL)], w=[(xsc_r, ALL)])
        b.op("dve", lambda e, dc=dc, xsc_r=xsc_r, gs_r=gs_r: e.tensor_tensor(
            out=yg_sb.h[:, dc * 512:(dc + 1) * 512], in0=xsc_r.h[:],
            in1=gs_r.h[:], op=ALU.mult),
            r=[(xsc_r, ALL), (gs_r, ALL)], w=[(yg_sb, dc)])

    def p5_out(tt4, ts_):
        sp_ = so_ps if ts_ % 2 == 0 else so_ps2
        ti = tt4 * 4 + ts_
        s_t, gf_t = s_t_r[ts_ % 2], gf_t_r[ts_ % 2]
        xo_t, o_t = xo_t_r[ts_ % 2], o_t_r[ts_ % 2]
        xr = xn_r_r[ts_ % 2]
        for j in range(2):
            for dc in range(16):
                b.op("pe", lambda e, sp_=sp_, ts_=ts_, j=j, dc=dc:
                     e.matmul(
                         sp_.h[:, j * 512:(j + 1) * 512],
                         yg_sb.h[:, dc * 512 + ts_ * 128:
                                 dc * 512 + (ts_ + 1) * 128],
                         wout_sb.h[:, dc * D + j * 512:
                                   dc * D + (j + 1) * 512],
                         start=(dc == 0), stop=(dc == 15)),
                     r=[(yg_sb, dc), (wout_sb, dc)], w=[(sp_, j)])
        b.dma("sp", lambda e, ti=ti, xr=xr: e.dma_start(
            out=xr.h[:], in_=d_xn32.h[ti * 128:(ti + 1) * 128, :]),
            r=[(d_xn32, ti)], w=[(xr, ALL)])
        b.op("dve", lambda e, sp_=sp_, s_t=s_t, xr=xr: e.tensor_tensor(
            out=s_t.h[:], in0=sp_.h[:], in1=xr.h[:], op=ALU.add),
            r=[(sp_, 0), (sp_, 1), (xr, ALL)], w=[(s_t, ALL)])
        b.dma("sp", lambda e, ti=ti, gf_t=gf_t: e.dma_start(
            out=gf_t.h[:], in_=d_g32.h[ti * 128:(ti + 1) * 128, :]),
            r=[(d_g32, ti)], w=[(gf_t, ALL)])
        b.dma("sp", lambda e, ti=ti, xo_t=xo_t: e.dma_start(
            out=xo_t.h[:],
            in_=xs_ext.h[PRE + ti * 128:PRE + (ti + 1) * 128, :]),
            w=[(xo_t, ALL)])
        b.op("dve", lambda e, s_t=s_t, gf_t=gf_t: e.tensor_tensor(
            out=s_t.h[:], in0=s_t.h[:], in1=gf_t.h[:], op=ALU.mult),
            r=[(s_t, ALL), (gf_t, ALL)], w=[(s_t, ALL)])
        b.op("dve", lambda e, s_t=s_t, xo_t=xo_t, o_t=o_t: e.tensor_tensor(
            out=o_t.h[:], in0=s_t.h[:], in1=xo_t.h[:], op=ALU.add),
            r=[(s_t, ALL), (xo_t, ALL)], w=[(o_t, ALL)])
        b.dma("pool", lambda e, ti=ti, o_t=o_t: e.dma_start(
            out=out.h[ti * 128:(ti + 1) * 128, :], in_=o_t.h[:]),
            r=[(o_t, ALL)], w=[(out, ti)])

    for tt4 in range(4):
        for dc in range(16):
            p5_y(tt4, dc)
        for ts_ in range(4):
            p5_out(tt4, ts_)

    _tail(b, nc, out, [])


def _tail(b, nc, out, extra):
    b.finalize()
    if out.sem is not None:
        nc.sync.wait_ge(out.sem, 16 * out.sem_cnt)
    for buf in extra:
        if buf.sem is not None:
            nc.sync.wait_ge(buf.sem, 16 * buf.sem_cnt)
    nc.all_engine_barrier()
    for s in b.all_sems:
        nc.gpsimd.sem_clear(s)
    nc.all_engine_barrier()
    b.phase.close()



_NC_CACHE = {}


def _get_nc(stage=STAGE):
    if stage not in _NC_CACHE:
        _NC_CACHE[stage] = build_nc(stage)
    return _NC_CACHE[stage]


def make_in_maps(inp):
    import ml_dtypes
    x = np.ascontiguousarray(np.asarray(inp["L0_attn_x"], dtype=np.float32))
    f32 = lambda v: np.ascontiguousarray(np.asarray(v, dtype=np.float32))
    common = dict(
        identf=np.eye(128, dtype=np.float32),
        identb=np.eye(128, dtype=np.float32).astype(ml_dtypes.bfloat16),
        w_lng=f32(inp["L0_attn_ln_gamma"]), w_lnb=f32(inp["L0_attn_ln_beta"]),
        w_in=f32(inp["L0_attn_W_in"]),
        w_convw=f32(np.asarray(inp["L0_attn_conv_w"]).reshape(DI, KCONV).T),
        w_convb=f32(inp["L0_attn_conv_b"]),
        w_alog=f32(np.asarray(inp["L0_attn_A_log"]).reshape(1, NST)),
        w_bmat=f32(inp["L0_attn_B_mat"]), w_cmat=f32(inp["L0_attn_C_mat"]),
        w_dvec=f32(inp["L0_attn_D_vec"]), w_wd=f32(inp["L0_attn_Wd"]),
        w_bd=f32(inp["L0_attn_bd"]), w_wout=f32(inp["L0_attn_W_out"]),
        w_wq=f32(inp["L0_ffn_Wq"]), w_bq=f32(inp["L0_ffn_bq"]),
        w_wk=f32(inp["L0_ffn_Wk"]), w_bk=f32(inp["L0_ffn_bk"]),
        w_wv=f32(inp["L0_ffn_Wv"]), w_bv=f32(inp["L0_ffn_bv"]),
        w_wo=f32(inp["L0_ffn_Wo"]),
        w_bo=f32(np.asarray(inp["L0_ffn_bo"]).reshape(1, D)),
        w_rw=f32(np.asarray(inp["L0_ffn_router_w"]).reshape(D)),
        w_rb=f32(np.asarray(inp["L0_ffn_router_b"]).reshape(1, 1)),
        w_w1=f32(inp["L0_ffn_W1"]), w_b1=f32(inp["L0_ffn_b1"]),
        w_w2=f32(inp["L0_ffn_W2"]), w_b2=f32(inp["L0_ffn_b2"]),
        w_ln1g=f32(inp["L0_ffn_ln1_g"]), w_ln1b=f32(inp["L0_ffn_ln1_b"]),
        w_ln2g=f32(inp["L0_ffn_ln2_g"]), w_ln2b=f32(inp["L0_ffn_ln2_b"]),
    )
    in_maps = []
    for c in range(8):
        bi, p = c // 2, c % 2
        own = x[bi, p * OWN:(p + 1) * OWN]
        oth = x[bi, (1 - p) * OWN:(2 - p) * OWN]
        pre = x[bi, 0:PRE] if p == 1 else np.zeros((PRE, D), np.float32)
        m = np.float32(p)
        in_maps.append(dict(
            common,
            xs_ext=np.ascontiguousarray(np.concatenate([pre, own], 0)),
            xkv=np.ascontiguousarray(oth),
            mpref16=np.full((16, 1), m, np.float32),
            mhalo=np.full((128, 1), m, np.float32)))
    return in_maps


def kernel(**inp):
    nc = _get_nc(STAGE)
    in_maps = make_in_maps(inp)
    res = run_bass_kernel_spmd(nc, in_maps, core_ids=list(range(8)))
    outp = np.empty((B, T, D), np.float32)
    for c in range(8):
        bi, p = c // 2, c % 2
        outp[bi, p * OWN:(p + 1) * OWN] = res.results[c]["out"]
    return outp


# version 11
# speedup vs baseline: 1.1320x; 1.0325x over previous
import numpy as np
from contextlib import ExitStack

import concourse.bass as bass
import concourse.mybir as mybir
from concourse.bass_utils import run_bass_kernel_spmd

F32 = mybir.dt.float32
BF16 = mybir.dt.bfloat16
I32 = mybir.dt.int32
AF = mybir.ActivationFunctionType
ALU = mybir.AluOpType
AX = mybir.AxisListType.X

B, T, D = 4, 4096, 1024
DI, NST, KCONV = 2048, 16, 4
DFF, NE, NH, HD = 4096, 4, 16, 64
EXT, PRE, OWN, NKV = 4096, 2048, 2048, 4096
EPS = 1e-5
ALL = None

DEBUG = False
STAGE = 5


class Buf:
    __slots__ = ("name", "h", "space", "sem", "sem_cnt", "writers", "readers")

    def __init__(self, name, h, space):
        self.name, self.h, self.space = name, h, space
        self.sem = None
        self.sem_cnt = 0
        self.writers = []
        self.readers = []


def _conf(k1, k2):
    return k1 is ALL or k2 is ALL or k1 == k2


class Op:
    __slots__ = ("idx", "eng", "fn", "reads", "writes", "is_dma", "sem",
                 "sem_target", "inc", "inc_seq", "waits")

    def __init__(self, idx, eng, fn, reads, writes, is_dma):
        self.idx, self.eng, self.fn = idx, eng, fn
        self.reads, self.writes, self.is_dma = list(reads), list(writes), is_dma
        self.sem = None
        self.sem_target = 0
        self.inc = False
        self.inc_seq = 0
        self.waits = []


class Bld:

    ENGS = ("pe", "dve", "act", "pool", "sp")

    def __init__(self, nc, stack):
        self.nc = nc
        self.stack = stack
        self.phase = ExitStack()
        self.ops = []
        self.e = {"pe": nc.tensor, "dve": nc.vector, "act": nc.scalar,
                  "pool": nc.gpsimd, "sp": nc.sync}
        self.all_sems = []
        self.engsem = {g: self._sem(f"es_{g}") for g in self.ENGS}

    def _sem(self, name):
        s = self.stack.enter_context(self.nc.semaphore(name))
        self.all_sems.append(s)
        return s

    def sb(self, name, shape, dtype, pers=False):
        st = self.stack if pers else self.phase
        h = st.enter_context(self.nc.sbuf_tensor(name, shape, dtype))
        return Buf(name, h, "sb")

    def ps(self, name, shape, dtype=F32):
        h = self.phase.enter_context(self.nc.psum_tensor(name, shape, dtype))
        return Buf(name, h, "ps")

    def dr(self, name, shape, dtype, kind="Internal"):
        if DEBUG and kind == "Internal":
            kind = "ExternalOutput"
        return Buf(name, self.nc.dram_tensor(name, shape, dtype, kind=kind),
                   "dr")

    def op(self, eng, fn, r=(), w=()):
        o = Op(len(self.ops), eng, fn, r, w, False)
        self.ops.append(o)
        return o

    def wave(self, lists):
        n = max(len(li) for li in lists) if lists else 0
        for step in range(n):
            for li in lists:
                if step < len(li):
                    kind, eng, fn, r, w = li[step]
                    if kind == "dma":
                        self.dma(eng, fn, r=r, w=w)
                    else:
                        self.op(eng, fn, r=r, w=w)

    def dma(self, q, fn, r=(), w=()):
        w = [(bf, ALL) for (bf, _k) in w]
        o = Op(len(self.ops), q, fn, r, w, True)
        tgt = o.writes[0][0]
        if tgt.sem is None:
            tgt.sem = self._sem(f"dm{len(self.all_sems)}")
        tgt.sem_cnt += 1
        o.sem = tgt.sem
        o.sem_target = 16 * tgt.sem_cnt
        self.ops.append(o)
        return o

    def barrier(self):
        self.ops.append("BARRIER")
        self.phase.close()
        self.phase = ExitStack()
        self._bc_ps = None
        self._bc_row = None

    def finalize(self):
        pairs = []
        allbufs = {}
        bar_dma = []
        dma_issued = {}

        def note(bf):
            if id(bf) not in allbufs:
                allbufs[id(bf)] = bf
                bf.writers, bf.readers = [], []

        for item in self.ops:
            if item == "BARRIER":
                bar_dma.append(list(dma_issued.values()))
                for bf in allbufs.values():
                    bf.writers, bf.readers = [], []
                continue
            o = item
            for (bf, k) in o.reads:
                note(bf)
                for (pi, pk) in bf.writers:
                    if _conf(k, pk):
                        pairs.append((pi, o.idx))
                bf.readers.append((o.idx, k))
            for (bf, k) in o.writes:
                note(bf)
                for (pi, pk) in bf.writers:
                    if _conf(k, pk):
                        pairs.append((pi, o.idx))
                for (pi, pk) in bf.readers:
                    if _conf(k, pk) and pi != o.idx:
                        pairs.append((pi, o.idx))
                bf.writers = [(x, kk) for (x, kk) in bf.writers
                              if not _conf(k, kk)]
                bf.readers = [(x, kk) for (x, kk) in bf.readers
                              if not _conf(k, kk)]
                bf.writers.append((o.idx, k))
            if o.is_dma:
                dma_issued[o.sem.name] = (o.sem, o.sem_target)

        idx2op = {o.idx: o for o in self.ops if o != "BARRIER"}
        for (pi, ci) in pairs:
            p = idx2op[pi]
            if not p.is_dma and not (p.eng == "pe" and idx2op[ci].eng == "pe"):
                p.inc = True
        seg_last = {}
        for o in self.ops:
            if o == "BARRIER":
                for g, lo in seg_last.items():
                    lo.inc = True
                seg_last = {}
                continue
            if not o.is_dma:
                seg_last[o.eng] = o
        cnt = {g: 0 for g in self.ENGS}
        bar_snap = []
        for o in self.ops:
            if o == "BARRIER":
                bar_snap.append(dict(cnt))
                continue
            if o.is_dma:
                continue
            if o.inc:
                cnt[o.eng] += 1
                o.inc_seq = cnt[o.eng]
        for (pi, ci) in pairs:
            p, c = idx2op[pi], idx2op[ci]
            if p.is_dma:
                c.waits.append((p.sem, p.sem_target))
            elif not (p.eng == "pe" and c.eng == "pe"):
                c.waits.append((self.engsem[p.eng], p.inc_seq))

        floors = {g: {} for g in self.ENGS}
        bar_i = 0
        for item in self.ops:
            if item == "BARRIER":
                snap = bar_snap[bar_i]
                dsems = bar_dma[bar_i]
                bar_i += 1
                for g in self.ENGS:
                    fl = floors[g]
                    for g2 in self.ENGS:
                        nm = self.engsem[g2].name
                        if g2 != g and snap[g2] > 0 and \
                                fl.get(nm, -1) < snap[g2]:
                            self.e[g].wait_ge(self.engsem[g2], snap[g2])
                            fl[nm] = snap[g2]
                    for (sem, v) in dsems:
                        if fl.get(sem.name, -1) < v:
                            self.e[g].wait_ge(sem, v)
                            fl[sem.name] = v
                continue
            o = item
            fl = floors[o.eng]
            best = {}
            for (sem, val) in o.waits:
                if val > best.get(sem.name, (None, -1))[1]:
                    best[sem.name] = (sem, val)
            for kname, (sem, val) in best.items():
                if fl.get(kname, -1) < val:
                    self.e[o.eng].wait_ge(sem, val)
                    fl[kname] = val
            inst = o.fn(self.e[o.eng])
            if o.is_dma:
                inst.then_inc(o.sem, 16)
            elif o.inc:
                inst.then_inc(self.engsem[o.eng], 1)



def build_nc(stage=STAGE):
    nc = bass.Bass("TRN2", target_bir_lowering=False)
    stack = ExitStack()
    with stack:
        _body(nc, stack, stage)
    return nc


def _body(nc, stack, stage):
    b = Bld(nc, stack)

    def din(name, shape, dtype=F32):
        return Buf(name, nc.dram_tensor(name, shape, dtype,
                                        kind="ExternalInput"), "dr")

    xs_ext = din("xs_ext", [EXT, D])
    xkv = din("xkv", [OWN, D])
    mpref16 = din("mpref16", [16, 1])
    mhalo = din("mhalo", [128, 1])
    identf = din("identf", [128, 128])
    identb = din("identb", [128, 128], BF16)
    w_lng = din("w_lng", [D]);        w_lnb = din("w_lnb", [D])
    w_in = din("w_in", [D, 2 * DI])
    w_convw = din("w_convw", [KCONV, DI])
    w_convb = din("w_convb", [DI])
    w_alog = din("w_alog", [1, NST])
    w_bmat = din("w_bmat", [NST, DI])
    w_cmat = din("w_cmat", [DI, NST])
    w_dvec = din("w_dvec", [DI])
    w_wd = din("w_wd", [DI, DI]);     w_bd = din("w_bd", [DI])
    w_wout = din("w_wout", [DI, D])
    w_wq = din("w_wq", [D, D]);       w_bq = din("w_bq", [D])
    w_wk = din("w_wk", [D, D]);       w_bk = din("w_bk", [D])
    w_wv = din("w_wv", [D, D]);       w_bv = din("w_bv", [D])
    w_wo = din("w_wo", [D, D]);       w_bo = din("w_bo", [1, D])
    w_rw = din("w_rw", [D]);          w_rb = din("w_rb", [1, 1])
    w_w1 = din("w_w1", [NE, D, DFF]); w_b1 = din("w_b1", [NE, DFF])
    w_w2 = din("w_w2", [NE, DFF, D]); w_b2 = din("w_b2", [NE, D])
    w_ln1g = din("w_ln1g", [D]);      w_ln1b = din("w_ln1b", [D])
    w_ln2g = din("w_ln2g", [D]);      w_ln2b = din("w_ln2b", [D])
    out = Buf("out", nc.dram_tensor("out", [OWN, D], F32,
                                    kind="ExternalOutput"), "dr")

    d_xnbf = b.dr("d_xnbf", [NKV, D], BF16)
    d_xn32 = b.dr("d_xn32", [OWN, D], F32)
    d_xhbf = b.dr("d_xhbf", [EXT, D], BF16)
    d_KT = b.dr("d_KT", [D, NKV], BF16)
    d_VT = b.dr("d_VT", [D, NKV], BF16)
    d_QT = b.dr("d_QT", [D, OWN], BF16)
    d_xscbf = b.dr("d_xscbf", [DI, EXT], BF16)
    d_xsc32 = b.dr("d_xsc32", [DI, OWN], F32)
    d_gs32 = b.dr("d_gs32", [DI, OWN], F32)
    d_x1bf = b.dr("d_x1bf", [OWN, D], BF16)
    d_x132 = b.dr("d_x132", [OWN, D], F32)
    d_g32 = b.dr("d_g32", [OWN, D], F32)

    P = dict(pers=True)
    ident_f = b.sb("ident_f", [128, 128], F32, **P)
    ident_b = b.sb("ident_b", [128, 128], BF16, **P)
    c_eps = b.sb("c_eps", [128, 1], F32, **P)
    c_nhalf = b.sb("c_nhalf", [128, 1], F32, **P)
    c_one = b.sb("c_one", [128, 1], F32, **P)
    c_oner = b.sb("c_oner", [1, 128], F32, **P)
    c_oneb = b.sb("c_oneb", [128, 1], BF16, **P)
    c_eighth = b.sb("c_eighth", [128, 1], F32, **P)
    c_i2048 = b.sb("c_i2048", [128, 1], F32, **P)
    hTb = b.sb("hTb", [NST, OWN], BF16, **P)
    idx_cols = b.sb("idx_cols", [128, 16], F32, **P)
    s_mpref = b.sb("s_mpref", [16, 1], F32, **P)
    s_mhalo = b.sb("s_mhalo", [128, 1], F32, **P)
    a_col = b.sb("a_col", [16, 1], F32, **P)
    NLN = 8
    ln_sets = []
    for si in range(NLN):
        ln_sets.append({n: b.sb(f"ln_{n}{si}", [128, 1], F32, **P)
                        for n in ("sum", "ssq", "mu", "e2", "ms", "var", "rs")})
    bq_c = b.sb("bq_c", [128, 8], F32, **P)
    bk_c = b.sb("bk_c", [128, 8], F32, **P)
    bv_c = b.sb("bv_c", [128, 8], F32, **P)
    bd_c = b.sb("bd_c", [128, 16], F32, **P)
    cb_c = b.sb("cb_c", [128, 16], F32, **P)
    dv_c = b.sb("dv_c", [128, 16], F32, **P)
    cw_c = [b.sb(f"cw{k}_c", [128, 16], F32, **P) for k in range(KCONV)]
    rb_c = b.sb("rb_c", [128, 1], F32, **P)
    xn_r_r = [b.sb(f"xn_rr{i}", [128, D], F32, **P) for i in range(2)]

    _ln_i = [0]

    def ln_tile_ops(x_t, out_t, width=D):
        s = ln_sets[_ln_i[0] % NLN]
        _ln_i[0] += 1
        L = {n: (bb, ALL) for n, bb in s.items()}
        o = []
        o.append(("op", "dve", lambda e, s=s, x_t=x_t: e.reduce_sum(
            out=s["sum"].h[:], in_=x_t.h[:], axis=AX),
            [(x_t, ALL)], [L["sum"]]))
        o.append(("op", "act", lambda e, s=s, x_t=x_t, out_t=out_t,
                  width=width: e.activation(
                      out=out_t.h[:, 0:width], in_=x_t.h[:], func=AF.Square,
                      accum_out=s["ssq"].h[:]),
                  [(x_t, ALL)], [(out_t, ALL), L["ssq"]]))
        o.append(("op", "dve", lambda e, s=s, width=width: e.tensor_scalar(
            out=s["mu"].h[:], in0=s["sum"].h[:], scalar1=1.0 / width,
            scalar2=None, op0=ALU.mult), [L["sum"]], [L["mu"]]))
        o.append(("op", "dve", lambda e, s=s, width=width: e.tensor_scalar(
            out=s["e2"].h[:], in0=s["ssq"].h[:], scalar1=1.0 / width,
            scalar2=None, op0=ALU.mult), [L["ssq"]], [L["e2"]]))
        o.append(("op", "dve", lambda e, s=s: e.tensor_tensor(
            out=s["ms"].h[:], in0=s["mu"].h[:], in1=s["mu"].h[:],
            op=ALU.mult), [L["mu"]], [L["ms"]]))
        o.append(("op", "dve", lambda e, s=s: e.tensor_sub(
            out=s["var"].h[:], in0=s["e2"].h[:], in1=s["ms"].h[:]),
            [L["e2"], L["ms"]], [L["var"]]))
        o.append(("op", "act", lambda e, s=s: e.activation(
            out=s["rs"].h[:], in_=s["var"].h[:], func=AF.Ln,
            bias=c_eps.h[:]), [L["var"], (c_eps, ALL)], [L["rs"]]))
        o.append(("op", "act", lambda e, s=s: e.activation(
            out=s["rs"].h[:], in_=s["rs"].h[:], func=AF.Exp,
            scale=c_nhalf.h[:]), [L["rs"], (c_nhalf, ALL)], [L["rs"]]))
        o.append(("op", "dve", lambda e, s=s, x_t=x_t, out_t=out_t:
                  e.tensor_scalar(
                      out=out_t.h[:], in0=x_t.h[:], scalar1=s["mu"].h[:],
                      scalar2=s["rs"].h[:], op0=ALU.subtract, op1=ALU.mult),
                  [(x_t, ALL), L["mu"], L["rs"]], [(out_t, ALL)]))
        return o

    def ln_tile(x_t, out_t, width=D):
        for (kind, eng, fn, r, w) in ln_tile_ops(x_t, out_t, width):
            b.op(eng, fn, r=r, w=w)
        return

    def _ln_tile_body(x_t, out_t, width=D):
        s = ln_sets[_ln_i[0] % NLN]
        _ln_i[0] += 1
        L = {n: (bb, ALL) for n, bb in s.items()}
        b.op("dve", lambda e: e.reduce_sum(out=s["sum"].h[:], in_=x_t.h[:],
                                           axis=AX),
             r=[(x_t, ALL)], w=[L["sum"]])
        b.op("act", lambda e: e.activation(out=out_t.h[:, 0:width],
                                           in_=x_t.h[:], func=AF.Square,
                                           accum_out=s["ssq"].h[:]),
             r=[(x_t, ALL)], w=[(out_t, ALL), L["ssq"]])
        b.op("dve", lambda e: e.tensor_scalar(
            out=s["mu"].h[:], in0=s["sum"].h[:], scalar1=1.0 / width,
            scalar2=None, op0=ALU.mult),
            r=[L["sum"]], w=[L["mu"]])
        b.op("dve", lambda e: e.tensor_scalar(
            out=s["e2"].h[:], in0=s["ssq"].h[:], scalar1=1.0 / width,
            scalar2=None, op0=ALU.mult),
            r=[L["ssq"]], w=[L["e2"]])
        b.op("dve", lambda e: e.tensor_tensor(
            out=s["ms"].h[:], in0=s["mu"].h[:], in1=s["mu"].h[:],
            op=ALU.mult),
            r=[L["mu"]], w=[L["ms"]])
        b.op("dve", lambda e: e.tensor_sub(out=s["var"].h[:],
                                           in0=s["e2"].h[:],
                                           in1=s["ms"].h[:]),
             r=[L["e2"], L["ms"]], w=[L["var"]])
        b.op("act", lambda e: e.activation(out=s["rs"].h[:], in_=s["var"].h[:],
                                           func=AF.Ln, bias=c_eps.h[:]),
             r=[L["var"], (c_eps, ALL)], w=[L["rs"]])
        b.op("act", lambda e: e.activation(out=s["rs"].h[:], in_=s["rs"].h[:],
                                           func=AF.Exp, scale=c_nhalf.h[:]),
             r=[L["rs"], (c_nhalf, ALL)], w=[L["rs"]])
        b.op("dve", lambda e: e.tensor_scalar(
            out=out_t.h[:], in0=x_t.h[:], scalar1=s["mu"].h[:],
            scalar2=s["rs"].h[:], op0=ALU.subtract, op1=ALU.mult),
            r=[(x_t, ALL), L["mu"], L["rs"]], w=[(out_t, ALL)])

    def bcast_row(src, dst, cols=D):
        if getattr(b, "_bc_row", None) is None:
            b._bc_row = b.sb(f"bcr_{dst.name}", [1, D], F32)
        row = b._bc_row
        if getattr(b, "_bc_ps", None) is None:
            b._bc_ps = b.ps(f"bcp_{dst.name}", [128, 512], F32)
        psb = b._bc_ps
        b.dma("sp", lambda e: e.dma_start(
            out=row.h[:], in_=src.h.ap().rearrange("(a d) -> a d", a=1)),
            r=[], w=[(row, ALL)])
        for j0 in range(0, cols, 512):
            n = min(512, cols - j0)
            b.op("pe", lambda e, j0=j0, n=n: e.matmul(
                psb.h[:, 0:n], c_oner.h[0:1, :], row.h[0:1, j0:j0 + n],
                start=True, stop=True),
                r=[(c_oner, ALL), (row, ALL)], w=[(psb, ALL)])
            b.op("dve", lambda e, j0=j0, n=n: e.tensor_copy(
                out=dst.h[:, j0:j0 + n], in_=psb.h[:, 0:n]),
                r=[(psb, ALL)], w=[(dst, j0)])

    ps_set = b.ps("ps_set", [128, 512], F32)

    def load_cols(dst, ap2d, nch):
        row = b.sb(f"lc_{dst.name}", [nch, 128], F32)
        b.dma("sp", lambda e: e.dma_start(out=row.h[:], in_=ap2d),
              r=[], w=[(row, ALL)])
        b.op("pe", lambda e: e.transpose(ps_set.h[:, 0:nch], row.h[:],
                                         ident_f.h[0:nch, 0:nch]),
             r=[(row, ALL), (ident_f, ALL)], w=[(ps_set, ALL)])
        b.op("dve", lambda e: e.tensor_copy(out=dst.h[:], in_=ps_set.h[:, 0:nch]),
             r=[(ps_set, ALL)], w=[(dst, ALL)])

    b.dma("sp", lambda e: e.dma_start(out=ident_f.h[:], in_=identf.h.ap()),
          w=[(ident_f, ALL)])
    b.dma("sp", lambda e: e.dma_start(out=ident_b.h[:], in_=identb.h.ap()),
          w=[(ident_b, ALL)])
    b.dma("sp", lambda e: e.dma_start(out=s_mpref.h[:], in_=mpref16.h.ap()),
          w=[(s_mpref, ALL)])
    b.dma("sp", lambda e: e.dma_start(out=s_mhalo.h[:], in_=mhalo.h.ap()),
          w=[(s_mhalo, ALL)])
    for buf, val in ((c_eps, EPS), (c_nhalf, -0.5), (c_one, 1.0),
                     (c_eighth, 0.125), (c_i2048, 1.0 / 2048)):
        b.op("dve", lambda e, buf=buf, val=val: e.memset(buf.h[:], val),
             w=[(buf, ALL)])
    b.op("dve", lambda e: e.memset(c_oner.h[:], 1.0), w=[(c_oner, ALL)])
    b.op("dve", lambda e: e.memset(c_oneb.h[:], 1.0), w=[(c_oneb, ALL)])

    load_cols(bq_c, w_bq.h.ap().rearrange("(c p) -> c p", p=128), 8)
    load_cols(bk_c, w_bk.h.ap().rearrange("(c p) -> c p", p=128), 8)
    load_cols(bv_c, w_bv.h.ap().rearrange("(c p) -> c p", p=128), 8)
    load_cols(bd_c, w_bd.h.ap().rearrange("(c p) -> c p", p=128), 16)
    load_cols(cb_c, w_convb.h.ap().rearrange("(c p) -> c p", p=128), 16)
    load_cols(dv_c, w_dvec.h.ap().rearrange("(c p) -> c p", p=128), 16)
    for k in range(KCONV):
        load_cols(cw_c[k],
                  w_convw.h[k:k + 1, :].rearrange("a (c p) -> (a c) p", p=128),
                  16)
    rb_row = b.sb("rb_row", [1, 1], F32)
    b.dma("sp", lambda e: e.dma_start(out=rb_row.h[:], in_=w_rb.h.ap()),
          w=[(rb_row, ALL)])
    b.op("pe", lambda e: e.matmul(ps_set.h[:, 0:1], c_oner.h[0:1, :],
                                  rb_row.h[:], start=True, stop=True),
         r=[(c_oner, ALL), (rb_row, ALL)], w=[(ps_set, ALL)])
    b.op("dve", lambda e: e.tensor_copy(out=rb_c.h[:], in_=ps_set.h[:, 0:1]),
         r=[(ps_set, ALL)], w=[(rb_c, ALL)])
    alog_r = b.sb("alog_r", [1, NST], F32)
    b.dma("sp", lambda e: e.dma_start(out=alog_r.h[:], in_=w_alog.h.ap()),
          w=[(alog_r, ALL)])
    b.op("pe", lambda e: e.transpose(ps_set.h[0:NST, 0:1], alog_r.h[:],
                                     ident_f.h[0:1, 0:1]),
         r=[(alog_r, ALL), (ident_f, ALL)], w=[(ps_set, ALL)])
    b.op("act", lambda e: e.activation(out=a_col.h[:], in_=ps_set.h[0:NST, 0:1],
                                       func=AF.Exp),
         r=[(ps_set, ALL)], w=[(a_col, ALL)])
    b.op("dve", lambda e: e.tensor_scalar(
        out=a_col.h[:], in0=a_col.h[:], scalar1=-1.0 / 2048, scalar2=None,
        op0=ALU.mult), r=[(a_col, ALL)], w=[(a_col, ALL)])

    wq_sb = b.sb("wq_sb", [128, 8 * D], BF16)
    wk_sb = b.sb("wk_sb", [128, 8 * D], BF16)
    wv_sb = b.sb("wv_sb", [128, 8 * D], BF16)
    for (wsb, wdr) in ((wq_sb, w_wq), (wk_sb, w_wk), (wv_sb, w_wv)):
        for dc in range(8):
            b.dma("pool", lambda e, wsb=wsb, wdr=wdr, dc=dc: e.dma_start(
                out=wsb.h[:, dc * D:(dc + 1) * D],
                in_=wdr.h[dc * 128:(dc + 1) * 128, :]), w=[(wsb, dc)])
    p1_x = [b.sb(f"p1_x{i}", [128, D], F32) for i in range(4)]
    p1_xn = [b.sb(f"p1_xn{i}", [128, D], F32) for i in range(4)]
    xnfm = [b.sb(f"xnfm{i}", [128, 8 * 512], BF16) for i in range(2)]
    kvq_ps = b.ps("kvq_ps", [128, 512], F32)
    kvq_ps2 = b.ps("kvq_ps2", [128, 512], F32)
    kvq_ev = [b.sb(f"kvq_ev{i}", [128, 512], BF16) for i in range(2)]

    def p1_ln_ops(i):
        own = i < 16
        src = xs_ext.h if own else xkv.h
        r0 = (PRE + i * 128) if own else (i - 16) * 128
        xt, xnt = p1_x[i % 4], p1_xn[i % 4]
        o = [("dma", "sp", lambda e, src=src, r0=r0, xt=xt: e.dma_start(
            out=xt.h[:], in_=src[r0:r0 + 128, :]), [], [(xt, ALL)])]
        o += ln_tile_ops(xt, xnt)
        if own:
            o.append(("dma", "pool", lambda e, i=i, xnt=xnt: e.dma_start(
                out=d_xn32.h[i * 128:(i + 1) * 128, :], in_=xnt.h[:]),
                [(xnt, ALL)], [(d_xn32, i)]))
        o.append(("dma", "pool", lambda e, i=i, xnt=xnt: e.dma_start(
            out=d_xnbf.h[i * 128:(i + 1) * 128, :], in_=xnt.h[:]),
            [(xnt, ALL)], [(d_xnbf, i)]))
        return o

    b.wave([p1_ln_ops(i) for i in range(4)])
    for tt in range(8):
        xf = xnfm[tt % 2]
        for dc in range(8):
            b.dma("act", lambda e, tt=tt, dc=dc, xf=xf: e.dma_start(
                out=xf.h[:, dc * 512:(dc + 1) * 512],
                in_=d_xnbf.h[tt * 512:(tt + 1) * 512,
                             dc * 128:(dc + 1) * 128], transpose=True),
                r=[(d_xnbf, ALL)], w=[(xf, dc)])
        grp = [p1_ln_ops(i) for i in range(4 * tt + 4, min(4 * tt + 8, 32))]
        if grp:
            b.wave(grp)
        plan = [(wk_sb, bk_c, d_KT), (wv_sb, bv_c, d_VT)]
        if tt < 4:
            plan.append((wq_sb, bq_c, d_QT))
        for (wsb, bcols, ddst) in plan:
            for oc in range(8):
                ps = kvq_ps if oc % 2 == 0 else kvq_ps2
                ev = kvq_ev[oc % 2]
                for dc in range(8):
                    b.op("pe", lambda e, ps=ps, wsb=wsb, oc=oc, dc=dc, xf=xf:
                         e.matmul(
                             ps.h[:],
                             wsb.h[:, dc * D + oc * 128:dc * D + (oc + 1) * 128],
                             xf.h[:, dc * 512:(dc + 1) * 512],
                             start=(dc == 0), stop=(dc == 7)),
                         r=[(wsb, dc), (xf, dc)], w=[(ps, ALL)])
                b.op("act", lambda e, ps=ps, bcols=bcols, oc=oc, ev=ev:
                     e.activation(
                         out=ev.h[:], in_=ps.h[:], func=AF.Identity,
                         bias=bcols.h[:, oc:oc + 1]),
                     r=[(ps, ALL), (bcols, ALL)], w=[(ev, ALL)])
                b.dma("sp", lambda e, ddst=ddst, oc=oc, tt=tt, ev=ev:
                      e.dma_start(
                          out=ddst.h[oc * 128:(oc + 1) * 128,
                                     tt * 512:(tt + 1) * 512], in_=ev.h[:]),
                      r=[(ev, ALL)], w=[(ddst, (oc, tt))])

    if stage < 2:
        _tail(b, nc, out, [d_xnbf, d_xn32, d_KT, d_VT, d_QT])
        return

    b.barrier()
    gb_g = b.sb("gb_g", [128, D], F32)
    gb_b = b.sb("gb_b", [128, D], F32)
    bcast_row(w_lng, gb_g)
    bcast_row(w_lnb, gb_b)
    wxs_sb = b.sb("wxs_sb", [128, 8 * DI], BF16)
    wg_sb = b.sb("wg_sb", [128, 8 * DI], BF16)
    for dc in range(8):
        b.dma("pool", lambda e, dc=dc: e.dma_start(
            out=wxs_sb.h[:, dc * DI:(dc + 1) * DI],
            in_=w_in.h[dc * 128:(dc + 1) * 128, 0:DI]), w=[(wxs_sb, dc)])
        b.dma("pool", lambda e, dc=dc: e.dma_start(
            out=wg_sb.h[:, dc * DI:(dc + 1) * DI],
            in_=w_in.h[dc * 128:(dc + 1) * 128, DI:2 * DI]), w=[(wg_sb, dc)])
    p2_x = [b.sb(f"p2_x{i}", [128, D], F32) for i in range(4)]
    p2_xn = [b.sb(f"p2_xn{i}", [128, D], F32) for i in range(4)]
    p2_xh = [b.sb(f"p2_xh{i}", [128, D], F32) for i in range(4)]

    def p2_ln_ops(i):
        xt, xnt, xht = p2_x[i % 4], p2_xn[i % 4], p2_xh[i % 4]
        o = [("dma", "sp", lambda e, i=i, xt=xt: e.dma_start(
            out=xt.h[:], in_=xs_ext.h[i * 128:(i + 1) * 128, :]),
            [], [(xt, ALL)])]
        o += ln_tile_ops(xt, xnt)
        o += ln_tile_ops(xnt, xht)
        o.append(("op", "dve", lambda e, xht=xht: e.tensor_tensor(
            out=xht.h[:], in0=xht.h[:], in1=gb_g.h[:], op=ALU.mult),
            [(xht, ALL), (gb_g, ALL)], [(xht, ALL)]))
        o.append(("op", "pool", lambda e, xht=xht: e.tensor_tensor(
            out=xht.h[:], in0=xht.h[:], in1=gb_b.h[:], op=ALU.add),
            [(xht, ALL), (gb_b, ALL)], [(xht, ALL)]))
        o.append(("dma", "pool", lambda e, i=i, xht=xht: e.dma_start(
            out=d_xhbf.h[i * 128:(i + 1) * 128, :], in_=xht.h[:]),
            [(xht, ALL)], [(d_xhbf, i)]))
        return o

    b.wave([p2_ln_ops(i) for i in range(4)])
    xhfm = b.sb("xhfm", [128, 8 * 512], BF16)
    xsp_r = [b.sb(f"xsp{i}", [128, 515], F32) for i in range(2)]
    xsp_tail = b.sb("xsp_tail", [128, 16 * 3], F32)
    win_ps = b.ps("win_ps", [128, 512], F32)
    win_ps2 = b.ps("win_ps2", [128, 512], F32)
    conv_ps = b.ps("conv_ps", [128, 512], F32)
    conv_ps2 = b.ps("conv_ps2", [128, 512], F32)
    xsc_t_r = [b.sb(f"xsc_t{i}", [128, 512], F32) for i in range(2)]
    gt_t_r = [b.sb(f"gt_t{i}", [128, 512], F32) for i in range(2)]
    cwd = [b.sb(f"cwd{k}", [128, 16 * 128], F32) for k in range(KCONV)]
    for k in range(KCONV):
        for oc in range(16):
            b.op("dve", lambda e, k=k, oc=oc: e.tensor_scalar(
                out=cwd[k].h[:, oc * 128:(oc + 1) * 128], in0=ident_f.h[:],
                scalar1=cw_c[k].h[:, oc:oc + 1], scalar2=None, op0=ALU.mult),
                r=[(ident_f, ALL), (cw_c[k], ALL)], w=[(cwd[k], oc)])
    b.op("dve", lambda e: e.memset(xsp_tail.h[:], 0.0), w=[(xsp_tail, ALL)])
    for tt in range(8):
        for dc in range(8):
            b.dma("act", lambda e, tt=tt, dc=dc: e.dma_start(
                out=xhfm.h[:, dc * 512:(dc + 1) * 512],
                in_=d_xhbf.h[tt * 512:(tt + 1) * 512,
                             dc * 128:(dc + 1) * 128], transpose=True),
                r=[(d_xhbf, ALL)], w=[(xhfm, dc)])
        grp = [p2_ln_ops(i) for i in range(4 * tt + 4, min(4 * tt + 8, 32))]
        if grp:
            b.wave(grp)
        def p2a_A(oc):
            ps = win_ps if oc % 2 == 0 else win_ps2
            for dc in range(8):
                b.op("pe", lambda e, ps=ps, oc=oc, dc=dc: e.matmul(
                    ps.h[:],
                    wxs_sb.h[:, dc * DI + oc * 128:dc * DI + (oc + 1) * 128],
                    xhfm.h[:, dc * 512:(dc + 1) * 512],
                    start=(dc == 0), stop=(dc == 7)),
                    r=[(wxs_sb, dc), (xhfm, dc)], w=[(ps, ALL)])

        def p2a_B(oc, tt):
            ps = win_ps if oc % 2 == 0 else win_ps2
            cps = conv_ps if oc % 2 == 0 else conv_ps2
            xsp = xsp_r[oc % 2]
            b.op("act", lambda e, ps=ps, xsp=xsp: e.activation(
                out=xsp.h[:, 3:515], in_=ps.h[:], func=AF.Identity),
                r=[(ps, ALL)], w=[(xsp, "b")])
            if tt == 3:
                b.op("dve", lambda e, xsp=xsp: e.tensor_scalar(
                    out=xsp.h[:, 512:515], in0=xsp.h[:, 512:515],
                    scalar1=s_mhalo.h[:], scalar2=None, op0=ALU.mult),
                    r=[(xsp, "b"), (s_mhalo, ALL)], w=[(xsp, "b")])
            b.op("dve", lambda e, oc=oc, xsp=xsp: e.tensor_copy(
                out=xsp.h[:, 0:3], in_=xsp_tail.h[:, oc * 3:(oc + 1) * 3]),
                r=[(xsp_tail, oc)], w=[(xsp, "h")])
            b.op("dve", lambda e, oc=oc, xsp=xsp: e.tensor_copy(
                out=xsp_tail.h[:, oc * 3:(oc + 1) * 3],
                in_=xsp.h[:, 512:515]),
                r=[(xsp, "b")], w=[(xsp_tail, oc)])
            xsc_t = xsc_t_r[oc % 2]
            for k in range(KCONV):
                b.op("pe", lambda e, oc=oc, k=k, cps=cps, xsp=xsp: e.matmul(
                    cps.h[:], cwd[k].h[:, oc * 128:(oc + 1) * 128],
                    xsp.h[:, k:k + 512],
                    start=(k == 0), stop=(k == KCONV - 1)),
                    r=[(cwd[k], oc), (xsp, "b"), (xsp, "h")],
                    w=[(cps, ALL)])
            b.op("act", lambda e, oc=oc, cps=cps, xsc_t=xsc_t: e.activation(
                out=xsc_t.h[:], in_=cps.h[:], func=AF.Silu,
                bias=cb_c.h[:, oc:oc + 1]),
                r=[(cps, ALL), (cb_c, ALL)], w=[(xsc_t, ALL)])
            b.dma("pool", lambda e, tt=tt, oc=oc, xsc_t=xsc_t: e.dma_start(
                out=d_xscbf.h[oc * 128:(oc + 1) * 128,
                              tt * 512:(tt + 1) * 512], in_=xsc_t.h[:]),
                r=[(xsc_t, ALL)], w=[(d_xscbf, (oc, tt))])
            if tt >= 4:
                b.dma("sp", lambda e, tt=tt, oc=oc, xsc_t=xsc_t: e.dma_start(
                    out=d_xsc32.h[oc * 128:(oc + 1) * 128,
                                  (tt - 4) * 512:(tt - 3) * 512],
                    in_=xsc_t.h[:]),
                    r=[(xsc_t, ALL)], w=[(d_xsc32, (oc, tt))])

        p2a_A(0)
        for oc in range(16):
            if oc < 15:
                p2a_A(oc + 1)
            p2a_B(oc, tt)
        if tt >= 4:
            for oc in range(16):
                ps = win_ps if oc % 2 == 0 else win_ps2
                for dc in range(8):
                    b.op("pe", lambda e, ps=ps, oc=oc, dc=dc: e.matmul(
                        ps.h[:],
                        wg_sb.h[:, dc * DI + oc * 128:dc * DI + (oc + 1) * 128],
                        xhfm.h[:, dc * 512:(dc + 1) * 512],
                        start=(dc == 0), stop=(dc == 7)),
                        r=[(wg_sb, dc), (xhfm, dc)], w=[(ps, ALL)])
                gt_t = gt_t_r[oc % 2]
                b.op("act", lambda e, ps=ps, gt_t=gt_t: e.activation(
                    out=gt_t.h[:], in_=ps.h[:], func=AF.Silu),
                    r=[(ps, ALL)], w=[(gt_t, ALL)])
                b.dma("sp", lambda e, tt=tt, oc=oc, gt_t=gt_t: e.dma_start(
                    out=d_gs32.h[oc * 128:(oc + 1) * 128,
                                 (tt - 4) * 512:(tt - 3) * 512],
                    in_=gt_t.h[:]),
                    r=[(gt_t, ALL)], w=[(d_gs32, (oc, tt))])

    b.barrier()
    wd_sb = b.sb("wd_sb", [128, 16 * DI], BF16)
    for dc in range(16):
        b.dma("pool", lambda e, dc=dc: e.dma_start(
            out=wd_sb.h[:, dc * DI:(dc + 1) * DI],
            in_=w_wd.h[dc * 128:(dc + 1) * 128, :]), w=[(wd_sb, dc)])
    bmat_sb = b.sb("bmat_sb", [NST, DI], F32)
    bt_sb = b.sb("bt_sb", [128, 16 * NST], BF16)
    ps_bt = b.ps("ps_bt", [128, 512], F32)
    b.dma("sp", lambda e: e.dma_start(out=bmat_sb.h[:], in_=w_bmat.h.ap()),
          w=[(bmat_sb, ALL)])
    for dc in range(16):
        b.op("pe", lambda e, dc=dc: e.transpose(
            ps_bt.h[:, 0:NST], bmat_sb.h[:, dc * 128:(dc + 1) * 128],
            ident_f.h[0:NST, 0:NST]),
            r=[(bmat_sb, ALL), (ident_f, ALL)], w=[(ps_bt, ALL)])
        b.op("act", lambda e, dc=dc: e.activation(
            out=bt_sb.h[:, dc * NST:(dc + 1) * NST], in_=ps_bt.h[:, 0:NST],
            func=AF.Identity, scale=c_i2048.h[:]),
            r=[(ps_bt, ALL), (c_i2048, ALL)], w=[(bt_sb, dc)])

    xscb_r = [b.sb(f"xscb{i}", [128, 16 * 512], BF16) for i in range(2)]
    z_ps = b.ps("z_ps", [128, 512], F32)
    z_ps2 = b.ps("z_ps2", [128, 512], F32)
    ez_t_r = [b.sb(f"ez_t{i}", [128, 512], F32) for i in range(2)]
    spz_t_r = [b.sb(f"spz_t{i}", [128, 512], F32) for i in range(2)]
    del_ps = b.ps("del_ps", [1, 512], F32)
    bx_ps = b.ps("bx_ps", [NST, 512], F32)
    db_ps = b.ps("db_ps", [NST, 512], F32)
    del_row_r = [b.sb(f"del_row{i}", [1, 512], F32) for i in range(2)]
    db_sb_r = [b.sb(f"db_sb{i}", [NST, 512], F32) for i in range(2)]
    dA = b.sb("dA", [NST, EXT], F32)
    BxT = b.sb("BxT", [NST, EXT], F32)
    hT = b.sb("hT", [NST, EXT], F32)

    for tt in range(8):
        xscb = xscb_r[tt % 2]
        for dc in range(16):
            b.dma("sp", lambda e, tt=tt, dc=dc, xscb=xscb: e.dma_start(
                out=xscb.h[:, dc * 512:(dc + 1) * 512],
                in_=d_xscbf.h[dc * 128:(dc + 1) * 128,
                              tt * 512:(tt + 1) * 512]),
                r=[(d_xscbf, (dc, tt))], w=[(xscb, dc)])
        del_row = del_row_r[tt % 2]
        db_sb = db_sb_r[tt % 2]

        def p2b_A(zc):
            ps = z_ps if zc % 2 == 0 else z_ps2
            for dc in range(16):
                b.op("pe", lambda e, ps=ps, zc=zc, dc=dc, xscb=xscb: e.matmul(
                    ps.h[:],
                    wd_sb.h[:, dc * DI + zc * 128:dc * DI + (zc + 1) * 128],
                    xscb.h[:, dc * 512:(dc + 1) * 512],
                    start=(dc == 0), stop=(dc == 15)),
                    r=[(wd_sb, dc), (xscb, dc)], w=[(ps, ALL)])

        def p2b_B(zc):
            ps = z_ps if zc % 2 == 0 else z_ps2
            ez_t, spz_t = ez_t_r[zc % 2], spz_t_r[zc % 2]
            b.op("act", lambda e, ps=ps, zc=zc, ez_t=ez_t: e.activation(
                out=ez_t.h[:], in_=ps.h[:], func=AF.Exp,
                bias=bd_c.h[:, zc:zc + 1]),
                r=[(ps, ALL), (bd_c, ALL)], w=[(ez_t, ALL)])
            b.op("act", lambda e, ez_t=ez_t, spz_t=spz_t: e.activation(
                out=spz_t.h[:], in_=ez_t.h[:], func=AF.Ln, bias=c_one.h[:]),
                r=[(ez_t, ALL), (c_one, ALL)], w=[(spz_t, ALL)])
            b.op("pe", lambda e, zc=zc, spz_t=spz_t: e.matmul(
                del_ps.h[:], c_one.h[:, 0:1], spz_t.h[:],
                start=(zc == 0), stop=(zc == 15)),
                r=[(c_one, ALL), (spz_t, ALL)], w=[(del_ps, ALL)])

        p2b_A(0)
        for zc in range(16):
            if zc < 15:
                p2b_A(zc + 1)
            p2b_B(zc)
        b.op("dve", lambda e, del_row=del_row: e.tensor_copy(
            out=del_row.h[:], in_=del_ps.h[:]),
            r=[(del_ps, ALL)], w=[(del_row, ALL)])
        for dc in range(16):
            b.op("pe", lambda e, dc=dc, xscb=xscb: e.matmul(
                bx_ps.h[:], bt_sb.h[:, dc * NST:(dc + 1) * NST],
                xscb.h[:, dc * 512:(dc + 1) * 512],
                start=(dc == 0), stop=(dc == 15)),
                r=[(bt_sb, dc), (xscb, dc)], w=[(bx_ps, ALL)])
        b.op("pe", lambda e, del_row=del_row: e.matmul(
            db_ps.h[:], c_oner.h[0:1, 0:NST], del_row.h[:],
            start=True, stop=True),
            r=[(c_oner, ALL), (del_row, ALL)], w=[(db_ps, ALL)])
        b.op("dve", lambda e, db_sb=db_sb: e.tensor_copy(
            out=db_sb.h[:], in_=db_ps.h[:]),
            r=[(db_ps, ALL)], w=[(db_sb, ALL)])
        b.op("act", lambda e, tt=tt, db_sb=db_sb: e.activation(
            out=dA.h[:, tt * 512:(tt + 1) * 512], in_=db_sb.h[:],
            func=AF.Exp, scale=a_col.h[:]),
            r=[(db_sb, ALL), (a_col, ALL)], w=[(dA, tt)])
        b.op("dve", lambda e, tt=tt, db_sb=db_sb: e.tensor_tensor(
            out=BxT.h[:, tt * 512:(tt + 1) * 512], in0=bx_ps.h[:],
            in1=db_sb.h[:], op=ALU.mult),
            r=[(bx_ps, ALL), (db_sb, ALL)], w=[(BxT, tt)])
        if tt < 4:
            b.op("dve", lambda e, tt=tt: e.tensor_scalar(
                out=BxT.h[:, tt * 512:(tt + 1) * 512],
                in0=BxT.h[:, tt * 512:(tt + 1) * 512],
                scalar1=s_mpref.h[:], scalar2=None, op0=ALU.mult),
                r=[(BxT, tt), (s_mpref, ALL)], w=[(BxT, tt)])

    b.op("dve", lambda e: e.tensor_tensor_scan(
        out=hT.h[:], data0=dA.h[:], data1=BxT.h[:], initial=0.0,
        op0=ALU.mult, op1=ALU.add),
        r=[(dA, ALL)] + [(dA, t) for t in range(8)] +
          [(BxT, ALL)] + [(BxT, t) for t in range(8)],
        w=[(hT, ALL)])
    b.op("dve", lambda e: e.tensor_copy(out=hTb.h[:], in_=hT.h[:, PRE:]),
         r=[(hT, ALL)], w=[(hTb, ALL)])
    if DEBUG:
        d_hT = b.dr("d_hT", [NST, EXT], F32)
        d_dA = b.dr("d_dA", [NST, EXT], F32)
        d_BxT = b.dr("d_BxT", [NST, EXT], F32)
        b.dma("pool", lambda e: e.dma_start(out=d_hT.h[:], in_=hT.h[:]),
              r=[(hT, ALL)], w=[(d_hT, ALL)])
        b.dma("pool", lambda e: e.dma_start(out=d_dA.h[:], in_=dA.h[:]),
              r=[(dA, ALL)] + [(dA, t) for t in range(8)], w=[(d_dA, ALL)])
        b.dma("pool", lambda e: e.dma_start(out=d_BxT.h[:], in_=BxT.h[:]),
              r=[(BxT, ALL)] + [(BxT, t) for t in range(8)], w=[(d_BxT, ALL)])
        globals()["_dbg_p2"] = [d_hT, d_dA, d_BxT]

    if stage < 3:
        _tail(b, nc, out, [d_xhbf, d_xscbf, d_xsc32, d_gs32]
              + globals().get("_dbg_p2", []))
        return

    b.barrier()
    g1_g = b.sb("g1_g", [128, D], F32)
    g1_b = b.sb("g1_b", [128, D], F32)
    rw_b = b.sb("rw_b", [128, D], F32)
    psA = b.ps("psA", [128, 1024], F32)
    psB = b.ps("psB", [128, 1024], F32)
    b._bc_ps = Buf("psA_h", psA.h, "ps")
    b._bc_ps = psA
    bcast_row(w_ln1g, g1_g)
    bcast_row(w_ln1b, g1_b)
    bcast_row(w_rw, rw_b)
    ktp = [b.sb(f"ktp{i}", [128, NKV], BF16) for i in range(2)]
    vtp = [b.sb(f"vtp{i}", [128, NKV], F32) for i in range(2)]
    qtp = [b.sb(f"qtp{i}", [128, OWN], BF16) for i in range(2)]
    vtok = [b.sb(f"vtok{i}", [128, 32 * 65], BF16) for i in range(4)]
    ptA = [b.sb(f"ptA{i}", [128, 1024], BF16) for i in range(2)]
    ptB = [b.sb(f"ptB{i}", [128, 1024], BF16) for i in range(2)]
    qsq_s = b.sb("qsq_s", [128, NKV], BF16)
    qsq = [qsq_s, qsq_s]
    r2row = [b.sb(f"r2row{i}", [1, 8], F32) for i in range(2)]
    m_sc = [b.sb(f"m_sc{i}", [1, 8], F32) for i in range(2)]
    negm = [b.sb(f"negm{i}", [128, 2], F32) for i in range(2)]
    oA = b.ps("oA", [65, 1024], F32)
    oB = b.ps("oB", [65, 1024], F32)
    lrec = b.sb("lrec", [1, 1024], F32)
    lrecb = b.sb("lrecb", [64, 1024], F32)
    attnT = b.sb("attnT", [128, 8 * OWN], BF16)

    for i in range(4):
        b.op("dve", lambda e, i=i: e.memset(vtok[i].h[:], 1.0),
             w=[(vtok[i], ALL)])

    def p3_loads(hp):
        bi = hp % 2
        b.dma("sp", lambda e, hp=hp, bi=bi: e.dma_start(
            out=ktp[bi].h[:], in_=d_KT.h[hp * 128:(hp + 1) * 128, :]),
            r=[(d_KT, ALL)], w=[(ktp[bi], ALL)])
        b.dma("pool", lambda e, hp=hp, bi=bi: e.dma_start(
            out=vtp[bi].h[:], in_=d_VT.h[hp * 128:(hp + 1) * 128, :]),
            r=[(d_VT, ALL)], w=[(vtp[bi], ALL)])
        b.dma("sp", lambda e, hp=hp, bi=bi: e.dma_start(
            out=qtp[bi].h[:], in_=d_QT.h[hp * 128:(hp + 1) * 128, :]),
            r=[(d_QT, ALL)], w=[(qtp[bi], ALL)])

    def p3_setup(hp):
        bi = hp % 2
        kb, vb, qb = ktp[bi], vtp[bi], qtp[bi]
        qs, r2, ms, nm = qsq[bi], r2row[bi], m_sc[bi], negm[bi]
        b.op("act", lambda e, qs=qs, qb=qb: e.activation(
            out=qs.h[:, 0:OWN], in_=qb.h[:], func=AF.Square),
            r=[(qb, ALL)], w=[(qs, ALL)])
        for hi in range(2):
            p0 = hi * 64
            for j in range(4):
                b.op("pe", lambda e, j=j, qs=qs, p0=p0, hi=hi: e.matmul(
                    psA.h[0:1, hi * 512:hi * 512 + 512],
                    c_oneb.h[p0:p0 + 64, 0:1],
                    qs.h[p0:p0 + 64, j * 512:(j + 1) * 512],
                    start=(j == 0), stop=(j == 3)),
                    r=[(c_oneb, ALL), (qs, ALL)], w=[(psA, ALL)])
        for hi in range(2):
            b.op("dve", lambda e, r2=r2, hi=hi: e.reduce_max(
                out=r2.h[0:1, hi:hi + 1],
                in_=psA.h[0:1, hi * 512:hi * 512 + 512], axis=AX),
                r=[(psA, ALL)], w=[(r2, ("q", hi))])
        b.op("act", lambda e, qs=qs, kb=kb: e.activation(
            out=qs.h[:], in_=kb.h[:], func=AF.Square),
            r=[(kb, ALL), (qs, ALL)], w=[(qs, ALL)])
        for hi in range(2):
            p0 = hi * 64
            for j in range(8):
                b.op("pe", lambda e, j=j, qs=qs, p0=p0, hi=hi: e.matmul(
                    psA.h[0:1, hi * 512:hi * 512 + 512],
                    c_oneb.h[p0:p0 + 64, 0:1],
                    qs.h[p0:p0 + 64, j * 512:(j + 1) * 512],
                    start=(j == 0), stop=(j == 7)),
                    r=[(c_oneb, ALL), (qs, ALL)], w=[(psA, ALL)])
        for hi in range(2):
            b.op("dve", lambda e, r2=r2, hi=hi: e.reduce_max(
                out=r2.h[0:1, 2 + hi:3 + hi],
                in_=psA.h[0:1, hi * 512:hi * 512 + 512], axis=AX),
                r=[(psA, ALL)], w=[(r2, ("k", hi))])
        for hi in range(2):
            b.op("dve", lambda e, r2=r2, ms=ms, hi=hi: e.tensor_tensor(
                out=ms.h[0:1, hi:hi + 1], in0=r2.h[0:1, hi:hi + 1],
                in1=r2.h[0:1, 2 + hi:3 + hi], op=ALU.mult),
                r=[(r2, ("q", hi)), (r2, ("k", hi))], w=[(ms, ("p", hi))])
            b.op("act", lambda e, ms=ms, hi=hi: e.activation(
                out=ms.h[0:1, 2 + hi:3 + hi], in_=ms.h[0:1, hi:hi + 1],
                func=AF.Ln), r=[(ms, ("p", hi))], w=[(ms, ("l", hi))])
            b.op("act", lambda e, ms=ms, hi=hi: e.activation(
                out=ms.h[0:1, 4 + hi:5 + hi], in_=ms.h[0:1, 2 + hi:3 + hi],
                func=AF.Exp, scale=0.5),
                r=[(ms, ("l", hi))], w=[(ms, ("e", hi))])
            b.op("dve", lambda e, ms=ms, hi=hi: e.tensor_scalar(
                out=ms.h[0:1, 6 + hi:7 + hi], in0=ms.h[0:1, 4 + hi:5 + hi],
                scalar1=-0.125, scalar2=None, op0=ALU.mult),
                r=[(ms, ("e", hi))], w=[(ms, ("m", hi))])
            b.op("pe", lambda e, ms=ms, hi=hi: e.matmul(
                psA.h[:, hi:hi + 1], c_oner.h[0:1, :],
                ms.h[0:1, 6 + hi:7 + hi], start=True, stop=True),
                r=[(c_oner, ALL), (ms, ("m", hi))], w=[(psA, ALL)])
            b.op("dve", lambda e, nm=nm, hi=hi: e.tensor_copy(
                out=nm.h[:, hi:hi + 1], in_=psA.h[:, hi:hi + 1]),
                r=[(psA, ALL)], w=[(nm, hi)])
        for hi in range(2):
            vk = vtok[hp % 2 * 2 + hi]
            p0 = hi * 64
            for kt in range(32):
                b.op("pe", lambda e, kt=kt, vb=vb, p0=p0: e.transpose(
                    psA.h[:, 0:64], vb.h[p0:p0 + 64, kt * 128:(kt + 1) * 128],
                    ident_f.h[p0:p0 + 64, p0:p0 + 64]),
                    r=[(vb, ALL), (ident_f, ALL)], w=[(psA, ALL)])
                b.op("dve", lambda e, kt=kt, vk=vk: e.tensor_copy(
                    out=vk.h[:, kt * 65:kt * 65 + 64], in_=psA.h[:, 0:64]),
                    r=[(psA, ALL)], w=[(vk, kt)])

    def p3_qh(hp, qh):
        bi = hp % 2
        kb, qb, nm = ktp[bi], qtp[bi], negm[bi]
        vkA, vkB = vtok[bi * 2], vtok[bi * 2 + 1]
        ch, hwA, hwB = hp // 1, 0, 64
        chA, rwA = (2 * hp) // 2, ((2 * hp) % 2) * 64
        chB, rwB = (2 * hp + 1) // 2, ((2 * hp + 1) % 2) * 64
        q0 = qh * 1024

        def att_S(kt):
            for j in range(2):
                b.op("pe", lambda e, kt=kt, j=j, kb=kb, qb=qb: e.matmul(
                    psA.h[:, j * 512:(j + 1) * 512],
                    kb.h[0:64, kt * 128:(kt + 1) * 128],
                    qb.h[0:64, q0 + j * 512:q0 + (j + 1) * 512],
                    start=True, stop=True, tile_position=(0, 0)),
                    r=[(kb, ALL), (qb, ALL)], w=[(psA, ALL)])
            for j in range(2):
                b.op("pe", lambda e, kt=kt, j=j, kb=kb, qb=qb: e.matmul(
                    psB.h[:, j * 512:(j + 1) * 512],
                    kb.h[64:128, kt * 128:(kt + 1) * 128],
                    qb.h[64:128, q0 + j * 512:q0 + (j + 1) * 512],
                    start=True, stop=True, tile_position=(64, 0)),
                    r=[(kb, ALL), (qb, ALL)], w=[(psB, ALL)])

        att_S(0)
        for kt in range(32):
            pA, pB = ptA[kt % 2], ptB[kt % 2]
            b.op("act", lambda e, pA=pA, nm=nm: e.activation(
                out=pA.h[:], in_=psA.h[:], func=AF.Exp,
                scale=c_eighth.h[:], bias=nm.h[:, 0:1]),
                r=[(psA, ALL), (c_eighth, ALL), (nm, 0)], w=[(pA, ALL)])
            b.op("act", lambda e, pB=pB, nm=nm: e.activation(
                out=pB.h[:], in_=psB.h[:], func=AF.Exp,
                scale=c_eighth.h[:], bias=nm.h[:, 1:2]),
                r=[(psB, ALL), (c_eighth, ALL), (nm, 1)], w=[(pB, ALL)])
            if kt < 31:
                att_S(kt + 1)
            for j in range(2):
                b.op("pe", lambda e, pA=pA, kt=kt, j=j, vkA=vkA: e.matmul(
                    oA.h[:, j * 512:(j + 1) * 512],
                    vkA.h[:, kt * 65:(kt + 1) * 65],
                    pA.h[:, j * 512:(j + 1) * 512],
                    start=(kt == 0), stop=(kt == 31)),
                    r=[(vkA, kt), (pA, ALL)], w=[(oA, j)])
                b.op("pe", lambda e, pB=pB, kt=kt, j=j, vkB=vkB: e.matmul(
                    oB.h[:, j * 512:(j + 1) * 512],
                    vkB.h[:, kt * 65:(kt + 1) * 65],
                    pB.h[:, j * 512:(j + 1) * 512],
                    start=(kt == 0), stop=(kt == 31)),
                    r=[(vkB, kt), (pB, ALL)], w=[(oB, j)])
        for (ob, chx, rwx) in ((oA, chA, rwA), (oB, chB, rwB)):
            b.op("dve", lambda e, ob=ob: e.reciprocal(
                out=lrec.h[:], in_=ob.h[64:65, :]),
                r=[(ob, 0), (ob, 1)], w=[(lrec, ALL)])
            for j in range(2):
                b.op("pe", lambda e, j=j: e.matmul(
                    psA.h[0:64, 0:512], c_oner.h[0:1, 0:64],
                    lrec.h[0:1, j * 512:(j + 1) * 512],
                    start=True, stop=True),
                    r=[(c_oner, ALL), (lrec, ALL)], w=[(psA, ALL)])
                b.op("dve", lambda e, j=j: e.tensor_copy(
                    out=lrecb.h[:, j * 512:(j + 1) * 512],
                    in_=psA.h[0:64, 0:512]),
                    r=[(psA, ALL)], w=[(lrecb, j)])
            b.op("dve", lambda e, ob=ob, chx=chx, rwx=rwx, q0=q0:
                 e.tensor_tensor(
                     out=attnT.h[rwx:rwx + 64,
                                 chx * OWN + q0:chx * OWN + q0 + 1024],
                     in0=ob.h[0:64, :], in1=lrecb.h[:], op=ALU.mult),
                 r=[(ob, 0), (ob, 1), (lrecb, 0), (lrecb, 1)],
                 w=[(attnT, (chx, rwx, q0))])

    p3_loads(0)
    p3_setup(0)
    for hp in range(NH // 2):
        if hp + 1 < NH // 2:
            p3_loads(hp + 1)
        p3_qh(hp, 0)
        if hp + 1 < NH // 2:
            p3_setup(hp + 1)
        p3_qh(hp, 1)

    wo_sb = b.sb("wo_sb", [128, 8 * D], BF16)
    for dc in range(8):
        b.dma("pool", lambda e, dc=dc: e.dma_start(
            out=wo_sb.h[:, dc * D:(dc + 1) * D],
            in_=w_wo.h[dc * 128:(dc + 1) * 128, :]), w=[(wo_sb, dc)])
    bo_row = b.sb("bo_row", [1, D], F32)
    b.dma("sp", lambda e: e.dma_start(out=bo_row.h[:], in_=w_bo.h.ap()),
          w=[(bo_row, ALL)])
    r1_t_r = [b.sb(f"r1_t{i}", [128, D], F32) for i in range(2)]
    x1_t_r = [b.sb(f"x1_t{i}", [128, D], F32) for i in range(2)]
    keys_t_r = [b.sb(f"keys_t{i}", [128, D], F32) for i in range(2)]
    kcol_r = [b.sb(f"kcol{i}", [128, 1], F32) for i in range(2)]
    kint_r = [b.sb(f"kint{i}", [128, 1], I32) for i in range(2)]
    kback_r = [b.sb(f"kback{i}", [128, 1], F32) for i in range(2)]
    kd_r = [b.sb(f"kd{i}", [128, 1], F32) for i in range(2)]

    def p3_block(tt):
        r1_t, x1_t, keys_t = r1_t_r[tt % 2], x1_t_r[tt % 2], keys_t_r[tt % 2]
        kcol, kint, kback, kd = (kcol_r[tt % 2], kint_r[tt % 2],
                                 kback_r[tt % 2], kd_r[tt % 2])
        xn_r = xn_r_r[tt % 2]
        wps_t = psA if tt % 2 == 0 else psB
        for j in range(2):
            for dc in range(8):
                b.op("pe", lambda e, tt=tt, j=j, dc=dc, wps=wps_t: e.matmul(
                    wps.h[:, j * 512:(j + 1) * 512],
                    attnT.h[:, dc * OWN + tt * 128:dc * OWN + (tt + 1) * 128],
                    wo_sb.h[:, dc * D + j * 512:dc * D + (j + 1) * 512],
                    start=(dc == 0), stop=False),
                    r=[(attnT, ALL), (wo_sb, dc)], w=[(wps_t, j)])
            b.op("pe", lambda e, j=j, wps=wps_t: e.matmul(
                wps.h[:, j * 512:(j + 1) * 512], c_oner.h[0:1, :],
                bo_row.h[0:1, j * 512:(j + 1) * 512],
                start=False, stop=True),
                r=[(c_oner, ALL), (bo_row, ALL)], w=[(wps_t, j)])
        b.dma("sp", lambda e, tt=tt, xn_r=xn_r: e.dma_start(
            out=xn_r.h[:], in_=d_xn32.h[tt * 128:(tt + 1) * 128, :]),
            r=[(d_xn32, tt)], w=[(xn_r, ALL)])
        b.op("dve", lambda e, wps=wps_t, r1_t=r1_t, xn_r=xn_r:
             e.tensor_tensor(
                 out=r1_t.h[:], in0=wps.h[:], in1=xn_r.h[:], op=ALU.add),
             r=[(wps_t, 0), (wps_t, 1), (xn_r, ALL)], w=[(r1_t, ALL)])
        ln_tile(r1_t, x1_t)
        b.op("dve", lambda e: e.tensor_tensor(
            out=x1_t.h[:], in0=x1_t.h[:], in1=g1_g.h[:], op=ALU.mult),
            r=[(x1_t, ALL), (g1_g, ALL)], w=[(x1_t, ALL)])
        b.op("pool", lambda e: e.tensor_tensor(
            out=x1_t.h[:], in0=x1_t.h[:], in1=g1_b.h[:], op=ALU.add),
            r=[(x1_t, ALL), (g1_b, ALL)], w=[(x1_t, ALL)])
        b.dma("pool", lambda e, tt=tt: e.dma_start(
            out=d_x132.h[tt * 128:(tt + 1) * 128, :], in_=x1_t.h[:]),
            r=[(x1_t, ALL)], w=[(d_x132, tt)])
        b.dma("pool", lambda e, tt=tt: e.dma_start(
            out=d_x1bf.h[tt * 128:(tt + 1) * 128, :], in_=x1_t.h[:]),
            r=[(x1_t, ALL)], w=[(d_x1bf, tt)])
        b.op("dve", lambda e: e.tensor_tensor(
            out=keys_t.h[:], in0=x1_t.h[:], in1=rw_b.h[:], op=ALU.mult),
            r=[(x1_t, ALL), (rw_b, ALL)], w=[(keys_t, ALL)])
        b.op("dve", lambda e: e.reduce_sum(out=kcol.h[:], in_=keys_t.h[:],
                                           axis=AX),
             r=[(keys_t, ALL)], w=[(kcol, ALL)])
        b.op("dve", lambda e: e.tensor_scalar(
            out=kcol.h[:], in0=kcol.h[:], scalar1=rb_c.h[:], scalar2=None,
            op0=ALU.add), r=[(kcol, ALL), (rb_c, ALL)], w=[(kcol, ALL)])
        b.op("dve", lambda e: e.tensor_copy(out=kint.h[:], in_=kcol.h[:]),
             r=[(kcol, ALL)], w=[(kint, ALL)])
        b.op("dve", lambda e: e.tensor_copy(out=kback.h[:], in_=kint.h[:]),
             r=[(kint, ALL)], w=[(kback, ALL)])
        b.op("dve", lambda e: e.tensor_tensor(
            out=kd.h[:], in0=kcol.h[:], in1=kback.h[:], op=ALU.is_lt),
            r=[(kcol, ALL), (kback, ALL)], w=[(kd, ALL)])
        b.op("dve", lambda e: e.tensor_sub(out=kcol.h[:], in0=kback.h[:],
                                           in1=kd.h[:]),
             r=[(kback, ALL), (kd, ALL)], w=[(kcol, ALL)])
        b.op("dve", lambda e: e.tensor_scalar(
            out=kd.h[:], in0=kcol.h[:], scalar1=0.25, scalar2=None,
            op0=ALU.mult), r=[(kcol, ALL)], w=[(kd, ALL)])
        b.op("dve", lambda e: e.tensor_copy(out=kint.h[:], in_=kd.h[:]),
             r=[(kd, ALL)], w=[(kint, ALL)])
        b.op("dve", lambda e: e.tensor_copy(out=kback.h[:], in_=kint.h[:]),
             r=[(kint, ALL)], w=[(kback, ALL)])
        b.op("dve", lambda e: e.tensor_tensor(
            out=kd.h[:], in0=kd.h[:], in1=kback.h[:], op=ALU.is_lt),
            r=[(kd, ALL), (kback, ALL)], w=[(kd, ALL)])
        b.op("dve", lambda e: e.tensor_sub(out=kback.h[:], in0=kback.h[:],
                                           in1=kd.h[:]),
             r=[(kback, ALL), (kd, ALL)], w=[(kback, ALL)])
        b.op("dve", lambda e, tt=tt: e.scalar_tensor_tensor(
            out=idx_cols.h[:, tt:tt + 1], in0=kback.h[:], scalar=-4.0,
            in1=kcol.h[:], op0=ALU.mult, op1=ALU.add),
            r=[(kback, ALL), (kcol, ALL)], w=[(idx_cols, tt)])

    for tt in range(16):
        p3_block(tt)

    if stage < 4:
        _tail(b, nc, out, [d_x1bf, d_x132])
        return

    b.barrier()
    g2_g = b.sb("g2_g", [128, D], F32)
    g2_b = b.sb("g2_b", [128, D], F32)
    bcast_row(w_ln2g, g2_g)
    bcast_row(w_ln2b, g2_b)
    w1_sb = [b.sb(f"w1_sb{i}", [128, 8 * DI], BF16) for i in range(2)]
    w2_sb = [b.sb(f"w2_sb{i}", [128, 8 * D], BF16) for i in range(2)]
    b1cols = b.sb("b1cols", [128, 32], F32)
    b1row = b.sb("b1row", [32, 128], F32)
    b2row = b.sb("b2row", [1, D], F32)
    ps_b1 = b.ps("ps_b1", [128, 512], F32)
    mask_c = b.sb("mask_c", [128, 16], F32)
    x1fm = [b.sb(f"x1fm{i}", [128, 8 * 512], BF16) for i in range(2)]
    h_ps = b.ps("h_ps", [128, 512], F32)
    h_ps2 = b.ps("h_ps2", [128, 512], F32)
    h_sb = b.sb("h_sb", [128, 16 * 512], BF16)
    y_ps = b.ps("y_ps", [128, 1024], F32)
    y_ps2 = b.ps("y_ps2", [128, 1024], F32)
    acc = b.sb("acc", [128, 16 * 1024], BF16)
    g_t = b.sb("g_t", [128, D], F32)
    x2_t = xn_r_r[0]

    def p4_ln2(ti):
        b.op("dve", lambda e, ti=ti: e.tensor_copy(
            out=x2_t.h[:], in_=acc.h[:, ti * 1024:(ti + 1) * 1024]),
            r=[(acc, ti)], w=[(x2_t, ALL)])
        ln_tile(x2_t, g_t)
        b.op("dve", lambda e: e.tensor_tensor(
            out=g_t.h[:], in0=g_t.h[:], in1=g2_g.h[:], op=ALU.mult),
            r=[(g_t, ALL), (g2_g, ALL)], w=[(g_t, ALL)])
        b.op("pool", lambda e: e.tensor_tensor(
            out=g_t.h[:], in0=g_t.h[:], in1=g2_b.h[:], op=ALU.add),
            r=[(g_t, ALL), (g2_b, ALL)], w=[(g_t, ALL)])
        b.op("act", lambda e: e.activation(out=g_t.h[:], in_=g_t.h[:],
                                           func=AF.Silu),
             r=[(g_t, ALL)], w=[(g_t, ALL)])
        b.dma("pool", lambda e, ti=ti: e.dma_start(
            out=d_g32.h[ti * 128:(ti + 1) * 128, :], in_=g_t.h[:]),
            r=[(g_t, ALL)], w=[(d_g32, ti)])

    def p4_evac(e_i, half, yp, ti):
        if e_i == 0 and half == 0:
            xr = xn_r_r[ti % 2]
            b.dma("sp", lambda e, ti=ti, xr=xr: e.dma_start(
                out=xr.h[:], in_=d_x132.h[ti * 128:(ti + 1) * 128, :]),
                r=[(d_x132, ti)], w=[(xr, ALL)])
            b.op("dve", lambda e, yp=yp, ti=ti, xr=xr: e.scalar_tensor_tensor(
                out=acc.h[:, ti * 1024:(ti + 1) * 1024],
                in0=yp.h[:], scalar=mask_c.h[:, ti:ti + 1],
                in1=xr.h[:], op0=ALU.mult, op1=ALU.add),
                r=[(yp, 0), (yp, 1), (mask_c, ALL), (xr, ALL)],
                w=[(acc, ti)])
        else:
            b.op("dve", lambda e, yp=yp, ti=ti: e.scalar_tensor_tensor(
                out=acc.h[:, ti * 1024:(ti + 1) * 1024],
                in0=yp.h[:], scalar=mask_c.h[:, ti:ti + 1],
                in1=acc.h[:, ti * 1024:(ti + 1) * 1024],
                op0=ALU.mult, op1=ALU.add),
                r=[(yp, 0), (yp, 1), (mask_c, ALL), (acc, ti)],
                w=[(acc, ti)])
        if e_i == NE - 1 and half == 1:
            p4_ln2(ti)

    def p4_w1_load(e_i, half):
        wb = w1_sb[(e_i * 2 + half) % 2]
        for dc in range(8):
            b.dma("pool", lambda e, e_i=e_i, half=half, dc=dc, wb=wb:
                  e.dma_start(
                      out=wb.h[:, dc * DI:(dc + 1) * DI],
                      in_=w_w1.h[e_i, dc * 128:(dc + 1) * 128,
                                 half * 2048:(half + 1) * 2048]),
                  w=[(wb, dc)])
        return wb

    def p4_w2_load(e_i, half, quarter):
        wb = w2_sb[quarter % 2]
        for fc8 in range(8):
            fcg = half * 2048 + (quarter * 8 + fc8) * 128
            b.dma("pool", lambda e, e_i=e_i, fcg=fcg, fc8=fc8, wb=wb:
                  e.dma_start(out=wb.h[:, fc8 * D:(fc8 + 1) * D],
                              in_=w_w2.h[e_i, fcg:fcg + 128, :]),
                  w=[(wb, fc8)])
        return wb

    w1_cur = p4_w1_load(0, 0)
    for e_i in range(NE):
        b.dma("sp", lambda e, e_i=e_i: e.dma_start(
            out=b1row.h[:], in_=w_b1.h[e_i:e_i + 1, :].rearrange(
                "a (c p) -> (a c) p", p=128)),
            w=[(b1row, ALL)])
        b.op("pe", lambda e: e.transpose(ps_b1.h[:, 0:32], b1row.h[:],
                                         ident_f.h[0:32, 0:32]),
             r=[(b1row, ALL), (ident_f, ALL)], w=[(ps_b1, ALL)])
        b.op("dve", lambda e: e.tensor_copy(out=b1cols.h[:],
                                            in_=ps_b1.h[:, 0:32]),
             r=[(ps_b1, ALL)], w=[(b1cols, ALL)])
        b.dma("sp", lambda e, e_i=e_i: e.dma_start(
            out=b2row.h[:], in_=w_b2.h[e_i:e_i + 1, :]), w=[(b2row, ALL)])
        b.op("dve", lambda e, e_i=e_i: e.tensor_scalar(
            out=mask_c.h[:], in0=idx_cols.h[:], scalar1=float(e_i),
            scalar2=None, op0=ALU.is_equal),
            r=[(idx_cols, ALL)] + [(idx_cols, t) for t in range(16)],
            w=[(mask_c, ALL)])
        for half in range(2):
            wb1 = w1_cur
            if not (e_i == NE - 1 and half == 1):
                ne_, nh_ = (e_i, 1) if half == 0 else (e_i + 1, 0)
                w1_cur = p4_w1_load(ne_, nh_)
            for tt4 in range(4):
                xf = x1fm[tt4 % 2]
                for dc in range(8):
                    b.dma("act", lambda e, tt4=tt4, dc=dc, xf=xf: e.dma_start(
                        out=xf.h[:, dc * 512:(dc + 1) * 512],
                        in_=d_x1bf.h[tt4 * 512:(tt4 + 1) * 512,
                                     dc * 128:(dc + 1) * 128], transpose=True),
                        r=[(d_x1bf, ALL)], w=[(xf, dc)])
                for fc in range(16):
                    ps = h_ps if fc % 2 == 0 else h_ps2
                    for dc in range(8):
                        b.op("pe", lambda e, ps=ps, fc=fc, dc=dc, xf=xf,
                             wb1=wb1: e.matmul(
                                 ps.h[:],
                                 wb1.h[:, dc * DI + fc * 128:
                                       dc * DI + (fc + 1) * 128],
                                 xf.h[:, dc * 512:(dc + 1) * 512],
                                 start=(dc == 0), stop=(dc == 7)),
                             r=[(wb1, dc), (xf, dc)], w=[(ps, ALL)])
                    b.op("act", lambda e, ps=ps, fc=fc, half=half:
                         e.activation(
                             out=h_sb.h[:, fc * 512:(fc + 1) * 512],
                             in_=ps.h[:], func=AF.Relu,
                             bias=b1cols.h[:, half * 16 + fc:
                                           half * 16 + fc + 1]),
                         r=[(ps, ALL), (b1cols, ALL)], w=[(h_sb, fc)])
                if tt4 == 0:
                    wq = [p4_w2_load(e_i, half, 0), p4_w2_load(e_i, half, 1)]
                else:
                    wq = [w2_sb[0], w2_sb[1]]
                for ts_ in range(4):
                    yp = y_ps if ts_ % 2 == 0 else y_ps2
                    ti = tt4 * 4 + ts_
                    for j in range(2):
                        for fc in range(16):
                            wb2 = wq[fc // 8]
                            fc8 = fc % 8
                            b.op("pe", lambda e, yp=yp, j=j, fc=fc, fc8=fc8,
                                 ts_=ts_, half=half, wb2=wb2: e.matmul(
                                     yp.h[:, j * 512:(j + 1) * 512],
                                     h_sb.h[:, fc * 512 + ts_ * 128:
                                            fc * 512 + (ts_ + 1) * 128],
                                     wb2.h[:, fc8 * D + j * 512:
                                           fc8 * D + (j + 1) * 512],
                                     start=(fc == 0),
                                     stop=(half == 0 and fc == 15)),
                                 r=[(h_sb, fc), (wb2, fc8)],
                                 w=[(yp, j)])
                        if half == 1:
                            b.op("pe", lambda e, yp=yp, j=j: e.matmul(
                                yp.h[:, j * 512:(j + 1) * 512],
                                c_oner.h[0:1, :],
                                b2row.h[0:1, j * 512:(j + 1) * 512],
                                start=False, stop=True),
                                r=[(c_oner, ALL), (b2row, ALL)],
                                w=[(yp, j)])
                    p4_evac(e_i, half, yp, ti)

    if stage < 5:
        _tail(b, nc, out, [d_g32])
        return

    b.barrier()
    cmat_t = b.sb("cmat_t", [128, NST], F32)
    ct_sb = b.sb("ct_sb", [NST, 16 * 128], BF16)
    ps_ct = b.ps("ps_ct", [128, 512], F32)
    for dc in range(16):
        b.dma("sp", lambda e, dc=dc: e.dma_start(
            out=cmat_t.h[:], in_=w_cmat.h[dc * 128:(dc + 1) * 128, :]),
            w=[(cmat_t, ALL)])
        b.op("pe", lambda e, dc=dc: e.transpose(
            ps_ct.h[0:NST, 0:128], cmat_t.h[:], ident_f.h[:]),
            r=[(cmat_t, ALL), (ident_f, ALL)], w=[(ps_ct, ALL)])
        b.op("dve", lambda e, dc=dc: e.tensor_copy(
            out=ct_sb.h[:, dc * 128:(dc + 1) * 128],
            in_=ps_ct.h[0:NST, 0:128]),
            r=[(ps_ct, ALL)], w=[(ct_sb, dc)])
    wout_sb = b.sb("wout_sb", [128, 16 * D], BF16)
    for dc in range(16):
        b.dma("pool", lambda e, dc=dc: e.dma_start(
            out=wout_sb.h[:, dc * D:(dc + 1) * D],
            in_=w_wout.h[dc * 128:(dc + 1) * 128, :]), w=[(wout_sb, dc)])

    y2_ps = b.ps("y2_ps", [128, 512], F32)
    y2_ps2 = b.ps("y2_ps2", [128, 512], F32)
    xsc_r_r = [b.sb(f"xsc_r{i}", [128, 512], F32) for i in range(2)]
    gs_r_r = [b.sb(f"gs_r{i}", [128, 512], F32) for i in range(2)]
    yg_sb = b.sb("yg_sb", [128, 16 * 512], BF16)
    so_ps = b.ps("so_ps", [128, 1024], F32)
    so_ps2 = b.ps("so_ps2", [128, 1024], F32)
    s_t_r = [b.sb(f"s_t{i}", [128, D], F32) for i in range(2)]
    gf_t_r = [b.sb(f"gf_t{i}", [128, D], F32) for i in range(2)]
    xo_t_r = [b.sb(f"xo_t{i}", [128, D], F32) for i in range(2)]
    o_t_r = [b.sb(f"o_t{i}", [128, D], F32) for i in range(2)]

    def p5_y(tt4, dc):
        yps = y2_ps if dc % 2 == 0 else y2_ps2
        xsc_r, gs_r = xsc_r_r[dc % 2], gs_r_r[dc % 2]
        b.op("pe", lambda e, yps=yps, tt4=tt4, dc=dc: e.matmul(
            yps.h[:], ct_sb.h[:, dc * 128:(dc + 1) * 128],
            hTb.h[:, tt4 * 512:(tt4 + 1) * 512], start=True, stop=True),
            r=[(ct_sb, dc), (hTb, ALL)], w=[(yps, ALL)])
        b.dma("sp", lambda e, tt4=tt4, dc=dc, xsc_r=xsc_r: e.dma_start(
            out=xsc_r.h[:], in_=d_xsc32.h[dc * 128:(dc + 1) * 128,
                                          tt4 * 512:(tt4 + 1) * 512]),
            r=[(d_xsc32, (dc, tt4 + 4))], w=[(xsc_r, ALL)])
        b.dma("sp", lambda e, tt4=tt4, dc=dc, gs_r=gs_r: e.dma_start(
            out=gs_r.h[:], in_=d_gs32.h[dc * 128:(dc + 1) * 128,
                                        tt4 * 512:(tt4 + 1) * 512]),
            r=[(d_gs32, (dc, tt4 + 4))], w=[(gs_r, ALL)])
        b.op("dve", lambda e, yps=yps, dc=dc, xsc_r=xsc_r:
             e.scalar_tensor_tensor(
                 out=xsc_r.h[:], in0=xsc_r.h[:], scalar=dv_c.h[:, dc:dc + 1],
                 in1=yps.h[:], op0=ALU.mult, op1=ALU.add),
             r=[(xsc_r, ALL), (dv_c, ALL), (yps, ALL)], w=[(xsc_r, ALL)])
        b.op("dve", lambda e, dc=dc, xsc_r=xsc_r, gs_r=gs_r: e.tensor_tensor(
            out=yg_sb.h[:, dc * 512:(dc + 1) * 512], in0=xsc_r.h[:],
            in1=gs_r.h[:], op=ALU.mult),
            r=[(xsc_r, ALL), (gs_r, ALL)], w=[(yg_sb, dc)])

    def p5_out(tt4, ts_):
        sp_ = so_ps if ts_ % 2 == 0 else so_ps2
        ti = tt4 * 4 + ts_
        s_t, gf_t = s_t_r[ts_ % 2], gf_t_r[ts_ % 2]
        xo_t, o_t = xo_t_r[ts_ % 2], o_t_r[ts_ % 2]
        xr = xn_r_r[ts_ % 2]
        for j in range(2):
            for dc in range(16):
                b.op("pe", lambda e, sp_=sp_, ts_=ts_, j=j, dc=dc:
                     e.matmul(
                         sp_.h[:, j * 512:(j + 1) * 512],
                         yg_sb.h[:, dc * 512 + ts_ * 128:
                                 dc * 512 + (ts_ + 1) * 128],
                         wout_sb.h[:, dc * D + j * 512:
                                   dc * D + (j + 1) * 512],
                         start=(dc == 0), stop=(dc == 15)),
                     r=[(yg_sb, dc), (wout_sb, dc)], w=[(sp_, j)])
        b.dma("sp", lambda e, ti=ti, xr=xr: e.dma_start(
            out=xr.h[:], in_=d_xn32.h[ti * 128:(ti + 1) * 128, :]),
            r=[(d_xn32, ti)], w=[(xr, ALL)])
        b.op("dve", lambda e, sp_=sp_, s_t=s_t, xr=xr: e.tensor_tensor(
            out=s_t.h[:], in0=sp_.h[:], in1=xr.h[:], op=ALU.add),
            r=[(sp_, 0), (sp_, 1), (xr, ALL)], w=[(s_t, ALL)])
        b.dma("sp", lambda e, ti=ti, gf_t=gf_t: e.dma_start(
            out=gf_t.h[:], in_=d_g32.h[ti * 128:(ti + 1) * 128, :]),
            r=[(d_g32, ti)], w=[(gf_t, ALL)])
        b.dma("sp", lambda e, ti=ti, xo_t=xo_t: e.dma_start(
            out=xo_t.h[:],
            in_=xs_ext.h[PRE + ti * 128:PRE + (ti + 1) * 128, :]),
            w=[(xo_t, ALL)])
        b.op("dve", lambda e, s_t=s_t, gf_t=gf_t: e.tensor_tensor(
            out=s_t.h[:], in0=s_t.h[:], in1=gf_t.h[:], op=ALU.mult),
            r=[(s_t, ALL), (gf_t, ALL)], w=[(s_t, ALL)])
        b.op("dve", lambda e, s_t=s_t, xo_t=xo_t, o_t=o_t: e.tensor_tensor(
            out=o_t.h[:], in0=s_t.h[:], in1=xo_t.h[:], op=ALU.add),
            r=[(s_t, ALL), (xo_t, ALL)], w=[(o_t, ALL)])
        b.dma("pool", lambda e, ti=ti, o_t=o_t: e.dma_start(
            out=out.h[ti * 128:(ti + 1) * 128, :], in_=o_t.h[:]),
            r=[(o_t, ALL)], w=[(out, ti)])

    for tt4 in range(4):
        for dc in range(16):
            p5_y(tt4, dc)
        for ts_ in range(4):
            p5_out(tt4, ts_)

    _tail(b, nc, out, [])


def _tail(b, nc, out, extra):
    b.finalize()
    if out.sem is not None:
        nc.sync.wait_ge(out.sem, 16 * out.sem_cnt)
    for buf in extra:
        if buf.sem is not None:
            nc.sync.wait_ge(buf.sem, 16 * buf.sem_cnt)
    nc.all_engine_barrier()
    for s in b.all_sems:
        nc.gpsimd.sem_clear(s)
    nc.all_engine_barrier()
    b.phase.close()



_NC_CACHE = {}


def _get_nc(stage=STAGE):
    if stage not in _NC_CACHE:
        _NC_CACHE[stage] = build_nc(stage)
    return _NC_CACHE[stage]


def make_in_maps(inp):
    import ml_dtypes
    x = np.ascontiguousarray(np.asarray(inp["L0_attn_x"], dtype=np.float32))
    f32 = lambda v: np.ascontiguousarray(np.asarray(v, dtype=np.float32))
    common = dict(
        identf=np.eye(128, dtype=np.float32),
        identb=np.eye(128, dtype=np.float32).astype(ml_dtypes.bfloat16),
        w_lng=f32(inp["L0_attn_ln_gamma"]), w_lnb=f32(inp["L0_attn_ln_beta"]),
        w_in=f32(inp["L0_attn_W_in"]),
        w_convw=f32(np.asarray(inp["L0_attn_conv_w"]).reshape(DI, KCONV).T),
        w_convb=f32(inp["L0_attn_conv_b"]),
        w_alog=f32(np.asarray(inp["L0_attn_A_log"]).reshape(1, NST)),
        w_bmat=f32(inp["L0_attn_B_mat"]), w_cmat=f32(inp["L0_attn_C_mat"]),
        w_dvec=f32(inp["L0_attn_D_vec"]), w_wd=f32(inp["L0_attn_Wd"]),
        w_bd=f32(inp["L0_attn_bd"]), w_wout=f32(inp["L0_attn_W_out"]),
        w_wq=f32(inp["L0_ffn_Wq"]), w_bq=f32(inp["L0_ffn_bq"]),
        w_wk=f32(inp["L0_ffn_Wk"]), w_bk=f32(inp["L0_ffn_bk"]),
        w_wv=f32(inp["L0_ffn_Wv"]), w_bv=f32(inp["L0_ffn_bv"]),
        w_wo=f32(inp["L0_ffn_Wo"]),
        w_bo=f32(np.asarray(inp["L0_ffn_bo"]).reshape(1, D)),
        w_rw=f32(np.asarray(inp["L0_ffn_router_w"]).reshape(D)),
        w_rb=f32(np.asarray(inp["L0_ffn_router_b"]).reshape(1, 1)),
        w_w1=f32(inp["L0_ffn_W1"]), w_b1=f32(inp["L0_ffn_b1"]),
        w_w2=f32(inp["L0_ffn_W2"]), w_b2=f32(inp["L0_ffn_b2"]),
        w_ln1g=f32(inp["L0_ffn_ln1_g"]), w_ln1b=f32(inp["L0_ffn_ln1_b"]),
        w_ln2g=f32(inp["L0_ffn_ln2_g"]), w_ln2b=f32(inp["L0_ffn_ln2_b"]),
    )
    in_maps = []
    for c in range(8):
        bi, p = c // 2, c % 2
        own = x[bi, p * OWN:(p + 1) * OWN]
        oth = x[bi, (1 - p) * OWN:(2 - p) * OWN]
        pre = x[bi, 0:PRE] if p == 1 else np.zeros((PRE, D), np.float32)
        m = np.float32(p)
        in_maps.append(dict(
            common,
            xs_ext=np.ascontiguousarray(np.concatenate([pre, own], 0)),
            xkv=np.ascontiguousarray(oth),
            mpref16=np.full((16, 1), m, np.float32),
            mhalo=np.full((128, 1), m, np.float32)))
    return in_maps


def kernel(**inp):
    nc = _get_nc(STAGE)
    in_maps = make_in_maps(inp)
    res = run_bass_kernel_spmd(nc, in_maps, core_ids=list(range(8)))
    outp = np.empty((B, T, D), np.float32)
    for c in range(8):
        bi, p = c // 2, c % 2
        outp[bi, p * OWN:(p + 1) * OWN] = res.results[c]["out"]
    return outp
